# version 01
# baseline (speedup 1.0000x reference)
import numpy as np
from contextlib import ExitStack

import concourse.bass as bass
import concourse.mybir as mybir
from concourse.bass_utils import run_bass_kernel_spmd

F32 = mybir.dt.float32
BF16 = mybir.dt.bfloat16
AF = mybir.ActivationFunctionType
ALU = mybir.AluOpType
AX = mybir.AxisListType

ENGINES = ("sync", "tensor", "vector", "scalar", "gpsimd")

D = 1024
NT = 16
NJ = 8
NH = 8
FF = 4096
NQ4 = 4
NC8 = 8


class Prog:
    def __init__(self, semnames):
        self.ev = {e: [] for e in ENGINES}
        self.cnt = {s: 0 for s in semnames}

    def emit(self, eng, fn=None, waits=(), inc=None, n=1):
        ws = tuple((s, v) for (s, v) in waits if v and v > 0)
        self.ev[eng].append((fn, ws, inc, n))
        if inc is not None:
            self.cnt[inc] += n
            return self.cnt[inc]
        return None


def build_nc(NB=2, NL=6, dbg=(), only_b0=False):
    nc = bass.Bass("TRN2", target_bir_lowering=False)

    x_in = nc.dram_tensor("x_in", [NB, 2048, D], F32, kind="ExternalInput")
    ln1_w = nc.dram_tensor("ln1_w", [6, D], F32, kind="ExternalInput")
    ln1_b = nc.dram_tensor("ln1_b", [6, D], F32, kind="ExternalInput")
    ln2_w = nc.dram_tensor("ln2_w", [6, D], F32, kind="ExternalInput")
    ln2_b = nc.dram_tensor("ln2_b", [6, D], F32, kind="ExternalInput")
    bo_d = nc.dram_tensor("bo_d", [6, D], F32, kind="ExternalInput")
    wq_d = nc.dram_tensor("wq_d", [6, D, D], F32, kind="ExternalInput")
    wk_d = nc.dram_tensor("wk_d", [6, D, D], F32, kind="ExternalInput")
    wv_d = nc.dram_tensor("wv_d", [6, D, D], F32, kind="ExternalInput")
    wo_d = nc.dram_tensor("wo_d", [6, D, D], F32, kind="ExternalInput")
    w1_d = nc.dram_tensor("w1_d", [6, D, FF], F32, kind="ExternalInput")
    w2_d = nc.dram_tensor("w2_d", [6, FF, D], F32, kind="ExternalInput")
    fgamma = nc.dram_tensor("fgamma", [D], F32, kind="ExternalInput")
    fbeta = nc.dram_tensor("fbeta", [D], F32, kind="ExternalInput")
    fmu = nc.dram_tensor("fmu", [D, 256], F32, kind="ExternalInput")
    fscale = nc.dram_tensor("fscale", [1], F32, kind="ExternalInput")
    fw1 = nc.dram_tensor("fw1", [16, D, 256], F32, kind="ExternalInput")
    fb1 = nc.dram_tensor("fb1", [16, 256], F32, kind="ExternalInput")
    fw2 = nc.dram_tensor("fw2", [16, 256, D], F32, kind="ExternalInput")
    fb2 = nc.dram_tensor("fb2", [16, D], F32, kind="ExternalInput")
    y_out = nc.dram_tensor("y_out", [NB, 2048, D], F32, kind="ExternalOutput")

    es = ExitStack()
    sb = lambda n, s, d: es.enter_context(nc.sbuf_tensor(n, s, d))
    psa = lambda n, s, d: es.enter_context(nc.psum_tensor(n, s, d))

    x_sb = sb("x_sb", [128, NT, D], F32)
    pans = [sb(f"pan{i}", [128, 8192], BF16) for i in range(6)]
    u_arena = sb("u_arena", [128, 16384], mybir.dt.uint8)
    ua_addr = nc.lookup_mloc(u_arena).addr
    sc_xnT = nc.alloc_sbuf_tensor_at("sc_xnT", [128, 2, D], BF16,
                                     offset=ua_addr)
    sc_a = nc.alloc_sbuf_tensor_at("sc_a", [128, 2, D], BF16,
                                   offset=ua_addr + 4096)
    xio = nc.alloc_sbuf_tensor_at("xio", [128, 2, D], F32,
                                  offset=ua_addr + 8192)
    x2T = nc.alloc_sbuf_tensor_at("x2T", [128, NJ, 1024], BF16,
                                  offset=ua_addr)
    rep = sb("rep", [128, 4, D], BF16)
    borow = sb("borow", [1, D], BF16)
    ident = sb("ident", [128, 128], BF16)
    ones_bf = sb("ones_bf", [128, 1], BF16)
    ones_row = sb("ones_row", [1, 128], BF16)
    ones_f32c = sb("ones_f32c", [128, 1], F32)
    ones_f32r = sb("ones_f32r", [1, 128], F32)
    ctx_sb = sb("ctx_sb", [128, D], BF16)
    sc_xn = sb("sc_xn", [128, 2, D], BF16)
    sc_b = sb("sc_b", [128, 2, D], BF16)
    sc_c = sb("sc_c", [128, 2, D], BF16)
    fdump = sb("fdump", [128, D], F32)
    ldump = sb("ldump", [128, 256], F32)
    st = sb("st", [128, 16], F32)
    nm_all = sb("nm_all", [128, NT], F32)
    rs_all = sb("rs_all", [128, NT], F32)
    qrs = sb("qrs", [128, 16], F32)
    qrcp = sb("qrcp", [128, 16], F32)
    dnr = sb("dnr", [128, 8], F32)
    fsc_rep = sb("fsc_rep", [128, 1], F32)
    b1T = sb("b1T", [128, 2, 16], F32)
    b2T = sb("b2T", [128, 8, 16], F32)
    hT = sb("hT", [128, 2, 16], BF16)
    zrcp = sb("zrcp", [128, 2], F32)
    eps_sb = sb("eps_sb", [128, 1], F32)
    mu_tmp = nc.alloc_sbuf_tensor_at(
        "mu_tmp", [128, 256], F32,
        offset=nc.lookup_mloc(fdump).addr + 1024)

    ps_pA = psa("ps_pA", [128, 1024], F32)
    ps_pB = psa("ps_pB", [128, 1024], F32)
    ps_ctx = psa("ps_ctx", [128, 1024], F32)
    ps_tr = psa("ps_tr", [128, 1024], BF16)
    ps_ms = psa("ps_ms", [128, 512], F32)

    sems = ("dG", "dS", "tT", "tV", "tS", "tG")
    p = Prog(sems)
    E = p.emit
    C = p.cnt

    S = {
        "fdump": ("tV", 0),
        "pA": ("tV", 0),
        "pB": ("tV", 0),
        "ctxps": ("tV", 0),
        "ms": ("tV", 0),
        "tr": ("tV", 0),
        "rep": ("tV", 0),
        "x_free": ("tV", 0),
    }

    dbg_outs = {}

    def dump(tag, src_ap, shape, dt, cast=False):
        if tag not in dbg:
            return
        h = nc.dram_tensor("dbg_" + tag, shape, F32 if cast else dt,
                           kind="ExternalOutput")
        eng = "gpsimd" if cast else "sync"
        sem = "dG" if eng == "gpsimd" else "dS"
        snapd = E(eng, lambda g, d=h, s2=src_ap: g.dma_start(
            bass.AP(d, 0, [[int(np.prod(shape[1:])), shape[0]]] +
                    [[1, int(np.prod(shape[1:]))]]), s2),
          waits=[("tT", C["tT"]), ("tV", C["tV"]), ("tS", C["tS"]),
                 ("tG", C["tG"])],
          inc=sem, n=16)
        dbg_outs[tag] = True
        return (sem, snapd)

    def init_fn(g):
        g.memset(ident[:, :], 0.0)
        g.affine_select(out=ident[:, :], in_=ident[:, :],
                        compare_op=ALU.not_equal, fill=1.0, base=0,
                        pattern=[[-1, 128]], channel_multiplier=1)
        g.memset(ones_bf[:, :], 1.0)
        g.memset(ones_row[:, :], 1.0)
        g.memset(ones_f32c[:, :], 1.0)
        g.memset(eps_sb[:, :], 1e-5)
        return g.memset(ones_f32r[:, :], 1.0)
    ginit = E("gpsimd", init_fn, inc="tG")
    E("gpsimd", lambda g: g.dma_start(
        fsc_rep[:, :], bass.AP(fscale, 0, [[0, 128], [1, 1]])),
        inc="dG", n=16)

    panel_free = [[] for _ in range(6)]
    panel_loaded = [0] * 6

    def alloc_panel():
        def key(i):
            f = panel_free[i]
            return max((v for (_, v) in f), default=0)
        idx = min(range(6), key=key)
        w = list(panel_free[idx])
        panel_free[idx] = [("zz_taken", 10 ** 9)]
        return idx, w

    def free_panel(idx):
        panel_free[idx] = [("tT", C["tT"]), ("tV", C["tV"]), ("tS", C["tS"])]

    def wload2(idx, dram_h, base_off, row_stride, cols, w):
        pan = pans[idx]
        for j in range(NJ):
            src = bass.AP(dram_h, base_off + j * 128 * row_stride,
                          [[row_stride, 128], [1, cols]])
            E("gpsimd",
              lambda g, d=pan[:, j * cols:(j + 1) * cols], s2=src:
              g.dma_start(d, s2),
              waits=(w if j == 0 else ()), inc="dG", n=16)
        panel_loaded[idx] = C["dG"]

    def bc_load(dst_ap, src_handle, src_off, ncols, waits):
        return E("gpsimd",
                 lambda g, d=dst_ap, h=src_handle, o=src_off, n2=ncols:
                 g.dma_start(d, bass.AP(h, o, [[0, 128], [1, n2]])),
                 waits=waits, inc="dG", n=16)

    def ln_stats(tile_ap, u, col, xwait=()):
        b0 = u * 8
        ns, sq, vr, n2v, v2, sd = b0, b0 + 1, b0 + 2, b0 + 3, b0 + 4, b0 + 5
        v1 = E("vector", lambda v, a=tile_ap: v.reduce_sum(
            st[:, ns:ns + 1], a, axis=AX.X, negate=True),
            waits=xwait, inc="tV")
        s1 = E("scalar", lambda s, a=tile_ap: s.activation(
            fdump[:, :], a, AF.Square, accum_out=st[:, sq:sq + 1]),
            waits=[S["fdump"]] + list(xwait), inc="tS")
        S["fdump"] = ("tS", s1)
        v2s = E("vector", lambda v, c2=col: v.tensor_scalar(
            nm_all[:, c2:c2 + 1], st[:, ns:ns + 1], 1.0 / D, None, ALU.mult),
            waits=[("tV", v1)], inc="tV")
        v3 = E("vector", lambda v: v.tensor_scalar(
            st[:, vr:vr + 1], st[:, sq:sq + 1], 1.0 / D, None, ALU.mult),
            waits=[("tS", s1)], inc="tV")
        v4 = E("vector", lambda v, c2=col: v.tensor_scalar(
            st[:, n2v:n2v + 1], nm_all[:, c2:c2 + 1],
            nm_all[:, c2:c2 + 1], None, ALU.mult),
            waits=[("tV", v2s)], inc="tV")
        v5 = E("vector", lambda v: v.tensor_sub(
            st[:, v2:v2 + 1], st[:, vr:vr + 1], st[:, n2v:n2v + 1]),
            waits=[("tV", v4)], inc="tV")
        s2 = E("scalar", lambda s: s.activation(
            st[:, sd:sd + 1], st[:, v2:v2 + 1], AF.Sqrt,
            bias=eps_sb[:, 0:1]),
            waits=[("tV", v5), ("tG", ginit)], inc="tS")
        v6 = E("vector", lambda v, c2=col: v.reciprocal(
            rs_all[:, c2:c2 + 1], st[:, sd:sd + 1]),
            waits=[("tS", s2)], inc="tV")
        return s1, v6

    def ln_apply(tile_ap, u, col, wrep_ap, brep_ap, sq_snap, rstd_snap,
                 extra=()):
        E("vector", lambda v, a=tile_ap, c2=col: v.tensor_scalar(
            fdump[:, :], a, nm_all[:, c2:c2 + 1], rs_all[:, c2:c2 + 1],
            ALU.add, ALU.mult),
            waits=[("tV", rstd_snap), S["fdump"]] + list(extra), inc="tV")
        E("vector", lambda v, w=wrep_ap: v.tensor_mul(
            fdump[:, :], fdump[:, :], w), inc=None)
        vc = E("vector", lambda v, b2=brep_ap, u2=u: v.tensor_add(
            sc_xn[:, u2, :], fdump[:, :], b2), inc="tV")
        S["fdump"] = ("tV", vc)
        return vc

    def transposes(src_view, src_wait, dst_ap, nblk=8, extra=()):
        last_t = None
        for j in range(nblk):
            w = []
            if j == 0:
                w = [src_wait, S["tr"], ("tG", ginit)] + list(extra)
            last_t = E("tensor",
                       lambda t, jj=j, sv=src_view: t.transpose(
                           ps_tr[:, jj * 128:(jj + 1) * 128],
                           sv(jj), ident[:, :]),
                       waits=w, inc=("tT" if j == nblk - 1 else None))
        ev = E("vector", lambda v, d=dst_ap, n2=nblk: v.tensor_copy(
            d, ps_tr[:, 0:n2 * 128]),
            waits=[("tT", last_t)], inc="tV")
        S["tr"] = ("tV", ev)
        return ev

    for b in range(1 if only_b0 else NB):
        if b > 0:
            bar = [("tT", C["tT"]), ("tV", C["tV"]), ("tS", C["tS"]),
                   ("tG", C["tG"]), ("dS", C["dS"]), ("dG", C["dG"])]
            for eng in ENGINES:
                E(eng, None, waits=bar)
        xl = None
        for ii in range(0, NT, 2):
            src = bass.AP(x_in, b * 2048 * D + ii * 128 * D,
                          [[D, 128], [128 * D, 2], [1, D]])
            xl = E("sync", lambda s, d=x_sb[:, ii:ii + 2, :], sr=src:
                   s.dma_start(d, sr),
                   waits=([S["x_free"]] if ii == 0 else ()),
                   inc="dS", n=16)
        xload = xl

        for l in range(NL):
            for ri, hsrc in enumerate((ln1_w, ln1_b, ln2_w, ln2_b)):
                bc_load(rep[:, ri, :], hsrc, l * D, D,
                        [S["rep"]] if ri == 0 else ())
            E("gpsimd", lambda g, o=l * D: g.dma_start(
                borow[:, :], bass.AP(bo_d, o, [[D, 1], [1, D]])),
                inc="dG", n=16)
            rep_ready = C["dG"]

            i_wk, w = alloc_panel()
            wload2(i_wk, wk_d, l * D * D, D, D, w)
            kv_dg = panel_loaded[i_wk]
            i_wv, w = alloc_panel()
            wload2(i_wv, wv_d, l * D * D, D, D, w)
            kv_dg = panel_loaded[i_wv]
            i_wq, w = alloc_panel()
            wload2(i_wq, wq_d, l * D * D, D, D, w)
            i_wo, w = alloc_panel()
            wload2(i_wo, wo_d, l * D * D, D, D, w)

            uph = [("tT", C["tT"])]

            snap_ctx = None
            for i in range(NT):
                u = i & 1
                x_i = x_sb[:, i, :]
                sq_s, rstd_s = ln_stats(x_i, u, i, xwait=[("dS", xload)])
                xn_s = ln_apply(x_i, u, i, rep[:, 0, :], rep[:, 1, :],
                                sq_s, rstd_s, extra=[("dG", rep_ready)])
                if b == 0 and l == 0 and i == 0:
                    dump("xn0", sc_xn[:, 0, :], [128, D], BF16, cast=True)
                xnT_s = transposes(
                    lambda j, uu=u: sc_xn[:, uu, j * 128:(j + 1) * 128],
                    ("tV", xn_s), sc_xnT[:, u, :],
                    extra=(uph if i <= 1 else ()))
                if b == 0 and l == 0 and i == 0:
                    dump("xnT0", sc_xnT[:, 0, :], [128, D], BF16, cast=True)
                def projmm(dst, pan_idx, warpair, xnTs, uu):
                    pan = pans[pan_idx]
                    last = None
                    for n0 in (0, 512):
                        for j in range(NJ):
                            w2 = []
                            if n0 == 0 and j == 0:
                                w2 = [("tV", xnTs), warpair,
                                      ("dG", panel_loaded[pan_idx])]
                            last = E("tensor",
                                     lambda t, jj=j, nn=n0, u3=uu, d=dst,
                                     pn=pan:
                                     t.matmul(d[:, nn:nn + 512],
                                              sc_xnT[:, u3,
                                                     jj * 128:(jj + 1) * 128],
                                              pn[:, jj * D + nn:
                                                 jj * D + nn + 512],
                                              start=(jj == 0),
                                              stop=(jj == NJ - 1)),
                                     waits=w2,
                                     inc=("tT" if (n0 == 512 and j == NJ - 1)
                                          else None))
                    return last
                k_s = projmm(ps_pA, i_wk, S["pA"], xnT_s, u)
                v_s = projmm(ps_pB, i_wv, S["pB"], xnT_s, u)
                ek = E("scalar", lambda s, uu=u: s.activation(
                    sc_a[:, uu, :], ps_pA[:, :], AF.Exp),
                    waits=[("tT", k_s)] + (uph if i <= 1 else []), inc="tS")
                S["pA"] = ("tS", ek)
                vv = E("vector", lambda v, uu=u: v.tensor_copy(
                    sc_b[:, uu, :], ps_pB[:, :]),
                    waits=[("tT", v_s)], inc="tV")
                S["pB"] = ("tV", vv)
                if b == 0 and l == 0 and i == 0:
                    dump("ek0", sc_a[:, 0, :], [128, D], BF16, cast=True)
                    dump("v0", sc_b[:, 0, :], [128, D], BF16, cast=True)
                w0 = [("tS", ek), ("tV", vv)]
                if i == 0:
                    w0 += [S["ctxps"], S["ms"]]
                for h in range(NH):
                    E("tensor", lambda t, hh=h, uu=u, ii=i: t.matmul(
                        ps_ctx[:, hh * 128:(hh + 1) * 128],
                        sc_a[:, uu, hh * 128:(hh + 1) * 128],
                        sc_b[:, uu, hh * 128:(hh + 1) * 128],
                        start=(ii == 0), stop=(ii == NT - 1)),
                        waits=(w0 if h == 0 else ()), inc=None)
                    snap_ctx = E("tensor", lambda t, hh=h, uu=u, ii=i:
                                 t.matmul(
                                     ps_ms[:, hh:hh + 1],
                                     sc_a[:, uu, hh * 128:(hh + 1) * 128],
                                     ones_bf[:, :],
                                     start=(ii == 0), stop=(ii == NT - 1)),
                                 inc=("tT" if h == NH - 1 else None))
            free_panel(i_wk)
            free_panel(i_wv)
            v11 = E("vector", lambda v: v.reciprocal(
                dnr[:, 0:8], ps_ms[:, 0:8]),
                waits=[("tT", snap_ctx)], inc="tV")
            S["ms"] = ("tV", v11)
            ctxsb_s = None
            for h in range(NH):
                ctxsb_s = E("vector", lambda v, hh=h: v.tensor_scalar(
                    ctx_sb[:, hh * 128:(hh + 1) * 128],
                    ps_ctx[:, hh * 128:(hh + 1) * 128],
                    dnr[:, hh:hh + 1], None, ALU.mult),
                    waits=([("tV", v11)] if h == 0 else ()),
                    inc=("tV" if h == NH - 1 else None))
            S["ctxps"] = ("tV", ctxsb_s)
            if b == 0 and l == 0:
                dump("ctx", ctx_sb[:, :], [128, D], BF16, cast=True)
                dump("dnr", dnr[:, 0:8], [128, 8], F32)

            for i in range(NT):
                u = i & 1
                x_i = x_sb[:, i, :]
                E("vector", lambda v, a=x_i, ii=i: v.tensor_scalar(
                    fdump[:, :], a, nm_all[:, ii:ii + 1],
                    rs_all[:, ii:ii + 1], ALU.add, ALU.mult),
                    waits=[S["fdump"]], inc="tV")
                E("vector", lambda v: v.tensor_mul(
                    fdump[:, :], fdump[:, :], rep[:, 0, :]), inc=None)
                xn_s = E("vector", lambda v, uu=u: v.tensor_add(
                    sc_xn[:, uu, :], fdump[:, :], rep[:, 1, :]), inc="tV")
                S["fdump"] = ("tV", xn_s)
                xnT_s = transposes(
                    lambda j, uu=u: sc_xn[:, uu, j * 128:(j + 1) * 128],
                    ("tV", xn_s), sc_xnT[:, u, :])
                pan = pans[i_wq]
                q_s = None
                for n0 in (0, 512):
                    for j in range(NJ):
                        w2 = []
                        if n0 == 0 and j == 0:
                            w2 = [("tV", xnT_s), S["pA"],
                                  ("dG", panel_loaded[i_wq])]
                        q_s = E("tensor",
                                lambda t, jj=j, nn=n0, uu=u, pn=pan:
                                t.matmul(ps_pA[:, nn:nn + 512],
                                         sc_xnT[:, uu,
                                                jj * 128:(jj + 1) * 128],
                                         pn[:, jj * D + nn:jj * D + nn + 512],
                                         start=(jj == 0), stop=(jj == NJ - 1)),
                                waits=w2,
                                inc=("tT" if (n0 == 512 and j == NJ - 1)
                                     else None))
                eq_s = None
                for h in range(NH):
                    eq_s = E("scalar", lambda s, hh=h, uu=u: s.activation(
                        sc_a[:, uu, hh * 128:(hh + 1) * 128],
                        ps_pA[:, hh * 128:(hh + 1) * 128], AF.Exp,
                        accum_out=qrs[:, uu * 8 + hh:uu * 8 + hh + 1]),
                        waits=([("tT", q_s)] if h == 0 else ()),
                        inc=("tS" if h == NH - 1 else None))
                S["pA"] = ("tS", eq_s)
                qr_s = E("vector", lambda v, uu=u: v.reciprocal(
                    qrcp[:, uu * 8:uu * 8 + 8], qrs[:, uu * 8:uu * 8 + 8]),
                    waits=[("tS", eq_s)], inc="tV")
                sqm_s = None
                for h in range(NH):
                    sqm_s = E("vector", lambda v, hh=h, uu=u: v.tensor_scalar(
                        sc_b[:, uu, hh * 128:(hh + 1) * 128],
                        sc_a[:, uu, hh * 128:(hh + 1) * 128],
                        qrcp[:, uu * 8 + hh:uu * 8 + hh + 1], None, ALU.mult),
                        waits=([("tV", qr_s)] if h == 0 else ()),
                        inc=("tV" if h == NH - 1 else None))
                sqT_s = transposes(
                    lambda j, uu=u: sc_b[:, uu, j * 128:(j + 1) * 128],
                    ("tV", sqm_s), sc_a[:, u, :])
                at_s = None
                for h in range(NH):
                    w2 = []
                    if h == 0:
                        w2 = [("tV", sqT_s), ("tV", ctxsb_s), S["pB"]]
                    at_s = E("tensor", lambda t, hh=h, uu=u: t.matmul(
                        ps_pB[:, hh * 128:(hh + 1) * 128],
                        ctx_sb[:, hh * 128:(hh + 1) * 128],
                        sc_a[:, uu, hh * 128:(hh + 1) * 128],
                        start=True, stop=True),
                        waits=w2, inc=("tT" if h == NH - 1 else None))
                atT_s = E("vector", lambda v, uu=u: v.tensor_copy(
                    sc_c[:, uu, :], ps_pB[:, :]),
                    waits=[("tT", at_s)], inc="tV")
                S["pB"] = ("tV", atT_s)
                pan = pans[i_wo]
                y_s = None
                for n0 in (0, 512):
                    for h in range(NH):
                        w2 = []
                        if n0 == 0 and h == 0:
                            w2 = [("tV", atT_s), ("tS", eq_s),
                                  ("dG", panel_loaded[i_wo]),
                                  ("dG", rep_ready)]
                        E("tensor", lambda t, hh=h, nn=n0, uu=u, pn=pan:
                          t.matmul(ps_pA[:, nn:nn + 512],
                                   sc_c[:, uu, hh * 128:(hh + 1) * 128],
                                   pn[:, hh * D + nn:hh * D + nn + 512],
                                   start=(hh == 0), stop=False),
                          waits=w2, inc=None)
                    y_s = E("tensor", lambda t, nn=n0: t.matmul(
                        ps_pA[:, nn:nn + 512], ones_row[:, :],
                        borow[:, nn:nn + 512], start=False, stop=True),
                        inc=("tT" if n0 == 512 else None))
                xres = E("vector", lambda v, a=x_i: v.tensor_add(
                    a, a, ps_pA[:, :]),
                    waits=[("tT", y_s)], inc="tV")
                S["pA"] = ("tV", xres)
            free_panel(i_wq)
            free_panel(i_wo)
            if b == 0 and l == 0:
                dump("x_p1b", x_sb[:, :, :], [128, NT * D], F32)

            uph3 = [("tT", C["tT"]), ("tS", C["tS"])]
            for half in range(2):
                tbase = half * 8
                x2T_last = None
                for tt in range(8):
                    i = tbase + tt
                    u = i & 1
                    x_i = x_sb[:, i, :]
                    sq_s, rstd_s = ln_stats(x_i, u, i)
                    xn_s = ln_apply(x_i, u, i, rep[:, 2, :], rep[:, 3, :],
                                    sq_s, rstd_s)
                    dst = bass.AP(x2T, tt * 128,
                                  [[16384 // 2, 128], [1024, NJ], [1, 128]])
                    x2T_last = transposes(
                        lambda j, uu=u: sc_xn[:, uu, j * 128:(j + 1) * 128],
                        ("tV", xn_s), dst,
                        extra=(uph3 if (half == 0 and tt <= 1) else ()))
                for q in range(NQ4):
                    i_w1, w = alloc_panel()
                    wload2(i_w1, w1_d, l * D * FF + q * 1024, FF, 1024, w)
                    i_w2, w = alloc_panel()
                    wload2(i_w2, w2_d, l * FF * D + q * 1024 * D, D, 1024, w)
                    pw1, pw2 = pans[i_w1], pans[i_w2]
                    for gl in range(4):
                        t0 = tbase + 2 * gl
                        y2_last = None
                        for c in range(NC8):
                            hf = (c & 1) * 256
                            lastmm = None
                            for j in range(NJ):
                                w2 = []
                                if j == 0:
                                    w2 = [("tV", x2T_last), S["ms"],
                                          ("dG", panel_loaded[i_w1])]
                                lastmm = E(
                                    "tensor",
                                    lambda t, jj=j, h2=hf, cc=c, g2=gl,
                                    p1=pw1:
                                    t.matmul(
                                        ps_ms[:, h2:h2 + 256],
                                        p1[:, jj * 1024 + cc * 128:
                                           jj * 1024 + cc * 128 + 128],
                                        bass.AP(x2T,
                                                jj * 1024 + g2 * 256,
                                                [[8192, 128], [1, 256]]),
                                        start=(jj == 0), stop=(jj == NJ - 1)),
                                    waits=w2,
                                    inc=("tT" if j == NJ - 1 else None))
                            E("vector", lambda v, h2=hf: v.tensor_scalar(
                                ldump[:, :], ps_ms[:, h2:h2 + 256], 0.01,
                                None, ALU.mult),
                                waits=[("tT", lastmm)], inc=None)
                            lk = E("vector", lambda v, h2=hf, cc=c:
                                   v.tensor_max(
                                       sc_c[:, cc & 1, 0:256], ldump[:, :],
                                       ps_ms[:, h2:h2 + 256]), inc="tV")
                            S["ms"] = ("tV", lk)
                            for sub in range(2):
                                dst = ps_pA if sub == 0 else ps_pB
                                for n0 in (0, 512):
                                    w2 = []
                                    if sub == 0 and n0 == 0:
                                        w2 = [("tV", lk),
                                              ("dG", panel_loaded[i_w2])]
                                        if c == 0:
                                            w2 += [S["pA"], S["pB"]]
                                    y2_last = E(
                                        "tensor",
                                        lambda t, ss=sub, nn=n0, cc=c, d=dst,
                                        p2=pw2:
                                        t.matmul(
                                            d[:, nn:nn + 512],
                                            sc_c[:, cc & 1,
                                                 ss * 128:ss * 128 + 128],
                                            p2[:, cc * 1024 + nn:
                                               cc * 1024 + nn + 512],
                                            start=(cc == 0),
                                            stop=(cc == NC8 - 1)),
                                        waits=w2,
                                        inc=("tT" if (sub == 1 and n0 == 512
                                                      and c == NC8 - 1)
                                             else None))
                        ev = None
                        for sub in range(2):
                            src = ps_pA if sub == 0 else ps_pB
                            ev = E("vector",
                                   lambda v, ss=sub, s2=src, t2=t0:
                                   v.tensor_add(x_sb[:, t2 + ss, :],
                                                x_sb[:, t2 + ss, :],
                                                s2[:, :]),
                                   waits=([("tT", y2_last)] if sub == 0
                                          else ()),
                                   inc=("tV" if sub == 1 else None))
                        S["pA"] = ("tV", ev)
                        S["pB"] = ("tV", ev)
                    free_panel(i_w1)
                    free_panel(i_w2)
            if b == 0 and l == 0:
                dump("x_p3", x_sb[:, :, :], [128, NT * D], F32)
            S["rep"] = S["fdump"] if S["fdump"][0] == "tV" else ("tV", C["tV"])

        uphm = [("tT", C["tT"])]
        i_sp, sp_w = alloc_panel()
        pan_sp = pans[i_sp]
        mu_off = 4096
        so_off = 6144

        bc_load(rep[:, 0, :], fgamma, 0, D, [S["rep"]])
        bc_load(rep[:, 1, :], fbeta, 0, D, ())
        mod_rep_dg = C["dG"]
        mu_mm = 0
        mu_cp = 0
        last_mm = None
        for j in range(NJ):
            dl = E("sync", lambda s, jj=j: s.dma_start(
                mu_tmp[:, :], fmu[jj * 128:(jj + 1) * 128, :]),
                waits=([("tT", mu_mm), ("tV", mu_cp)] if j > 0
                       else [S["fdump"], ("tS", C["tS"])]),
                inc="dS", n=16)
            sqv = E("vector", lambda v: v.tensor_mul(
                ldump[:, :], mu_tmp[:, 0:256], mu_tmp[:, 0:256]),
                waits=[("dS", dl), ("tT", mu_mm)], inc="tV")
            mu_cp = E("vector", lambda v, jj=j, sp=pan_sp: v.tensor_copy(
                sp[:, mu_off + jj * 256:mu_off + (jj + 1) * 256],
                mu_tmp[:, :]),
                waits=(sp_w if j == 0 else ()), inc="tV")
            last_mm = E("tensor", lambda t, jj=j: t.matmul(
                ps_ms[0:1, 0:256], ones_f32c[:, :], ldump[:, :],
                start=(jj == 0), stop=(jj == NJ - 1)),
                waits=[("tV", sqv)] + ([S["ms"]] if j == 0 else []),
                inc="tT")
            mu_mm = last_mm
        n1 = E("scalar", lambda s: s.activation(
            ldump[0:1, 0:256], ps_ms[0:1, 0:256], AF.Sqrt),
            waits=[("tT", last_mm)], inc="tS")
        S["ms"] = ("tS", n1)
        n2 = E("vector", lambda v: v.tensor_scalar(
            ldump[0:1, 0:256], ldump[0:1, 0:256], 1e-12, None, ALU.max),
            waits=[("tS", n1)], inc="tV")
        n3 = E("vector", lambda v: v.reciprocal(ldump[0:1, 0:256], ldump[0:1, 0:256]),
               waits=[("tV", n2)], inc="tV")
        n4 = E("vector", lambda v: v.tensor_scalar(
            ldump[0:1, 0:256], ldump[0:1, 0:256], fsc_rep[0:1, 0:1], None, ALU.mult),
            waits=[("tV", n3)], inc="tV")
        nrep = E("tensor", lambda t: t.matmul(
            ps_ctx[:, 0:256], ones_f32r[:, :], ldump[0:1, 0:256],
            start=True, stop=True),
            waits=[("tV", n4), S["ctxps"], ("tG", ginit)], inc="tT")
        mu_ready = None
        for j in range(NJ):
            mu_ready = E("vector", lambda v, jj=j, sp=pan_sp: v.tensor_mul(
                sp[:, mu_off + jj * 256:mu_off + (jj + 1) * 256],
                sp[:, mu_off + jj * 256:mu_off + (jj + 1) * 256],
                ps_ctx[:, 0:256]),
                waits=([("tT", nrep), ("tV", mu_cp)] if j == 0 else ()),
                inc=("tV" if j == NJ - 1 else None))
        S["ctxps"] = ("tV", mu_ready)
        gid = E("gpsimd", lambda g: (
            g.memset(ldump[:, 0:128], 0.0),
            g.affine_select(out=ldump[:, 0:128], in_=ldump[:, 0:128],
                            compare_op=ALU.not_equal, fill=1.0, base=0,
                            pattern=[[-1, 128]], channel_multiplier=1))[-1],
            waits=[("tV", mu_cp), ("tV", C["tV"])], inc="tG")
        d1 = E("sync", lambda s: s.dma_start(
            fdump[0:16, 0:256], fb1[:, :]),
            waits=[S["fdump"]], inc="dS", n=16)
        S["fdump"] = ("dS", d1)
        lastv = None
        for hc in range(2):
            w2 = [("dS", d1), ("tG", gid), S["pA"]] if hc == 0 else [
                ("tV", lastv)]
            t1 = E("tensor", lambda t, h2=hc: t.transpose(
                ps_pA[:, h2 * 128:h2 * 128 + 16],
                fdump[0:16, h2 * 128:h2 * 128 + 128],
                ldump[0:16, 0:16]),
                waits=w2, inc="tT")
            lastv = E("vector", lambda v, h2=hc: v.tensor_copy(
                b1T[:, h2, :], ps_pA[:, h2 * 128:h2 * 128 + 16]),
                waits=[("tT", t1)], inc="tV")
        d2 = E("sync", lambda s: s.dma_start(fdump[0:16, :], fb2[:, :]),
               waits=[("tV", lastv)], inc="dS", n=16)
        S["fdump"] = ("dS", d2)
        for dc in range(NJ):
            w2 = [("dS", d2), ("tV", lastv)] if dc == 0 else [("tV", lastv)]
            t1 = E("tensor", lambda t, d3=dc: t.transpose(
                ps_pA[:, d3 * 128:d3 * 128 + 16],
                fdump[0:16, d3 * 128:d3 * 128 + 128],
                ldump[0:16, 0:16]),
                waits=w2, inc="tT")
            lastv = E("vector", lambda v, d3=dc: v.tensor_copy(
                b2T[:, d3, :], ps_pA[:, d3 * 128:d3 * 128 + 16]),
                waits=[("tT", t1)], inc="tV")
        S["pA"] = ("tV", lastv)
        S["fdump"] = ("tV", lastv)

        quartets = {}

        def load_quartet(tq):
            iq1, w = alloc_panel()
            for e2 in range(4):
                for j2 in range(NJ):
                    src = bass.AP(fw1, (4 * tq + e2) * D * 256 + j2 * 128 * 256,
                                  [[256, 128], [1, 256]])
                    E("gpsimd",
                      lambda g, d=pans[iq1][:, e2 * 2048 + j2 * 256:
                                            e2 * 2048 + (j2 + 1) * 256],
                      s2=src: g.dma_start(d, s2),
                      waits=(w if (e2 == 0 and j2 == 0) else ()),
                      inc="dG", n=16)
            panel_loaded[iq1] = C["dG"]
            iq2, w = alloc_panel()
            for e2 in range(4):
                for h2 in range(2):
                    src = bass.AP(fw2, (4 * tq + e2) * 256 * D + h2 * 128 * D,
                                  [[D, 128], [1, D]])
                    E("gpsimd",
                      lambda g, d=pans[iq2][:, e2 * 2048 + h2 * 1024:
                                            e2 * 2048 + (h2 + 1) * 1024],
                      s2=src: g.dma_start(d, s2),
                      waits=(w if (e2 == 0 and h2 == 0) else ()),
                      inc="dG", n=16)
            panel_loaded[iq2] = C["dG"]
            quartets[tq] = (iq1, iq2)

        load_quartet(0)
        load_quartet(1)

        xio_free = {0: ("tV", 0), 1: ("tV", 0)}
        logit_free = ("tS", 0)
        slot_mm_last = None
        for i in range(NT):
            u = i & 1
            x_i = x_sb[:, i, :]
            src = bass.AP(x_in, b * 2048 * D + i * 128 * D,
                          [[D, 128], [1, D]])
            dl = E("sync", lambda s, uu=u, sr=src: s.dma_start(
                xio[:, uu, :], sr),
                waits=[xio_free[u]] + (uphm if i <= 1 else []),
                inc="dS", n=16)
            add_s = E("vector", lambda v, a=x_i, uu=u: v.tensor_add(
                a, a, xio[:, uu, :]),
                waits=[("dS", dl)], inc="tV")
            xio_free[u] = ("tV", add_s)
            sq_s, rstd_s = ln_stats(x_i, u, i, xwait=[("tV", add_s)])
            x1_s = E("vector", lambda v, a=x_i, ii=i: v.tensor_scalar(
                a, a, nm_all[:, ii:ii + 1], rs_all[:, ii:ii + 1],
                ALU.add, ALU.mult),
                waits=[("tV", rstd_s), S["fdump"]], inc="tV")
            sq2, rstd2 = ln_stats(x_i, u, i, xwait=[("tV", x1_s)])
            xn_s = ln_apply(x_i, u, i, rep[:, 0, :], rep[:, 1, :],
                            sq2, rstd2, extra=[("dG", mod_rep_dg)])
            s3 = E("scalar", lambda s, uu=u: s.activation(
                fdump[:, :], sc_xn[:, uu, :], AF.Square,
                accum_out=st[:, uu * 8 + 6:uu * 8 + 7]),
                waits=[("tV", xn_s)], inc="tS")
            S["fdump"] = ("tS", s3)
            s4 = E("scalar", lambda s, uu=u: s.activation(
                st[:, uu * 8 + 7:uu * 8 + 8], st[:, uu * 8 + 6:uu * 8 + 7],
                AF.Sqrt),
                waits=[("tS", s3)], inc="tS")
            v8 = E("vector", lambda v, uu=u: v.tensor_scalar(
                st[:, uu * 8 + 6:uu * 8 + 7], st[:, uu * 8 + 7:uu * 8 + 8],
                1e-12, None, ALU.max),
                waits=[("tS", s4)], inc="tV")
            v9 = E("vector", lambda v, uu=u: v.reciprocal(
                st[:, uu * 8 + 7:uu * 8 + 8], st[:, uu * 8 + 6:uu * 8 + 7]),
                waits=[("tV", v8)], inc="tV")
            rn_s = E("vector", lambda v, uu=u: v.tensor_scalar(
                st[:, uu * 8 + 6:uu * 8 + 7], st[:, uu * 8 + 7:uu * 8 + 8],
                fsc_rep[:, 0:1], None, ALU.mult),
                waits=[("tV", v9)], inc="tV")
            if b == 0 and i == 0:
                dump("rn0", st[:, 0:16], [128, 16], F32)
                dump("mus", pan_sp[:, 4096:6144], [128, 2048], BF16, cast=True)
                dump("xnm0", sc_xn[:, 0, :], [128, D], BF16, cast=True)
            xnT_s = transposes(
                lambda j, uu=u: sc_xn[:, uu, j * 128:(j + 1) * 128],
                ("tV", xn_s), sc_xnT[:, u, :],
                extra=(uphm if i <= 1 else ()))
            lg_s = None
            for j in range(NJ):
                w2 = []
                if j == 0:
                    w2 = [("tV", xnT_s), ("tV", mu_ready), logit_free,
                          S["ctxps"] if i == 0 else ("tV", 0)]
                lg_s = E("tensor", lambda t, jj=j, uu=u, sp=pan_sp: t.matmul(
                    ps_ctx[:, 0:256],
                    sc_xnT[:, uu, jj * 128:(jj + 1) * 128],
                    sp[:, mu_off + jj * 256:mu_off + (jj + 1) * 256],
                    start=(jj == 0), stop=(jj == NJ - 1)),
                    waits=w2, inc=("tT" if j == NJ - 1 else None))
            eE = E("scalar", lambda s, ii=i, uu=u, sp=pan_sp: s.activation(
                sp[:, ii * 256:(ii + 1) * 256], ps_ctx[:, 0:256],
                AF.Exp, scale=st[:, uu * 8 + 6:uu * 8 + 7]),
                waits=[("tT", lg_s), ("tV", rn_s)], inc="tS")
            logit_free = ("tS", eE)
            for esc in range(2):
                w2 = [("tS", eE)]
                if i == 0 and esc == 0:
                    w2 += [S["ms"], S["pA"], S["pB"]]
                E("tensor", lambda t, e2=esc, ii=i, sp=pan_sp: t.matmul(
                    ps_ms[:, e2:e2 + 1],
                    sp[:, ii * 256 + e2 * 128:ii * 256 + e2 * 128 + 128],
                    ones_bf[:, :],
                    start=(ii == 0), stop=(ii == NT - 1)),
                    waits=w2, inc=None)
                dst = ps_pA if esc == 0 else ps_pB
                for n0 in (0, 512):
                    slot_mm_last = E(
                        "tensor",
                        lambda t, e2=esc, nn=n0, ii=i, uu=u, d=dst,
                        sp=pan_sp:
                        t.matmul(
                            d[:, nn:nn + 512],
                            sp[:, ii * 256 + e2 * 128:
                               ii * 256 + e2 * 128 + 128],
                            sc_xn[:, uu, nn:nn + 512],
                            start=(ii == 0), stop=(ii == NT - 1)),
                        inc=("tT" if (esc == 1 and n0 == 512) else None))
        if b == 0:
            dump("x1", x_sb[:, :, :], [128, NT * D], F32)
            dump("E", pan_sp[:, 0:4096], [128, 4096], BF16, cast=True)
        vz = E("vector", lambda v: v.reciprocal(zrcp[:, :], ps_ms[:, 0:2]),
               waits=[("tT", slot_mm_last)], inc="tV")
        S["ms"] = ("tV", vz)
        si_s = None
        for esc in range(2):
            src = ps_pA if esc == 0 else ps_pB
            si_s = E("vector", lambda v, e2=esc, s2=src: v.tensor_scalar(
                sc_a[:, e2, :], s2[:, :], zrcp[:, e2:e2 + 1], None, ALU.mult),
                waits=([("tV", vz)] if esc == 0 else ()),
                inc=("tV" if esc == 1 else None))
        S["pA"] = ("tV", si_s)
        S["pB"] = ("tV", si_s)
        if b == 0:
            dump("zrcp", zrcp[:, :], [128, 2], F32)
            dump("slotin", sc_a[:, :, :], [128, 2 * D], BF16, cast=True)
        siT_s = None
        for esc in range(2):
            dstv = bass.AP(sc_b, esc * 128, [[2048, 128], [256, NJ], [1, 128]])
            siT_s = transposes(
                lambda j, e2=esc: sc_a[:, e2, j * 128:(j + 1) * 128],
                ("tV", si_s), dstv)
        so_free = ("tV", 0)
        gprev = ("tS", 0)
        so_evac = None
        for e in range(16):
            tq, eo = e // 4, e % 4
            iq1, iq2 = quartets[tq]
            pq1, pq2 = pans[iq1], pans[iq2]
            gel = {}
            for hc in range(2):
                last = None
                for j in range(NJ):
                    w2 = []
                    if j == 0 and hc == 0:
                        w2 = [("tV", siT_s), ("dG", panel_loaded[iq1]), gprev]
                    elif j == 0 and hc == 1:
                        w2 = [("tS", gel[0])]
                    last = E("tensor",
                             lambda t, jj=j, h2=hc, e2=eo, p1=pq1, ee=e:
                             t.matmul(
                                 ps_ms[:, 32 + h2 * 16:32 + h2 * 16 + 16],
                                 p1[:, e2 * 2048 + jj * 256 + h2 * 128:
                                    e2 * 2048 + jj * 256 + h2 * 128 + 128],
                                 bass.AP(sc_b, jj * 256 + ee * 16,
                                         [[2048, 128], [1, 16]]),
                                 start=(jj == 0), stop=(jj == NJ - 1)),
                             waits=w2, inc=("tT" if j == NJ - 1 else None))
                gel[hc] = E("scalar", lambda s, h2=hc, e2=e: s.activation(
                    hT[:, h2, :], ps_ms[:, 32 + h2 * 16:32 + h2 * 16 + 16],
                    AF.Gelu, bias=b1T[:, h2, e2:e2 + 1]),
                    waits=[("tT", last)], inc="tS")
            gprev = ("tS", gel[1])
            if b == 0 and e == 0 and "hT0" in dbg:
                dw = dump("hT0", hT[:, :, :], [128, 32], BF16, cast=True)
                if dw:
                    gprev = dw
            if b == 0 and e == 0 and "siT" in dbg:
                dump("siT", sc_b[:, :, :], [128, 2048], BF16, cast=True)
            if b == 0 and e == 0 and "fw2p" in dbg:
                dump("fw2p", pans[iq2][:, 0:2048], [128, 2048], BF16, cast=True)
                dump("b2Td", b2T[:, :, :], [128, 128], F32)
            last = None
            for dc in range(NJ):
                for hc in range(2):
                    w2 = []
                    if dc == 0 and hc == 0:
                        w2 = [("tS", gel[0]), ("tS", gel[1]),
                              ("dG", panel_loaded[iq2]), so_free]
                    last = E("tensor",
                             lambda t, d3=dc, h2=hc, e2=eo, p2=pq2:
                             t.matmul(
                                 ps_ctx[:, d3 * 16:d3 * 16 + 16],
                                 p2[:, e2 * 2048 + h2 * 1024 + d3 * 128:
                                    e2 * 2048 + h2 * 1024 + d3 * 128 + 128],
                                 hT[:, h2, :],
                                 start=(h2 == 0), stop=(h2 == 1)),
                             waits=w2, inc=("tT" if (dc == NJ - 1 and hc == 1)
                                            else None))
            for dc in range(NJ):
                so_evac = E("vector", lambda v, d3=dc, e2=e, sp=pan_sp:
                    v.tensor_scalar(
                    bass.AP(sp, so_off + d3 * 256 + e2 * 16,
                            [[8192, 128], [1, 16]]),
                    ps_ctx[:, d3 * 16:d3 * 16 + 16],
                    b2T[:, d3, e2:e2 + 1], None, ALU.add),
                    waits=([("tT", last)] if dc == 0 else ()),
                    inc=("tV" if dc == NJ - 1 else None))
            so_free = ("tV", so_evac)
            if eo == 3:
                free_panel(iq1)
                free_panel(iq2)
                if tq + 2 < 4:
                    load_quartet(tq + 2)
        S["ctxps"] = ("tV", so_evac)
        if b == 0:
            dump("soT", pan_sp[:, 6144:8192], [128, 2048], BF16, cast=True)
        so_s = None
        for esc in range(2):
            dstv = bass.AP(sc_b, esc * 1024, [[2048, 128], [1, 1024]])
            so_s = transposes(
                lambda dc, e2=esc, sp=pan_sp: sp[
                    :, so_off + dc * 256 + e2 * 128:
                    so_off + dc * 256 + e2 * 128 + 128],
                ("tV", so_evac), dstv)
        if b == 0:
            dump("so", sc_b[:, :, :], [128, 2 * D], BF16, cast=True)
        st_dma = {0: 0, 1: 0}
        for i in range(NT):
            u = i & 1
            x_i = x_sb[:, i, :]
            rsum = E("vector", lambda v, ii=i, uu=u, sp=pan_sp: v.reduce_sum(
                st[:, uu * 8:uu * 8 + 1],
                sp[:, ii * 256:(ii + 1) * 256], axis=AX.X),
                waits=[("tV", so_s)], inc="tV")
            rrec = E("vector", lambda v, uu=u: v.reciprocal(
                st[:, uu * 8 + 1:uu * 8 + 2], st[:, uu * 8:uu * 8 + 1]),
                waits=[("tV", rsum)], inc="tV")
            eT_s = transposes(
                lambda e2, ii=i, sp=pan_sp: sp[:, ii * 256 + e2 * 128:
                                               ii * 256 + e2 * 128 + 128],
                ("tV", 0), sc_a[:, u, 0:256], nblk=2)
            last = None
            for esc in range(2):
                for n0 in (0, 512):
                    w2 = []
                    if esc == 0 and n0 == 0:
                        w2 = [("tV", eT_s), S["pA"]]
                    last = E("tensor",
                             lambda t, e2=esc, nn=n0, uu=u: t.matmul(
                                 ps_pA[:, nn:nn + 512],
                                 sc_a[:, uu, e2 * 128:e2 * 128 + 128],
                                 bass.AP(sc_b, e2 * 1024 + nn,
                                         [[2048, 128], [1, 512]]),
                                 start=(e2 == 0), stop=(e2 == 1)),
                             waits=w2, inc=("tT" if (esc == 1 and n0 == 512)
                                            else None))
            x2 = E("vector", lambda v, a=x_i, uu=u: v.scalar_tensor_tensor(
                a, ps_pA[:, :], st[:, uu * 8 + 1:uu * 8 + 2], a,
                ALU.mult, ALU.add),
                waits=[("tT", last), ("tV", rrec)], inc="tV")
            S["pA"] = ("tV", x2)
            sq_s, rstd_s = ln_stats(x_i, u, i, xwait=[("tV", x2)])
            ov = E("vector", lambda v, a=x_i, ii=i, uu=u: v.tensor_scalar(
                xio[:, uu, :], a, nm_all[:, ii:ii + 1], rs_all[:, ii:ii + 1],
                ALU.add, ALU.mult),
                waits=[("tV", rstd_s), S["fdump"], ("dS", st_dma[u]),
                       xio_free[u]], inc="tV")
            xio_free[u] = ("tV", ov)
            dst = bass.AP(y_out, b * 2048 * D + i * 128 * D,
                          [[D, 128], [1, D]])
            st_dma[u] = E("sync", lambda s, uu=u, d2=dst: s.dma_start(
                d2, xio[:, uu, :]),
                waits=[("tV", ov)], inc="dS", n=16)
        if b == 0:
            dump("xend", x_sb[:, :, :], [128, NT * D], F32)
            dump("Eb0", pan_sp[:, 0:4096], [128, 4096], BF16, cast=True)
            dump("sob", sc_b[:, :, :], [128, 2048], BF16, cast=True)
        free_panel(i_sp)
        S["x_free"] = ("tV", C["tV"])
        S["rep"] = ("tV", C["tV"])

    E("sync", None, waits=[("dS", C["dS"])])

    semh = {s: es.enter_context(nc.semaphore(s)) for s in sems}
    block = es.enter_context(nc.Block())

    def mk_replay(name):
        def fn(engobj):
            waited = {}
            for f, ws, inc, n in p.ev[name]:
                for s2, v in ws:
                    if s2 == "zz_taken":
                        raise RuntimeError("panel used while reserved")
                    if waited.get(s2, 0) < v:
                        engobj.wait_ge(semh[s2], v)
                        waited[s2] = v
                ins = f(engobj) if f is not None else None
                if inc is not None:
                    if ins is not None:
                        ins.then_inc(semh[inc], n)
                    else:
                        engobj.sem_inc(semh[inc], n)
        return fn

    block.sync(mk_replay("sync"))
    block.tensor(mk_replay("tensor"))
    block.vector(mk_replay("vector"))
    block.scalar(mk_replay("scalar"))
    block.gpsimd(mk_replay("gpsimd"))

    es.close()
    return nc


def kernel(**inputs):
    f = lambda k: np.ascontiguousarray(np.asarray(inputs[k], dtype=np.float32))
    x = f("L0_attn_x")
    shared = {
        "ln1_w": f("L0_attn_ln1_w"), "ln1_b": f("L0_attn_ln1_b"),
        "ln2_w": f("L0_attn_ln2_w"), "ln2_b": f("L0_attn_ln2_b"),
        "bo_d": f("L0_attn_bo"),
        "wq_d": f("L0_attn_wq"), "wk_d": f("L0_attn_wk"),
        "wv_d": f("L0_attn_wv"), "wo_d": f("L0_attn_wo"),
        "w1_d": f("L0_attn_w1"), "w2_d": f("L0_attn_w2"),
        "fgamma": f("L0_ffn_ln_gamma"), "fbeta": f("L0_ffn_ln_beta"),
        "fmu": f("L0_ffn_mu").reshape(1024, 256),
        "fscale": f("L0_ffn_scale"),
        "fw1": f("L0_ffn_w1"), "fb1": f("L0_ffn_b1"),
        "fw2": f("L0_ffn_w2"), "fb2": f("L0_ffn_b2"),
    }
    nc = build_nc()
    in_maps = []
    for c in range(8):
        m = {"x_in": x[2 * c:2 * c + 2]}
        m.update(shared)
        in_maps.append(m)
    res = run_bass_kernel_spmd(nc, in_maps, core_ids=list(range(8)))
    out = np.concatenate([np.asarray(r["y_out"]) for r in res.results], axis=0)
    return out.astype(np.float32)


# version 02
# speedup vs baseline: 1.3988x; 1.3988x over previous
import numpy as np
from contextlib import ExitStack

import concourse.bass as bass
import concourse.mybir as mybir
from concourse.bass_utils import run_bass_kernel_spmd

F32 = mybir.dt.float32
BF16 = mybir.dt.bfloat16
AF = mybir.ActivationFunctionType
ALU = mybir.AluOpType
AX = mybir.AxisListType

ENGINES = ("sync", "tensor", "vector", "scalar", "gpsimd")

D = 1024
NT = 16
NJ = 8
NH = 8
FF = 4096
NQ4 = 4
NC8 = 8


class Prog:
    def __init__(self, semnames):
        self.ev = {e: [] for e in ENGINES}
        self.cnt = {s: 0 for s in semnames}

    def emit(self, eng, fn=None, waits=(), inc=None, n=1):
        ws = tuple((s, v) for (s, v) in waits if v and v > 0)
        self.ev[eng].append((fn, ws, inc, n))
        if inc is not None:
            self.cnt[inc] += n
            return self.cnt[inc]
        return None


def build_nc(NB=2, NL=6, dbg=(), only_b0=False):
    nc = bass.Bass("TRN2", target_bir_lowering=False)

    x_in = nc.dram_tensor("x_in", [NB, 2048, D], F32, kind="ExternalInput")
    ln1_w = nc.dram_tensor("ln1_w", [6, D], F32, kind="ExternalInput")
    ln1_b = nc.dram_tensor("ln1_b", [6, D], F32, kind="ExternalInput")
    ln2_w = nc.dram_tensor("ln2_w", [6, D], F32, kind="ExternalInput")
    ln2_b = nc.dram_tensor("ln2_b", [6, D], F32, kind="ExternalInput")
    bo_d = nc.dram_tensor("bo_d", [6, D], F32, kind="ExternalInput")
    wq_d = nc.dram_tensor("wq_d", [6, D, D], F32, kind="ExternalInput")
    wk_d = nc.dram_tensor("wk_d", [6, D, D], F32, kind="ExternalInput")
    wv_d = nc.dram_tensor("wv_d", [6, D, D], F32, kind="ExternalInput")
    wo_d = nc.dram_tensor("wo_d", [6, D, D], F32, kind="ExternalInput")
    w1_d = nc.dram_tensor("w1_d", [6, D, FF], F32, kind="ExternalInput")
    w2_d = nc.dram_tensor("w2_d", [6, FF, D], F32, kind="ExternalInput")
    fgamma = nc.dram_tensor("fgamma", [D], F32, kind="ExternalInput")
    fbeta = nc.dram_tensor("fbeta", [D], F32, kind="ExternalInput")
    fmu = nc.dram_tensor("fmu", [D, 256], F32, kind="ExternalInput")
    fscale = nc.dram_tensor("fscale", [1], F32, kind="ExternalInput")
    fw1 = nc.dram_tensor("fw1", [16, D, 256], F32, kind="ExternalInput")
    fb1 = nc.dram_tensor("fb1", [16, 256], F32, kind="ExternalInput")
    fw2 = nc.dram_tensor("fw2", [16, 256, D], F32, kind="ExternalInput")
    fb2 = nc.dram_tensor("fb2", [16, D], F32, kind="ExternalInput")
    y_out = nc.dram_tensor("y_out", [NB, 2048, D], F32, kind="ExternalOutput")

    es = ExitStack()
    sb = lambda n, s, d: es.enter_context(nc.sbuf_tensor(n, s, d))
    psa = lambda n, s, d: es.enter_context(nc.psum_tensor(n, s, d))

    x_sb = sb("x_sb", [128, NT, D], F32)
    pans = [sb(f"pan{i}", [128, 8192], BF16) for i in range(6)]
    u_arena = sb("u_arena", [128, 16384], mybir.dt.uint8)
    ua_addr = nc.lookup_mloc(u_arena).addr
    sc_xnT = nc.alloc_sbuf_tensor_at("sc_xnT", [128, 2, D], BF16,
                                     offset=ua_addr)
    sc_a = nc.alloc_sbuf_tensor_at("sc_a", [128, 2, D], BF16,
                                   offset=ua_addr + 4096)
    xio = nc.alloc_sbuf_tensor_at("xio", [128, 2, D], F32,
                                  offset=ua_addr + 8192)
    x2T = nc.alloc_sbuf_tensor_at("x2T", [128, NJ, 1024], BF16,
                                  offset=ua_addr)
    rep = sb("rep", [128, 4, D], BF16)
    borow = sb("borow", [1, D], BF16)
    ident = sb("ident", [128, 128], BF16)
    ones_bf = sb("ones_bf", [128, 1], BF16)
    ones_row = sb("ones_row", [1, 128], BF16)
    ones_f32c = sb("ones_f32c", [128, 1], F32)
    ones_f32r = sb("ones_f32r", [1, 128], F32)
    ctx_sb = sb("ctx_sb", [128, D], BF16)
    sc_xn = sb("sc_xn", [128, 2, D], BF16)
    sc_b = sb("sc_b", [128, 2, D], BF16)
    sc_c = sb("sc_c", [128, 2, D], BF16)
    fdump = sb("fdump", [128, D], F32)
    ldump = sb("ldump", [128, 256], F32)
    st = sb("st", [128, 16], F32)
    nm_all = sb("nm_all", [128, NT], F32)
    rs_all = sb("rs_all", [128, NT], F32)
    qrs = sb("qrs", [128, 16], F32)
    qrcp = sb("qrcp", [128, 16], F32)
    dnr = sb("dnr", [128, 8], F32)
    fsc_rep = sb("fsc_rep", [128, 1], F32)
    b1T = sb("b1T", [128, 2, 16], F32)
    b2T = sb("b2T", [128, 8, 16], F32)
    hT = sb("hT", [128, 2, 16], BF16)
    zrcp = sb("zrcp", [128, 2], F32)
    eps_sb = sb("eps_sb", [128, 1], F32)
    mu_tmp = nc.alloc_sbuf_tensor_at(
        "mu_tmp", [128, 256], F32,
        offset=nc.lookup_mloc(fdump).addr + 1024)

    ps_pA = psa("ps_pA", [128, 1024], F32)
    ps_pB = psa("ps_pB", [128, 1024], F32)
    ps_ctx = psa("ps_ctx", [128, 1024], F32)
    ps_tr = psa("ps_tr", [128, 1024], BF16)
    ps_ms = psa("ps_ms", [128, 512], F32)

    sems = ("dG", "dS", "tT", "tV", "tS", "tG")
    p = Prog(sems)
    E = p.emit
    C = p.cnt

    S = {
        "fdump": ("tV", 0),
        "pA": ("tV", 0),
        "pB": ("tV", 0),
        "ctxps": ("tV", 0),
        "ms": ("tV", 0),
        "tr": ("tV", 0),
        "rep": ("tV", 0),
        "x_free": ("tV", 0),
    }

    dbg_outs = {}

    def dump(tag, src_ap, shape, dt, cast=False):
        if tag not in dbg:
            return
        h = nc.dram_tensor("dbg_" + tag, shape, F32 if cast else dt,
                           kind="ExternalOutput")
        eng = "gpsimd" if cast else "sync"
        sem = "dG" if eng == "gpsimd" else "dS"
        snapd = E(eng, lambda g, d=h, s2=src_ap: g.dma_start(
            bass.AP(d, 0, [[int(np.prod(shape[1:])), shape[0]]] +
                    [[1, int(np.prod(shape[1:]))]]), s2),
          waits=[("tT", C["tT"]), ("tV", C["tV"]), ("tS", C["tS"]),
                 ("tG", C["tG"])],
          inc=sem, n=16)
        dbg_outs[tag] = True
        return (sem, snapd)

    def init_fn(g):
        g.memset(ident[:, :], 0.0)
        g.affine_select(out=ident[:, :], in_=ident[:, :],
                        compare_op=ALU.not_equal, fill=1.0, base=0,
                        pattern=[[-1, 128]], channel_multiplier=1)
        g.memset(ones_bf[:, :], 1.0)
        g.memset(ones_row[:, :], 1.0)
        g.memset(ones_f32c[:, :], 1.0)
        g.memset(eps_sb[:, :], 1e-5)
        return g.memset(ones_f32r[:, :], 1.0)
    ginit = E("gpsimd", init_fn, inc="tG")
    E("gpsimd", lambda g: g.dma_start(
        fsc_rep[:, :], bass.AP(fscale, 0, [[0, 128], [1, 1]])),
        inc="dG", n=16)

    panel_free = [[] for _ in range(6)]
    panel_loaded = [0] * 6

    def alloc_panel():
        def key(i):
            f = panel_free[i]
            return max((v for (_, v) in f), default=0)
        idx = min(range(6), key=key)
        w = list(panel_free[idx])
        panel_free[idx] = [("zz_taken", 10 ** 9)]
        return idx, w

    def free_panel(idx):
        panel_free[idx] = [("tT", C["tT"]), ("tV", C["tV"]), ("tS", C["tS"])]

    def wload2(idx, dram_h, base_off, row_stride, cols, w):
        pan = pans[idx]
        for j in range(NJ):
            src = bass.AP(dram_h, base_off + j * 128 * row_stride,
                          [[row_stride, 128], [1, cols]])
            E("gpsimd",
              lambda g, d=pan[:, j * cols:(j + 1) * cols], s2=src:
              g.dma_start(d, s2),
              waits=(w if j == 0 else ()), inc="dG", n=16)
        panel_loaded[idx] = C["dG"]

    def bc_load(dst_ap, src_handle, src_off, ncols, waits):
        return E("gpsimd",
                 lambda g, d=dst_ap, h=src_handle, o=src_off, n2=ncols:
                 g.dma_start(d, bass.AP(h, o, [[0, 128], [1, n2]])),
                 waits=waits, inc="dG", n=16)

    def ln_stats(tile_ap, u, col, xwait=()):
        b0 = u * 8
        ns, sq, vr, n2v, v2, sd = b0, b0 + 1, b0 + 2, b0 + 3, b0 + 4, b0 + 5
        v1 = E("vector", lambda v, a=tile_ap: v.reduce_sum(
            st[:, ns:ns + 1], a, axis=AX.X, negate=True),
            waits=xwait, inc="tV")
        s1 = E("scalar", lambda s, a=tile_ap: s.activation(
            fdump[:, :], a, AF.Square, accum_out=st[:, sq:sq + 1]),
            waits=[S["fdump"]] + list(xwait), inc="tS")
        S["fdump"] = ("tS", s1)
        v2s = E("vector", lambda v, c2=col: v.tensor_scalar(
            nm_all[:, c2:c2 + 1], st[:, ns:ns + 1], 1.0 / D, None, ALU.mult),
            waits=[("tV", v1)], inc="tV")
        v3 = E("vector", lambda v: v.tensor_scalar(
            st[:, vr:vr + 1], st[:, sq:sq + 1], 1.0 / D, None, ALU.mult),
            waits=[("tS", s1)], inc="tV")
        v4 = E("vector", lambda v, c2=col: v.tensor_scalar(
            st[:, n2v:n2v + 1], nm_all[:, c2:c2 + 1],
            nm_all[:, c2:c2 + 1], None, ALU.mult),
            waits=[("tV", v2s)], inc="tV")
        v5 = E("vector", lambda v: v.tensor_sub(
            st[:, v2:v2 + 1], st[:, vr:vr + 1], st[:, n2v:n2v + 1]),
            waits=[("tV", v4)], inc="tV")
        s2 = E("scalar", lambda s: s.activation(
            st[:, sd:sd + 1], st[:, v2:v2 + 1], AF.Sqrt,
            bias=eps_sb[:, 0:1]),
            waits=[("tV", v5), ("tG", ginit)], inc="tS")
        v6 = E("vector", lambda v, c2=col: v.reciprocal(
            rs_all[:, c2:c2 + 1], st[:, sd:sd + 1]),
            waits=[("tS", s2)], inc="tV")
        return s1, v6

    def ln_apply(tile_ap, u, col, wrep_ap, brep_ap, sq_snap, rstd_snap,
                 extra=()):
        E("vector", lambda v, a=tile_ap, c2=col: v.tensor_scalar(
            fdump[:, :], a, nm_all[:, c2:c2 + 1], rs_all[:, c2:c2 + 1],
            ALU.add, ALU.mult),
            waits=[("tV", rstd_snap), S["fdump"]] + list(extra), inc="tV")
        E("vector", lambda v, w=wrep_ap: v.tensor_mul(
            fdump[:, :], fdump[:, :], w), inc=None)
        vc = E("vector", lambda v, b2=brep_ap, u2=u: v.tensor_add(
            sc_xn[:, u2, :], fdump[:, :], b2), inc="tV")
        S["fdump"] = ("tV", vc)
        return vc

    def transposes(src_view, src_wait, dst_ap, nblk=8, extra=()):
        last_t = None
        for j in range(nblk):
            w = []
            if j == 0:
                w = [src_wait, S["tr"], ("tG", ginit)] + list(extra)
            last_t = E("tensor",
                       lambda t, jj=j, sv=src_view: t.transpose(
                           ps_tr[:, jj * 128:(jj + 1) * 128],
                           sv(jj), ident[:, :]),
                       waits=w, inc=("tT" if j == nblk - 1 else None))
        ev = E("vector", lambda v, d=dst_ap, n2=nblk: v.tensor_copy(
            d, ps_tr[:, 0:n2 * 128]),
            waits=[("tT", last_t)], inc="tV")
        S["tr"] = ("tV", ev)
        return ev

    for b in range(1 if only_b0 else NB):
        if b > 0:
            bar = [("tT", C["tT"]), ("tV", C["tV"]), ("tS", C["tS"]),
                   ("tG", C["tG"]), ("dS", C["dS"]), ("dG", C["dG"])]
            for eng in ENGINES:
                E(eng, None, waits=bar)
        xl = None
        for ii in range(0, NT, 2):
            src = bass.AP(x_in, b * 2048 * D + ii * 128 * D,
                          [[D, 128], [128 * D, 2], [1, D]])
            xl = E("sync", lambda s, d=x_sb[:, ii:ii + 2, :], sr=src:
                   s.dma_start(d, sr),
                   waits=([S["x_free"]] if ii == 0 else ()),
                   inc="dS", n=16)
        xload = xl

        for l in range(NL):
            for ri, hsrc in enumerate((ln1_w, ln1_b, ln2_w, ln2_b)):
                bc_load(rep[:, ri, :], hsrc, l * D, D,
                        [S["rep"]] if ri == 0 else ())
            E("gpsimd", lambda g, o=l * D: g.dma_start(
                borow[:, :], bass.AP(bo_d, o, [[D, 1], [1, D]])),
                inc="dG", n=16)
            rep_ready = C["dG"]

            i_wk, w = alloc_panel()
            wload2(i_wk, wk_d, l * D * D, D, D, w)
            kv_dg = panel_loaded[i_wk]
            i_wv, w = alloc_panel()
            wload2(i_wv, wv_d, l * D * D, D, D, w)
            kv_dg = panel_loaded[i_wv]
            i_wq, w = alloc_panel()
            wload2(i_wq, wq_d, l * D * D, D, D, w)
            i_wo, w = alloc_panel()
            wload2(i_wo, wo_d, l * D * D, D, D, w)

            uph = [("tT", C["tT"])]

            pank = pans[i_wk]
            panv = pans[i_wv]
            xnT_snap = {}
            ek_snap = {}
            vv_snap = {}
            snap_ctx = None
            for k_it in range(NT + 2):
                if k_it < NT:
                    i = k_it
                    u = i & 1
                    x_i = x_sb[:, i, :]
                    sq_s, rstd_s = ln_stats(x_i, u, i, xwait=[("dS", xload)])
                    xn_s = ln_apply(x_i, u, i, rep[:, 0, :], rep[:, 1, :],
                                    sq_s, rstd_s, extra=[("dG", rep_ready)])
                    xnT_snap[i] = transposes(
                        lambda j, uu=u: sc_xn[:, uu, j * 128:(j + 1) * 128],
                        ("tV", xn_s), sc_xnT[:, u, :],
                        extra=(uph if i <= 1 else ()))
                if 1 <= k_it <= NT:
                    i = k_it - 1
                    u = i & 1
                    last = None
                    for n0 in (0, 512):
                        for j in range(NJ):
                            w2 = []
                            if n0 == 0 and j == 0:
                                w2 = [("tV", xnT_snap[i]),
                                      ("tS", ek_snap.get(i - 1, 0)),
                                      ("dG", panel_loaded[i_wk])]
                            last = E("tensor",
                                     lambda t, jj=j, nn=n0, uu=u, pn=pank:
                                     t.matmul(ps_pA[:, nn:nn + 512],
                                              sc_xnT[:, uu,
                                                     jj * 128:(jj + 1) * 128],
                                              pn[:, jj * D + nn:
                                                 jj * D + nn + 512],
                                              start=(jj == 0),
                                              stop=(jj == NJ - 1)),
                                     waits=w2,
                                     inc=("tT" if (n0 == 512 and j == NJ - 1)
                                          else None))
                    kmm = last
                    for n0 in (0, 512):
                        for j in range(NJ):
                            w2 = []
                            if n0 == 0 and j == 0:
                                w2 = [("tV", vv_snap.get(i - 1, 0)),
                                      ("dG", panel_loaded[i_wv])]
                            last = E("tensor",
                                     lambda t, jj=j, nn=n0, uu=u, pn=panv:
                                     t.matmul(ps_pB[:, nn:nn + 512],
                                              sc_xnT[:, uu,
                                                     jj * 128:(jj + 1) * 128],
                                              pn[:, jj * D + nn:
                                                 jj * D + nn + 512],
                                              start=(jj == 0),
                                              stop=(jj == NJ - 1)),
                                     waits=w2,
                                     inc=("tT" if (n0 == 512 and j == NJ - 1)
                                          else None))
                    vmm = last
                    ek_snap[i] = E("scalar", lambda s, uu=u: s.activation(
                        sc_a[:, uu, :], ps_pA[:, :], AF.Exp),
                        waits=[("tT", kmm)] + (uph if i <= 1 else []),
                        inc="tS")
                    vv_snap[i] = E("vector", lambda v, uu=u: v.tensor_copy(
                        sc_b[:, uu, :], ps_pB[:, :]),
                        waits=[("tT", vmm)], inc="tV")
                if 2 <= k_it and k_it - 2 <= NT - 1:
                    i = k_it - 2
                    u = i & 1
                    w0 = [("tS", ek_snap[i]), ("tV", vv_snap[i])]
                    if i == 0:
                        w0 += [S["ctxps"], S["ms"]]
                    for h in range(NH):
                        E("tensor", lambda t, hh=h, uu=u, ii=i: t.matmul(
                            ps_ctx[:, hh * 128:(hh + 1) * 128],
                            sc_a[:, uu, hh * 128:(hh + 1) * 128],
                            sc_b[:, uu, hh * 128:(hh + 1) * 128],
                            start=(ii == 0), stop=(ii == NT - 1)),
                            waits=(w0 if h == 0 else ()), inc=None)
                        snap_ctx = E("tensor", lambda t, hh=h, uu=u, ii=i:
                                     t.matmul(
                                         ps_ms[:, hh:hh + 1],
                                         sc_a[:, uu, hh * 128:(hh + 1) * 128],
                                         ones_bf[:, :],
                                         start=(ii == 0), stop=(ii == NT - 1)),
                                     inc=("tT" if h == NH - 1 else None))
            free_panel(i_wk)
            free_panel(i_wv)
            v11 = E("vector", lambda v: v.reciprocal(
                dnr[:, 0:8], ps_ms[:, 0:8]),
                waits=[("tT", snap_ctx)], inc="tV")
            S["ms"] = ("tV", v11)
            ctxsb_s = None
            for h in range(NH):
                ctxsb_s = E("vector", lambda v, hh=h: v.tensor_scalar(
                    ctx_sb[:, hh * 128:(hh + 1) * 128],
                    ps_ctx[:, hh * 128:(hh + 1) * 128],
                    dnr[:, hh:hh + 1], None, ALU.mult),
                    waits=([("tV", v11)] if h == 0 else ()),
                    inc=("tV" if h == NH - 1 else None))
            S["ctxps"] = ("tV", ctxsb_s)

            panq = pans[i_wq]
            pano = pans[i_wo]
            xnTb = {}
            eq_snap = {}
            sq_snap = {}
            sqT_snap = {}
            atT_snap = {}
            res_snap = {}
            for k_it in range(NT + 3):
                if k_it < NT:
                    i = k_it
                    u = i & 1
                    x_i = x_sb[:, i, :]
                    E("vector", lambda v, a=x_i, ii=i: v.tensor_scalar(
                        fdump[:, :], a, nm_all[:, ii:ii + 1],
                        rs_all[:, ii:ii + 1], ALU.add, ALU.mult),
                        waits=[S["fdump"]], inc="tV")
                    E("vector", lambda v: v.tensor_mul(
                        fdump[:, :], fdump[:, :], rep[:, 0, :]), inc=None)
                    xn_s = E("vector", lambda v, uu=u: v.tensor_add(
                        sc_xn[:, uu, :], fdump[:, :], rep[:, 1, :]), inc="tV")
                    S["fdump"] = ("tV", xn_s)
                    xnTb[i] = transposes(
                        lambda j, uu=u: sc_xn[:, uu, j * 128:(j + 1) * 128],
                        ("tV", xn_s), sc_xnT[:, u, :])
                if 1 <= k_it <= NT:
                    i = k_it - 1
                    u = i & 1
                    q_s = None
                    for n0 in (0, 512):
                        for j in range(NJ):
                            w2 = []
                            if n0 == 0 and j == 0:
                                w2 = [("tV", xnTb[i]),
                                      ("tS", eq_snap.get(i - 1, 0)),
                                      ("dG", panel_loaded[i_wq])]
                            q_s = E("tensor",
                                    lambda t, jj=j, nn=n0, uu=u, pn=panq:
                                    t.matmul(ps_pA[:, nn:nn + 512],
                                             sc_xnT[:, uu,
                                                    jj * 128:(jj + 1) * 128],
                                             pn[:, jj * D + nn:
                                                jj * D + nn + 512],
                                             start=(jj == 0),
                                             stop=(jj == NJ - 1)),
                                    waits=w2,
                                    inc=("tT" if (n0 == 512 and j == NJ - 1)
                                         else None))
                    last = None
                    for h in range(NH):
                        last = E("scalar", lambda s, hh=h, uu=u: s.activation(
                            sc_a[:, uu, hh * 128:(hh + 1) * 128],
                            ps_pA[:, hh * 128:(hh + 1) * 128], AF.Exp,
                            accum_out=qrs[:, uu * 8 + hh:uu * 8 + hh + 1]),
                            waits=([("tT", q_s)] if h == 0 else ()),
                            inc=("tS" if h == NH - 1 else None))
                    eq_snap[i] = last
                    qr_s = E("vector", lambda v, uu=u: v.reciprocal(
                        qrcp[:, uu * 8:uu * 8 + 8],
                        qrs[:, uu * 8:uu * 8 + 8]),
                        waits=[("tS", last)], inc="tV")
                    last = None
                    for h in range(NH):
                        last = E("vector",
                                 lambda v, hh=h, uu=u: v.tensor_scalar(
                                     sc_b[:, uu, hh * 128:(hh + 1) * 128],
                                     sc_a[:, uu, hh * 128:(hh + 1) * 128],
                                     qrcp[:, uu * 8 + hh:uu * 8 + hh + 1],
                                     None, ALU.mult),
                                 waits=([("tV", qr_s)] if h == 0 else ()),
                                 inc=("tV" if h == NH - 1 else None))
                    sq_snap[i] = last
                if 2 <= k_it <= NT + 1:
                    c = k_it - 2
                    u = c & 1
                    sqT_snap[c] = transposes(
                        lambda j, uu=u: sc_b[:, uu, j * 128:(j + 1) * 128],
                        ("tV", sq_snap[c]), sc_a[:, u, :])
                if 3 <= k_it <= NT + 2:
                    yi = k_it - 3
                    u = yi & 1
                    x_i = x_sb[:, yi, :]
                    y_s = None
                    for n0 in (0, 512):
                        for h in range(NH):
                            w2 = []
                            if n0 == 0 and h == 0:
                                w2 = [("tV", atT_snap[yi]),
                                      ("tV", res_snap.get(yi - 1, 0)),
                                      ("dG", panel_loaded[i_wo]),
                                      ("dG", rep_ready)]
                                if yi == 0:
                                    w2.append(S["ctxps"])
                            E("tensor", lambda t, hh=h, nn=n0, uu=u,
                              pn=pano:
                              t.matmul(ps_ctx[:, nn:nn + 512],
                                       sc_c[:, uu, hh * 128:(hh + 1) * 128],
                                       pn[:, hh * D + nn:hh * D + nn + 512],
                                       start=(hh == 0), stop=False),
                              waits=w2, inc=None)
                        y_s = E("tensor", lambda t, nn=n0: t.matmul(
                            ps_ctx[:, nn:nn + 512], ones_row[:, :],
                            borow[:, nn:nn + 512], start=False, stop=True),
                            inc=("tT" if n0 == 512 else None))
                    res_snap[yi] = E("vector", lambda v, a=x_i: v.tensor_add(
                        a, a, ps_ctx[:, :]),
                        waits=[("tT", y_s)], inc="tV")
                if 2 <= k_it <= NT + 1:
                    c = k_it - 2
                    u = c & 1
                    at_s = None
                    for h in range(NH):
                        w2 = []
                        if h == 0:
                            w2 = [("tV", sqT_snap[c]),
                                  ("tV", atT_snap.get(c - 1, 0))]
                            if c == 0:
                                w2.append(("tV", ctxsb_s))
                        at_s = E("tensor", lambda t, hh=h, uu=u: t.matmul(
                            ps_pB[:, hh * 128:(hh + 1) * 128],
                            ctx_sb[:, hh * 128:(hh + 1) * 128],
                            sc_a[:, uu, hh * 128:(hh + 1) * 128],
                            start=True, stop=True),
                            waits=w2, inc=("tT" if h == NH - 1 else None))
                    atT_snap[c] = E("vector", lambda v, uu=u: v.tensor_copy(
                        sc_c[:, uu, :], ps_pB[:, :]),
                        waits=[("tT", at_s)], inc="tV")
            free_panel(i_wq)
            free_panel(i_wo)
            S["pA"] = ("tS", eq_snap[NT - 1])
            S["pB"] = ("tV", atT_snap[NT - 1])
            S["ctxps"] = ("tV", res_snap[NT - 1])

            for half in range(2):
                tbase = half * 8
                x2T_ready = {}

                def ln2_stage(gl, guard):
                    ev = None
                    for sub in range(2):
                        i = tbase + 2 * gl + sub
                        u = i & 1
                        x_i = x_sb[:, i, :]
                        sq_s, rstd_s = ln_stats(x_i, u, i)
                        xn_s = ln_apply(x_i, u, i, rep[:, 2, :], rep[:, 3, :],
                                        sq_s, rstd_s)
                        tt = 2 * gl + sub
                        dst = bass.AP(x2T, tt * 128,
                                      [[8192, 128], [1024, NJ], [1, 128]])
                        ev = transposes(
                            lambda j, uu=u: sc_xn[:, uu,
                                                  j * 128:(j + 1) * 128],
                            ("tV", xn_s), dst,
                            extra=(guard if sub == 0 else ()))
                    x2T_ready[gl] = ev

                ln2_stage(0, [("tT", C["tT"]), ("tS", C["tS"])])
                for q in range(NQ4):
                    i_w1, w = alloc_panel()
                    wload2(i_w1, w1_d, l * D * FF + q * 1024, FF, 1024, w)
                    i_w2, w = alloc_panel()
                    wload2(i_w2, w2_d, l * FF * D + q * 1024 * D, D, 1024, w)
                    pw1, pw2 = pans[i_w1], pans[i_w2]
                    seq = [(gl, c) for gl in range(4) for c in range(NC8)]
                    lk_snap = {}
                    y2_snap = {}
                    ev_snap = {}

                    def emit_y2(idx2):
                        gl2, c2 = seq[idx2]
                        t0 = tbase + 2 * gl2
                        lastmm = None
                        for sub in range(2):
                            dst2 = ps_pA if sub == 0 else ps_pB
                            for n0 in (0, 512):
                                w3 = []
                                if sub == 0 and n0 == 0:
                                    w3 = [("tV", lk_snap[idx2]),
                                          ("dG", panel_loaded[i_w2])]
                                    if c2 == 0:
                                        w3 += [S["pA"], S["pB"]]
                                lastmm = E(
                                    "tensor",
                                    lambda t, ss=sub, nn=n0, cc=c2, d=dst2,
                                    ii=idx2, pn=pw2:
                                    t.matmul(
                                        d[:, nn:nn + 512],
                                        sc_c[:, ii & 1,
                                             ss * 128:ss * 128 + 128],
                                        pn[:, cc * 1024 + nn:
                                           cc * 1024 + nn + 512],
                                        start=(cc == 0), stop=(cc == NC8 - 1)),
                                    waits=w3,
                                    inc=("tT" if (sub == 1 and n0 == 512)
                                         else None))
                        y2_snap[idx2] = lastmm
                        if c2 == NC8 - 1:
                            ev = None
                            for sub in range(2):
                                src2 = ps_pA if sub == 0 else ps_pB
                                ev = E("vector",
                                       lambda v, ss=sub, s2=src2, t2=t0:
                                       v.tensor_add(x_sb[:, t2 + ss, :],
                                                    x_sb[:, t2 + ss, :],
                                                    s2[:, :]),
                                       waits=([("tT", lastmm)] if sub == 0
                                              else ()),
                                       inc=("tV" if sub == 1 else None))
                            ev_snap[gl2] = ev
                            S["pA"] = ("tV", ev)
                            S["pB"] = ("tV", ev)

                    for idx, (gl, c) in enumerate(seq):
                        if q == 0 and c == 0 and gl < 3:
                            ln2_stage(gl + 1, [])
                        boff = (idx & 1) * 512
                        lastmm = None
                        for j in range(NJ):
                            w2 = []
                            if j == 0:
                                w2 = [("tV", x2T_ready[gl]),
                                      ("tV", lk_snap.get(idx - 2, 0)),
                                      ("dG", panel_loaded[i_w1])]
                                if idx == 0:
                                    w2.append(S["ctxps"])
                            lastmm = E(
                                "tensor",
                                lambda t, jj=j, bo2=boff, cc=c, g2=gl,
                                pn=pw1:
                                t.matmul(
                                    ps_ctx[:, bo2:bo2 + 256],
                                    pn[:, jj * 1024 + cc * 128:
                                       jj * 1024 + cc * 128 + 128],
                                    bass.AP(x2T, jj * 1024 + g2 * 256,
                                            [[8192, 128], [1, 256]]),
                                    start=(jj == 0), stop=(jj == NJ - 1)),
                                waits=w2,
                                inc=("tT" if j == NJ - 1 else None))
                        E("vector", lambda v, bo2=boff: v.tensor_scalar(
                            ldump[:, :], ps_ctx[:, bo2:bo2 + 256], 0.01,
                            None, ALU.mult),
                            waits=[("tT", lastmm)], inc=None)
                        lk_snap[idx] = E("vector",
                                         lambda v, bo2=boff, ii=idx:
                                         v.tensor_max(
                                             sc_c[:, ii & 1, 0:256],
                                             ldump[:, :],
                                             ps_ctx[:, bo2:bo2 + 256]),
                                         inc="tV")
                        if idx >= 1:
                            emit_y2(idx - 1)
                    emit_y2(len(seq) - 1)
                    S["ctxps"] = ("tV", lk_snap[len(seq) - 1])
                    free_panel(i_w1)
                    free_panel(i_w2)
            S["rep"] = S["fdump"] if S["fdump"][0] == "tV" else ("tV", C["tV"])

        uphm = [("tT", C["tT"])]
        i_sp, sp_w = alloc_panel()
        pan_sp = pans[i_sp]
        mu_off = 4096
        so_off = 6144

        bc_load(rep[:, 0, :], fgamma, 0, D, [S["rep"]])
        bc_load(rep[:, 1, :], fbeta, 0, D, ())
        mod_rep_dg = C["dG"]
        mu_mm = 0
        mu_cp = 0
        last_mm = None
        for j in range(NJ):
            dl = E("sync", lambda s, jj=j: s.dma_start(
                mu_tmp[:, :], fmu[jj * 128:(jj + 1) * 128, :]),
                waits=([("tT", mu_mm), ("tV", mu_cp)] if j > 0
                       else [S["fdump"], ("tS", C["tS"])]),
                inc="dS", n=16)
            sqv = E("vector", lambda v: v.tensor_mul(
                ldump[:, :], mu_tmp[:, 0:256], mu_tmp[:, 0:256]),
                waits=[("dS", dl), ("tT", mu_mm)], inc="tV")
            mu_cp = E("vector", lambda v, jj=j, sp=pan_sp: v.tensor_copy(
                sp[:, mu_off + jj * 256:mu_off + (jj + 1) * 256],
                mu_tmp[:, :]),
                waits=(sp_w if j == 0 else ()), inc="tV")
            last_mm = E("tensor", lambda t, jj=j: t.matmul(
                ps_ms[0:1, 0:256], ones_f32c[:, :], ldump[:, :],
                start=(jj == 0), stop=(jj == NJ - 1)),
                waits=[("tV", sqv)] + ([S["ms"]] if j == 0 else []),
                inc="tT")
            mu_mm = last_mm
        n1 = E("scalar", lambda s: s.activation(
            ldump[0:1, 0:256], ps_ms[0:1, 0:256], AF.Sqrt),
            waits=[("tT", last_mm)], inc="tS")
        S["ms"] = ("tS", n1)
        n2 = E("vector", lambda v: v.tensor_scalar(
            ldump[0:1, 0:256], ldump[0:1, 0:256], 1e-12, None, ALU.max),
            waits=[("tS", n1)], inc="tV")
        n3 = E("vector", lambda v: v.reciprocal(ldump[0:1, 0:256], ldump[0:1, 0:256]),
               waits=[("tV", n2)], inc="tV")
        n4 = E("vector", lambda v: v.tensor_scalar(
            ldump[0:1, 0:256], ldump[0:1, 0:256], fsc_rep[0:1, 0:1], None, ALU.mult),
            waits=[("tV", n3)], inc="tV")
        nrep = E("tensor", lambda t: t.matmul(
            ps_ctx[:, 0:256], ones_f32r[:, :], ldump[0:1, 0:256],
            start=True, stop=True),
            waits=[("tV", n4), S["ctxps"], ("tG", ginit)], inc="tT")
        mu_ready = None
        for j in range(NJ):
            mu_ready = E("vector", lambda v, jj=j, sp=pan_sp: v.tensor_mul(
                sp[:, mu_off + jj * 256:mu_off + (jj + 1) * 256],
                sp[:, mu_off + jj * 256:mu_off + (jj + 1) * 256],
                ps_ctx[:, 0:256]),
                waits=([("tT", nrep), ("tV", mu_cp)] if j == 0 else ()),
                inc=("tV" if j == NJ - 1 else None))
        S["ctxps"] = ("tV", mu_ready)
        gid = E("gpsimd", lambda g: (
            g.memset(ldump[:, 0:128], 0.0),
            g.affine_select(out=ldump[:, 0:128], in_=ldump[:, 0:128],
                            compare_op=ALU.not_equal, fill=1.0, base=0,
                            pattern=[[-1, 128]], channel_multiplier=1))[-1],
            waits=[("tV", mu_cp), ("tV", C["tV"])], inc="tG")
        d1 = E("sync", lambda s: s.dma_start(
            fdump[0:16, 0:256], fb1[:, :]),
            waits=[S["fdump"]], inc="dS", n=16)
        S["fdump"] = ("dS", d1)
        lastv = None
        for hc in range(2):
            w2 = [("dS", d1), ("tG", gid), S["pA"]] if hc == 0 else [
                ("tV", lastv)]
            t1 = E("tensor", lambda t, h2=hc: t.transpose(
                ps_pA[:, h2 * 128:h2 * 128 + 16],
                fdump[0:16, h2 * 128:h2 * 128 + 128],
                ldump[0:16, 0:16]),
                waits=w2, inc="tT")
            lastv = E("vector", lambda v, h2=hc: v.tensor_copy(
                b1T[:, h2, :], ps_pA[:, h2 * 128:h2 * 128 + 16]),
                waits=[("tT", t1)], inc="tV")
        d2 = E("sync", lambda s: s.dma_start(fdump[0:16, :], fb2[:, :]),
               waits=[("tV", lastv)], inc="dS", n=16)
        S["fdump"] = ("dS", d2)
        for dc in range(NJ):
            w2 = [("dS", d2), ("tV", lastv)] if dc == 0 else [("tV", lastv)]
            t1 = E("tensor", lambda t, d3=dc: t.transpose(
                ps_pA[:, d3 * 128:d3 * 128 + 16],
                fdump[0:16, d3 * 128:d3 * 128 + 128],
                ldump[0:16, 0:16]),
                waits=w2, inc="tT")
            lastv = E("vector", lambda v, d3=dc: v.tensor_copy(
                b2T[:, d3, :], ps_pA[:, d3 * 128:d3 * 128 + 16]),
                waits=[("tT", t1)], inc="tV")
        S["pA"] = ("tV", lastv)
        S["fdump"] = ("tV", lastv)

        quartets = {}

        def load_quartet(tq):
            iq1, w = alloc_panel()
            for e2 in range(4):
                for j2 in range(NJ):
                    src = bass.AP(fw1, (4 * tq + e2) * D * 256 + j2 * 128 * 256,
                                  [[256, 128], [1, 256]])
                    E("gpsimd",
                      lambda g, d=pans[iq1][:, e2 * 2048 + j2 * 256:
                                            e2 * 2048 + (j2 + 1) * 256],
                      s2=src: g.dma_start(d, s2),
                      waits=(w if (e2 == 0 and j2 == 0) else ()),
                      inc="dG", n=16)
            panel_loaded[iq1] = C["dG"]
            iq2, w = alloc_panel()
            for e2 in range(4):
                for h2 in range(2):
                    src = bass.AP(fw2, (4 * tq + e2) * 256 * D + h2 * 128 * D,
                                  [[D, 128], [1, D]])
                    E("gpsimd",
                      lambda g, d=pans[iq2][:, e2 * 2048 + h2 * 1024:
                                            e2 * 2048 + (h2 + 1) * 1024],
                      s2=src: g.dma_start(d, s2),
                      waits=(w if (e2 == 0 and h2 == 0) else ()),
                      inc="dG", n=16)
            panel_loaded[iq2] = C["dG"]
            quartets[tq] = (iq1, iq2)

        load_quartet(0)
        load_quartet(1)

        xio_free = {0: ("tV", 0), 1: ("tV", 0)}
        logit_free = ("tS", 0)
        slot_mm_last = None
        for i in range(NT):
            u = i & 1
            x_i = x_sb[:, i, :]
            src = bass.AP(x_in, b * 2048 * D + i * 128 * D,
                          [[D, 128], [1, D]])
            dl = E("sync", lambda s, uu=u, sr=src: s.dma_start(
                xio[:, uu, :], sr),
                waits=[xio_free[u]] + (uphm if i <= 1 else []),
                inc="dS", n=16)
            add_s = E("vector", lambda v, a=x_i, uu=u: v.tensor_add(
                a, a, xio[:, uu, :]),
                waits=[("dS", dl)], inc="tV")
            xio_free[u] = ("tV", add_s)
            sq_s, rstd_s = ln_stats(x_i, u, i, xwait=[("tV", add_s)])
            x1_s = E("vector", lambda v, a=x_i, ii=i: v.tensor_scalar(
                a, a, nm_all[:, ii:ii + 1], rs_all[:, ii:ii + 1],
                ALU.add, ALU.mult),
                waits=[("tV", rstd_s), S["fdump"]], inc="tV")
            sq2, rstd2 = ln_stats(x_i, u, i, xwait=[("tV", x1_s)])
            xn_s = ln_apply(x_i, u, i, rep[:, 0, :], rep[:, 1, :],
                            sq2, rstd2, extra=[("dG", mod_rep_dg)])
            s3 = E("scalar", lambda s, uu=u: s.activation(
                fdump[:, :], sc_xn[:, uu, :], AF.Square,
                accum_out=st[:, uu * 8 + 6:uu * 8 + 7]),
                waits=[("tV", xn_s)], inc="tS")
            S["fdump"] = ("tS", s3)
            s4 = E("scalar", lambda s, uu=u: s.activation(
                st[:, uu * 8 + 7:uu * 8 + 8], st[:, uu * 8 + 6:uu * 8 + 7],
                AF.Sqrt),
                waits=[("tS", s3)], inc="tS")
            v8 = E("vector", lambda v, uu=u: v.tensor_scalar(
                st[:, uu * 8 + 6:uu * 8 + 7], st[:, uu * 8 + 7:uu * 8 + 8],
                1e-12, None, ALU.max),
                waits=[("tS", s4)], inc="tV")
            v9 = E("vector", lambda v, uu=u: v.reciprocal(
                st[:, uu * 8 + 7:uu * 8 + 8], st[:, uu * 8 + 6:uu * 8 + 7]),
                waits=[("tV", v8)], inc="tV")
            rn_s = E("vector", lambda v, uu=u: v.tensor_scalar(
                st[:, uu * 8 + 6:uu * 8 + 7], st[:, uu * 8 + 7:uu * 8 + 8],
                fsc_rep[:, 0:1], None, ALU.mult),
                waits=[("tV", v9)], inc="tV")
            if b == 0 and i == 0:
                dump("rn0", st[:, 0:16], [128, 16], F32)
                dump("mus", pan_sp[:, 4096:6144], [128, 2048], BF16, cast=True)
                dump("xnm0", sc_xn[:, 0, :], [128, D], BF16, cast=True)
            xnT_s = transposes(
                lambda j, uu=u: sc_xn[:, uu, j * 128:(j + 1) * 128],
                ("tV", xn_s), sc_xnT[:, u, :],
                extra=(uphm if i <= 1 else ()))
            lg_s = None
            for j in range(NJ):
                w2 = []
                if j == 0:
                    w2 = [("tV", xnT_s), ("tV", mu_ready), logit_free,
                          S["ctxps"] if i == 0 else ("tV", 0)]
                lg_s = E("tensor", lambda t, jj=j, uu=u, sp=pan_sp: t.matmul(
                    ps_ctx[:, 0:256],
                    sc_xnT[:, uu, jj * 128:(jj + 1) * 128],
                    sp[:, mu_off + jj * 256:mu_off + (jj + 1) * 256],
                    start=(jj == 0), stop=(jj == NJ - 1)),
                    waits=w2, inc=("tT" if j == NJ - 1 else None))
            eE = E("scalar", lambda s, ii=i, uu=u, sp=pan_sp: s.activation(
                sp[:, ii * 256:(ii + 1) * 256], ps_ctx[:, 0:256],
                AF.Exp, scale=st[:, uu * 8 + 6:uu * 8 + 7]),
                waits=[("tT", lg_s), ("tV", rn_s)], inc="tS")
            logit_free = ("tS", eE)
            for esc in range(2):
                w2 = [("tS", eE)]
                if i == 0 and esc == 0:
                    w2 += [S["ms"], S["pA"], S["pB"]]
                E("tensor", lambda t, e2=esc, ii=i, sp=pan_sp: t.matmul(
                    ps_ms[:, e2:e2 + 1],
                    sp[:, ii * 256 + e2 * 128:ii * 256 + e2 * 128 + 128],
                    ones_bf[:, :],
                    start=(ii == 0), stop=(ii == NT - 1)),
                    waits=w2, inc=None)
                dst = ps_pA if esc == 0 else ps_pB
                for n0 in (0, 512):
                    slot_mm_last = E(
                        "tensor",
                        lambda t, e2=esc, nn=n0, ii=i, uu=u, d=dst,
                        sp=pan_sp:
                        t.matmul(
                            d[:, nn:nn + 512],
                            sp[:, ii * 256 + e2 * 128:
                               ii * 256 + e2 * 128 + 128],
                            sc_xn[:, uu, nn:nn + 512],
                            start=(ii == 0), stop=(ii == NT - 1)),
                        inc=("tT" if (esc == 1 and n0 == 512) else None))
        if b == 0:
            dump("x1", x_sb[:, :, :], [128, NT * D], F32)
            dump("E", pan_sp[:, 0:4096], [128, 4096], BF16, cast=True)
        vz = E("vector", lambda v: v.reciprocal(zrcp[:, :], ps_ms[:, 0:2]),
               waits=[("tT", slot_mm_last)], inc="tV")
        S["ms"] = ("tV", vz)
        si_s = None
        for esc in range(2):
            src = ps_pA if esc == 0 else ps_pB
            si_s = E("vector", lambda v, e2=esc, s2=src: v.tensor_scalar(
                sc_a[:, e2, :], s2[:, :], zrcp[:, e2:e2 + 1], None, ALU.mult),
                waits=([("tV", vz)] if esc == 0 else ()),
                inc=("tV" if esc == 1 else None))
        S["pA"] = ("tV", si_s)
        S["pB"] = ("tV", si_s)
        if b == 0:
            dump("zrcp", zrcp[:, :], [128, 2], F32)
            dump("slotin", sc_a[:, :, :], [128, 2 * D], BF16, cast=True)
        siT_s = None
        for esc in range(2):
            dstv = bass.AP(sc_b, esc * 128, [[2048, 128], [256, NJ], [1, 128]])
            siT_s = transposes(
                lambda j, e2=esc: sc_a[:, e2, j * 128:(j + 1) * 128],
                ("tV", si_s), dstv)
        so_free = ("tV", 0)
        gprev = ("tS", 0)
        so_evac = None
        for e in range(16):
            tq, eo = e // 4, e % 4
            iq1, iq2 = quartets[tq]
            pq1, pq2 = pans[iq1], pans[iq2]
            gel = {}
            for hc in range(2):
                last = None
                for j in range(NJ):
                    w2 = []
                    if j == 0 and hc == 0:
                        w2 = [("tV", siT_s), ("dG", panel_loaded[iq1]), gprev]
                    elif j == 0 and hc == 1:
                        w2 = [("tS", gel[0])]
                    last = E("tensor",
                             lambda t, jj=j, h2=hc, e2=eo, p1=pq1, ee=e:
                             t.matmul(
                                 ps_ms[:, 32 + h2 * 16:32 + h2 * 16 + 16],
                                 p1[:, e2 * 2048 + jj * 256 + h2 * 128:
                                    e2 * 2048 + jj * 256 + h2 * 128 + 128],
                                 bass.AP(sc_b, jj * 256 + ee * 16,
                                         [[2048, 128], [1, 16]]),
                                 start=(jj == 0), stop=(jj == NJ - 1)),
                             waits=w2, inc=("tT" if j == NJ - 1 else None))
                gel[hc] = E("scalar", lambda s, h2=hc, e2=e: s.activation(
                    hT[:, h2, :], ps_ms[:, 32 + h2 * 16:32 + h2 * 16 + 16],
                    AF.Gelu, bias=b1T[:, h2, e2:e2 + 1]),
                    waits=[("tT", last)], inc="tS")
            gprev = ("tS", gel[1])
            if b == 0 and e == 0 and "hT0" in dbg:
                dw = dump("hT0", hT[:, :, :], [128, 32], BF16, cast=True)
                if dw:
                    gprev = dw
            if b == 0 and e == 0 and "siT" in dbg:
                dump("siT", sc_b[:, :, :], [128, 2048], BF16, cast=True)
            if b == 0 and e == 0 and "fw2p" in dbg:
                dump("fw2p", pans[iq2][:, 0:2048], [128, 2048], BF16, cast=True)
                dump("b2Td", b2T[:, :, :], [128, 128], F32)
            last = None
            for dc in range(NJ):
                for hc in range(2):
                    w2 = []
                    if dc == 0 and hc == 0:
                        w2 = [("tS", gel[0]), ("tS", gel[1]),
                              ("dG", panel_loaded[iq2]), so_free]
                    last = E("tensor",
                             lambda t, d3=dc, h2=hc, e2=eo, p2=pq2:
                             t.matmul(
                                 ps_ctx[:, d3 * 16:d3 * 16 + 16],
                                 p2[:, e2 * 2048 + h2 * 1024 + d3 * 128:
                                    e2 * 2048 + h2 * 1024 + d3 * 128 + 128],
                                 hT[:, h2, :],
                                 start=(h2 == 0), stop=(h2 == 1)),
                             waits=w2, inc=("tT" if (dc == NJ - 1 and hc == 1)
                                            else None))
            for dc in range(NJ):
                so_evac = E("vector", lambda v, d3=dc, e2=e, sp=pan_sp:
                    v.tensor_scalar(
                    bass.AP(sp, so_off + d3 * 256 + e2 * 16,
                            [[8192, 128], [1, 16]]),
                    ps_ctx[:, d3 * 16:d3 * 16 + 16],
                    b2T[:, d3, e2:e2 + 1], None, ALU.add),
                    waits=([("tT", last)] if dc == 0 else ()),
                    inc=("tV" if dc == NJ - 1 else None))
            so_free = ("tV", so_evac)
            if eo == 3:
                free_panel(iq1)
                free_panel(iq2)
                if tq + 2 < 4:
                    load_quartet(tq + 2)
        S["ctxps"] = ("tV", so_evac)
        if b == 0:
            dump("soT", pan_sp[:, 6144:8192], [128, 2048], BF16, cast=True)
        so_s = None
        for esc in range(2):
            dstv = bass.AP(sc_b, esc * 1024, [[2048, 128], [1, 1024]])
            so_s = transposes(
                lambda dc, e2=esc, sp=pan_sp: sp[
                    :, so_off + dc * 256 + e2 * 128:
                    so_off + dc * 256 + e2 * 128 + 128],
                ("tV", so_evac), dstv)
        if b == 0:
            dump("so", sc_b[:, :, :], [128, 2 * D], BF16, cast=True)
        st_dma = {0: 0, 1: 0}
        for i in range(NT):
            u = i & 1
            x_i = x_sb[:, i, :]
            rsum = E("vector", lambda v, ii=i, uu=u, sp=pan_sp: v.reduce_sum(
                st[:, uu * 8:uu * 8 + 1],
                sp[:, ii * 256:(ii + 1) * 256], axis=AX.X),
                waits=[("tV", so_s)], inc="tV")
            rrec = E("vector", lambda v, uu=u: v.reciprocal(
                st[:, uu * 8 + 1:uu * 8 + 2], st[:, uu * 8:uu * 8 + 1]),
                waits=[("tV", rsum)], inc="tV")
            eT_s = transposes(
                lambda e2, ii=i, sp=pan_sp: sp[:, ii * 256 + e2 * 128:
                                               ii * 256 + e2 * 128 + 128],
                ("tV", 0), sc_a[:, u, 0:256], nblk=2)
            last = None
            for esc in range(2):
                for n0 in (0, 512):
                    w2 = []
                    if esc == 0 and n0 == 0:
                        w2 = [("tV", eT_s), S["pA"]]
                    last = E("tensor",
                             lambda t, e2=esc, nn=n0, uu=u: t.matmul(
                                 ps_pA[:, nn:nn + 512],
                                 sc_a[:, uu, e2 * 128:e2 * 128 + 128],
                                 bass.AP(sc_b, e2 * 1024 + nn,
                                         [[2048, 128], [1, 512]]),
                                 start=(e2 == 0), stop=(e2 == 1)),
                             waits=w2, inc=("tT" if (esc == 1 and n0 == 512)
                                            else None))
            x2 = E("vector", lambda v, a=x_i, uu=u: v.scalar_tensor_tensor(
                a, ps_pA[:, :], st[:, uu * 8 + 1:uu * 8 + 2], a,
                ALU.mult, ALU.add),
                waits=[("tT", last), ("tV", rrec)], inc="tV")
            S["pA"] = ("tV", x2)
            sq_s, rstd_s = ln_stats(x_i, u, i, xwait=[("tV", x2)])
            ov = E("vector", lambda v, a=x_i, ii=i, uu=u: v.tensor_scalar(
                xio[:, uu, :], a, nm_all[:, ii:ii + 1], rs_all[:, ii:ii + 1],
                ALU.add, ALU.mult),
                waits=[("tV", rstd_s), S["fdump"], ("dS", st_dma[u]),
                       xio_free[u]], inc="tV")
            xio_free[u] = ("tV", ov)
            dst = bass.AP(y_out, b * 2048 * D + i * 128 * D,
                          [[D, 128], [1, D]])
            st_dma[u] = E("sync", lambda s, uu=u, d2=dst: s.dma_start(
                d2, xio[:, uu, :]),
                waits=[("tV", ov)], inc="dS", n=16)
        if b == 0:
            dump("xend", x_sb[:, :, :], [128, NT * D], F32)
            dump("Eb0", pan_sp[:, 0:4096], [128, 4096], BF16, cast=True)
            dump("sob", sc_b[:, :, :], [128, 2048], BF16, cast=True)
        free_panel(i_sp)
        S["x_free"] = ("tV", C["tV"])
        S["rep"] = ("tV", C["tV"])

    E("sync", None, waits=[("dS", C["dS"])])

    semh = {s: es.enter_context(nc.semaphore(s)) for s in sems}
    block = es.enter_context(nc.Block())

    def mk_replay(name):
        def fn(engobj):
            waited = {}
            for f, ws, inc, n in p.ev[name]:
                for s2, v in ws:
                    if s2 == "zz_taken":
                        raise RuntimeError("panel used while reserved")
                    if waited.get(s2, 0) < v:
                        engobj.wait_ge(semh[s2], v)
                        waited[s2] = v
                ins = f(engobj) if f is not None else None
                if inc is not None:
                    if ins is not None:
                        ins.then_inc(semh[inc], n)
                    else:
                        engobj.sem_inc(semh[inc], n)
        return fn

    block.sync(mk_replay("sync"))
    block.tensor(mk_replay("tensor"))
    block.vector(mk_replay("vector"))
    block.scalar(mk_replay("scalar"))
    block.gpsimd(mk_replay("gpsimd"))

    es.close()
    return nc


def kernel(**inputs):
    f = lambda k: np.ascontiguousarray(np.asarray(inputs[k], dtype=np.float32))
    x = f("L0_attn_x")
    shared = {
        "ln1_w": f("L0_attn_ln1_w"), "ln1_b": f("L0_attn_ln1_b"),
        "ln2_w": f("L0_attn_ln2_w"), "ln2_b": f("L0_attn_ln2_b"),
        "bo_d": f("L0_attn_bo"),
        "wq_d": f("L0_attn_wq"), "wk_d": f("L0_attn_wk"),
        "wv_d": f("L0_attn_wv"), "wo_d": f("L0_attn_wo"),
        "w1_d": f("L0_attn_w1"), "w2_d": f("L0_attn_w2"),
        "fgamma": f("L0_ffn_ln_gamma"), "fbeta": f("L0_ffn_ln_beta"),
        "fmu": f("L0_ffn_mu").reshape(1024, 256),
        "fscale": f("L0_ffn_scale"),
        "fw1": f("L0_ffn_w1"), "fb1": f("L0_ffn_b1"),
        "fw2": f("L0_ffn_w2"), "fb2": f("L0_ffn_b2"),
    }
    nc = build_nc()
    in_maps = []
    for c in range(8):
        m = {"x_in": x[2 * c:2 * c + 2]}
        m.update(shared)
        in_maps.append(m)
    res = run_bass_kernel_spmd(nc, in_maps, core_ids=list(range(8)))
    out = np.concatenate([np.asarray(r["y_out"]) for r in res.results], axis=0)
    return out.astype(np.float32)


# version 03
# speedup vs baseline: 1.6493x; 1.1791x over previous
import numpy as np
from contextlib import ExitStack

import concourse.bass as bass
import concourse.mybir as mybir
from concourse.bass_utils import run_bass_kernel_spmd

F32 = mybir.dt.float32
BF16 = mybir.dt.bfloat16
AF = mybir.ActivationFunctionType
ALU = mybir.AluOpType
AX = mybir.AxisListType

ENGINES = ("sync", "tensor", "vector", "scalar", "gpsimd")

D = 1024
NT = 16
NJ = 8
NH = 8
FF = 4096
NQ4 = 4
NC8 = 8


class Prog:
    def __init__(self, semnames):
        self.ev = {e: [] for e in ENGINES}
        self.cnt = {s: 0 for s in semnames}

    def emit(self, eng, fn=None, waits=(), inc=None, n=1):
        ws = tuple((s, v) for (s, v) in waits if v and v > 0)
        self.ev[eng].append((fn, ws, inc, n))
        if inc is not None:
            self.cnt[inc] += n
            return self.cnt[inc]
        return None


def build_nc(NB=2, NL=6, dbg=(), only_b0=False):
    nc = bass.Bass("TRN2", target_bir_lowering=False)

    x_in = nc.dram_tensor("x_in", [NB, 2048, D], F32, kind="ExternalInput")
    ln1_w = nc.dram_tensor("ln1_w", [6, D], F32, kind="ExternalInput")
    ln1_b = nc.dram_tensor("ln1_b", [6, D], F32, kind="ExternalInput")
    ln2_w = nc.dram_tensor("ln2_w", [6, D], F32, kind="ExternalInput")
    ln2_b = nc.dram_tensor("ln2_b", [6, D], F32, kind="ExternalInput")
    bo_d = nc.dram_tensor("bo_d", [6, D], F32, kind="ExternalInput")
    wq_d = nc.dram_tensor("wq_d", [6, D, D], F32, kind="ExternalInput")
    wk_d = nc.dram_tensor("wk_d", [6, D, D], F32, kind="ExternalInput")
    wv_d = nc.dram_tensor("wv_d", [6, D, D], F32, kind="ExternalInput")
    wo_d = nc.dram_tensor("wo_d", [6, D, D], F32, kind="ExternalInput")
    w1_d = nc.dram_tensor("w1_d", [6, D, FF], F32, kind="ExternalInput")
    w2_d = nc.dram_tensor("w2_d", [6, FF, D], F32, kind="ExternalInput")
    fgamma = nc.dram_tensor("fgamma", [D], F32, kind="ExternalInput")
    fbeta = nc.dram_tensor("fbeta", [D], F32, kind="ExternalInput")
    fmu = nc.dram_tensor("fmu", [D, 256], F32, kind="ExternalInput")
    fscale = nc.dram_tensor("fscale", [1], F32, kind="ExternalInput")
    fw1 = nc.dram_tensor("fw1", [16, D, 256], F32, kind="ExternalInput")
    fb1 = nc.dram_tensor("fb1", [16, 256], F32, kind="ExternalInput")
    fw2 = nc.dram_tensor("fw2", [16, 256, D], F32, kind="ExternalInput")
    fb2 = nc.dram_tensor("fb2", [16, D], F32, kind="ExternalInput")
    y_out = nc.dram_tensor("y_out", [NB, 2048, D], F32, kind="ExternalOutput")

    es = ExitStack()
    sb = lambda n, s, d: es.enter_context(nc.sbuf_tensor(n, s, d))
    psa = lambda n, s, d: es.enter_context(nc.psum_tensor(n, s, d))

    x_sb = sb("x_sb", [128, NT, D], F32)
    pans = [sb(f"pan{i}", [128, 8192], BF16) for i in range(6)]
    u_arena = sb("u_arena", [128, 16384], mybir.dt.uint8)
    ua_addr = nc.lookup_mloc(u_arena).addr
    sc_xnT = nc.alloc_sbuf_tensor_at("sc_xnT", [128, 2, D], BF16,
                                     offset=ua_addr)
    sc_a = nc.alloc_sbuf_tensor_at("sc_a", [128, 2, D], BF16,
                                   offset=ua_addr + 4096)
    xio = nc.alloc_sbuf_tensor_at("xio", [128, 2, D], F32,
                                  offset=ua_addr + 8192)
    x2T = nc.alloc_sbuf_tensor_at("x2T", [128, NJ, 1024], BF16,
                                  offset=ua_addr)
    rep = sb("rep", [128, 4, D], BF16)
    borow = sb("borow", [1, D], BF16)
    ident = sb("ident", [128, 128], BF16)
    ones_bf = sb("ones_bf", [128, 1], BF16)
    ones_row = sb("ones_row", [1, 128], BF16)
    ones_f32c = sb("ones_f32c", [128, 1], F32)
    ones_f32r = sb("ones_f32r", [1, 128], F32)
    ctx_sb = sb("ctx_sb", [128, D], BF16)
    sc_xn = sb("sc_xn", [128, 2, D], BF16)
    sc_b = sb("sc_b", [128, 2, D], BF16)
    sc_c = sb("sc_c", [128, 2, D], BF16)
    fdump = sb("fdump", [128, D], F32)
    ldump = sb("ldump", [128, 256], F32)
    st = sb("st", [128, 16], F32)
    nm_all = sb("nm_all", [128, NT], F32)
    rs_all = sb("rs_all", [128, NT], F32)
    qrs = sb("qrs", [128, 16], F32)
    qrcp = sb("qrcp", [128, 16], F32)
    dnr = sb("dnr", [128, 8], F32)
    fsc_rep = sb("fsc_rep", [128, 1], F32)
    b1T = sb("b1T", [128, 2, 16], F32)
    b2T = sb("b2T", [128, 8, 16], F32)
    hT = sb("hT", [128, 2, 16], BF16)
    zrcp = sb("zrcp", [128, 2], F32)
    eps_sb = sb("eps_sb", [128, 1], F32)
    mu_tmp = nc.alloc_sbuf_tensor_at(
        "mu_tmp", [128, 256], F32,
        offset=nc.lookup_mloc(fdump).addr + 1024)

    ps_pA = psa("ps_pA", [128, 1024], F32)
    ps_pB = psa("ps_pB", [128, 1024], F32)
    ps_ctx = psa("ps_ctx", [128, 1024], F32)
    ps_tr = psa("ps_tr", [128, 1024], BF16)
    ps_ms = psa("ps_ms", [128, 512], F32)

    sems = ("dG", "dS", "tT", "tV", "tS", "tG")
    p = Prog(sems)
    E = p.emit
    C = p.cnt

    S = {
        "fdump": ("tV", 0),
        "pA": ("tV", 0),
        "pB": ("tV", 0),
        "ctxps": ("tV", 0),
        "ms": ("tV", 0),
        "tr": ("tV", 0),
        "rep": ("tV", 0),
        "x_free": ("tV", 0),
    }

    dbg_outs = {}

    def dump(tag, src_ap, shape, dt, cast=False):
        if tag not in dbg:
            return
        h = nc.dram_tensor("dbg_" + tag, shape, F32 if cast else dt,
                           kind="ExternalOutput")
        eng = "gpsimd" if cast else "sync"
        sem = "dG" if eng == "gpsimd" else "dS"
        snapd = E(eng, lambda g, d=h, s2=src_ap: g.dma_start(
            bass.AP(d, 0, [[int(np.prod(shape[1:])), shape[0]]] +
                    [[1, int(np.prod(shape[1:]))]]), s2),
          waits=[("tT", C["tT"]), ("tV", C["tV"]), ("tS", C["tS"]),
                 ("tG", C["tG"])],
          inc=sem, n=16)
        dbg_outs[tag] = True
        return (sem, snapd)

    def init_fn(g):
        g.memset(ident[:, :], 0.0)
        g.affine_select(out=ident[:, :], in_=ident[:, :],
                        compare_op=ALU.not_equal, fill=1.0, base=0,
                        pattern=[[-1, 128]], channel_multiplier=1)
        g.memset(ones_bf[:, :], 1.0)
        g.memset(ones_row[:, :], 1.0)
        g.memset(ones_f32c[:, :], 1.0)
        g.memset(eps_sb[:, :], 1e-5)
        return g.memset(ones_f32r[:, :], 1.0)
    ginit = E("gpsimd", init_fn, inc="tG")
    E("gpsimd", lambda g: g.dma_start(
        fsc_rep[:, :], bass.AP(fscale, 0, [[0, 128], [1, 1]])),
        inc="dG", n=16)

    panel_free = [[] for _ in range(6)]
    panel_loaded = [0] * 6

    def alloc_panel():
        def key(i):
            f = panel_free[i]
            return max((v for (_, v) in f), default=0)
        idx = min(range(6), key=key)
        w = list(panel_free[idx])
        panel_free[idx] = [("zz_taken", 10 ** 9)]
        return idx, w

    def free_panel(idx):
        panel_free[idx] = [("tT", C["tT"]), ("tV", C["tV"]), ("tS", C["tS"])]

    def wload2(idx, dram_h, base_off, row_stride, cols, w):
        pan = pans[idx]
        for j in range(NJ):
            src = bass.AP(dram_h, base_off + j * 128 * row_stride,
                          [[row_stride, 128], [1, cols]])
            E("gpsimd",
              lambda g, d=pan[:, j * cols:(j + 1) * cols], s2=src:
              g.dma_start(d, s2),
              waits=(w if j == 0 else ()), inc="dG", n=16)
        panel_loaded[idx] = C["dG"]

    def bc_load(dst_ap, src_handle, src_off, ncols, waits):
        return E("gpsimd",
                 lambda g, d=dst_ap, h=src_handle, o=src_off, n2=ncols:
                 g.dma_start(d, bass.AP(h, o, [[0, 128], [1, n2]])),
                 waits=waits, inc="dG", n=16)

    def ln_stats(tile_ap, u, col, xwait=()):
        b0 = u * 8
        ns, sq, vr, n2v, v2, sd = b0, b0 + 1, b0 + 2, b0 + 3, b0 + 4, b0 + 5
        v1 = E("vector", lambda v, a=tile_ap: v.reduce_sum(
            st[:, ns:ns + 1], a, axis=AX.X, negate=True),
            waits=xwait, inc="tV")
        s1 = E("scalar", lambda s, a=tile_ap: s.activation(
            fdump[:, :], a, AF.Square, accum_out=st[:, sq:sq + 1]),
            waits=[S["fdump"]] + list(xwait), inc="tS")
        S["fdump"] = ("tS", s1)
        v2s = E("vector", lambda v, c2=col: v.tensor_scalar(
            nm_all[:, c2:c2 + 1], st[:, ns:ns + 1], 1.0 / D, None, ALU.mult),
            waits=[("tV", v1)], inc="tV")
        v3 = E("vector", lambda v: v.tensor_scalar(
            st[:, vr:vr + 1], st[:, sq:sq + 1], 1.0 / D, None, ALU.mult),
            waits=[("tS", s1)], inc="tV")
        v4 = E("vector", lambda v, c2=col: v.tensor_scalar(
            st[:, n2v:n2v + 1], nm_all[:, c2:c2 + 1],
            nm_all[:, c2:c2 + 1], None, ALU.mult),
            waits=[("tV", v2s)], inc="tV")
        v5 = E("vector", lambda v: v.tensor_sub(
            st[:, v2:v2 + 1], st[:, vr:vr + 1], st[:, n2v:n2v + 1]),
            waits=[("tV", v4)], inc="tV")
        s2 = E("scalar", lambda s: s.activation(
            st[:, sd:sd + 1], st[:, v2:v2 + 1], AF.Sqrt,
            bias=eps_sb[:, 0:1]),
            waits=[("tV", v5), ("tG", ginit)], inc="tS")
        v6 = E("vector", lambda v, c2=col: v.reciprocal(
            rs_all[:, c2:c2 + 1], st[:, sd:sd + 1]),
            waits=[("tS", s2)], inc="tV")
        return s1, v6

    def ln_apply(tile_ap, u, col, wrep_ap, brep_ap, sq_snap, rstd_snap,
                 extra=()):
        E("vector", lambda v, a=tile_ap, c2=col: v.tensor_scalar(
            fdump[:, :], a, nm_all[:, c2:c2 + 1], rs_all[:, c2:c2 + 1],
            ALU.add, ALU.mult),
            waits=[("tV", rstd_snap), S["fdump"]] + list(extra), inc="tV")
        E("vector", lambda v, w=wrep_ap: v.tensor_mul(
            fdump[:, :], fdump[:, :], w), inc=None)
        vc = E("vector", lambda v, b2=brep_ap, u2=u: v.tensor_add(
            sc_xn[:, u2, :], fdump[:, :], b2), inc="tV")
        S["fdump"] = ("tV", vc)
        return vc

    def transposes(src_view, src_wait, dst_ap, nblk=8, extra=()):
        last_t = None
        for j in range(nblk):
            w = []
            if j == 0:
                w = [src_wait, S["tr"], ("tG", ginit)] + list(extra)
            last_t = E("tensor",
                       lambda t, jj=j, sv=src_view: t.transpose(
                           ps_tr[:, jj * 128:(jj + 1) * 128],
                           sv(jj), ident[:, :]),
                       waits=w, inc=("tT" if j == nblk - 1 else None))
        ev = E("vector", lambda v, d=dst_ap, n2=nblk: v.tensor_copy(
            d, ps_tr[:, 0:n2 * 128]),
            waits=[("tT", last_t)], inc="tV")
        S["tr"] = ("tV", ev)
        return ev

    for b in range(1 if only_b0 else NB):
        if b > 0:
            bar = [("tT", C["tT"]), ("tV", C["tV"]), ("tS", C["tS"]),
                   ("tG", C["tG"]), ("dS", C["dS"]), ("dG", C["dG"])]
            for eng in ENGINES:
                E(eng, None, waits=bar)
        xl = None
        for ii in range(0, NT, 2):
            src = bass.AP(x_in, b * 2048 * D + ii * 128 * D,
                          [[D, 128], [128 * D, 2], [1, D]])
            xl = E("sync", lambda s, d=x_sb[:, ii:ii + 2, :], sr=src:
                   s.dma_start(d, sr),
                   waits=([S["x_free"]] if ii == 0 else ()),
                   inc="dS", n=16)
        xload = xl

        for l in range(NL):
            for ri, hsrc in enumerate((ln1_w, ln1_b, ln2_w, ln2_b)):
                bc_load(rep[:, ri, :], hsrc, l * D, D,
                        [S["rep"]] if ri == 0 else ())
            E("gpsimd", lambda g, o=l * D: g.dma_start(
                borow[:, :], bass.AP(bo_d, o, [[D, 1], [1, D]])),
                inc="dG", n=16)
            rep_ready = C["dG"]

            i_wk, w = alloc_panel()
            wload2(i_wk, wk_d, l * D * D, D, D, w)
            kv_dg = panel_loaded[i_wk]
            i_wv, w = alloc_panel()
            wload2(i_wv, wv_d, l * D * D, D, D, w)
            kv_dg = panel_loaded[i_wv]
            i_wq, w = alloc_panel()
            wload2(i_wq, wq_d, l * D * D, D, D, w)
            i_wo, w = alloc_panel()
            wload2(i_wo, wo_d, l * D * D, D, D, w)

            uph = [("tT", C["tT"])]

            pank = pans[i_wk]
            panv = pans[i_wv]
            xnT_snap = {}
            ek_snap = {}
            vv_snap = {}
            snap_ctx = None
            xn_snap = {}
            for k_it in range(NT + 3):
                if k_it < NT:
                    i = k_it
                    u = i & 1
                    x_i = x_sb[:, i, :]
                    sq_s, rstd_s = ln_stats(x_i, u, i, xwait=[("dS", xload)])
                    xn_snap[i] = ln_apply(x_i, u, i, rep[:, 0, :],
                                          rep[:, 1, :], sq_s, rstd_s,
                                          extra=[("dG", rep_ready)])
                if 1 <= k_it <= NT:
                    i = k_it - 1
                    u = i & 1
                    xnT_snap[i] = transposes(
                        lambda j, uu=u: sc_xn[:, uu, j * 128:(j + 1) * 128],
                        ("tV", xn_snap[i]), sc_xnT[:, u, :],
                        extra=(uph if i <= 1 else ()))
                if 2 <= k_it <= NT + 1:
                    i = k_it - 2
                    u = i & 1
                    last = None
                    for n0 in (0, 512):
                        for j in range(NJ):
                            w2 = []
                            if n0 == 0 and j == 0:
                                w2 = [("tV", xnT_snap[i]),
                                      ("tS", ek_snap.get(i - 1, 0)),
                                      ("dG", panel_loaded[i_wk])]
                            last = E("tensor",
                                     lambda t, jj=j, nn=n0, uu=u, pn=pank:
                                     t.matmul(ps_pA[:, nn:nn + 512],
                                              sc_xnT[:, uu,
                                                     jj * 128:(jj + 1) * 128],
                                              pn[:, jj * D + nn:
                                                 jj * D + nn + 512],
                                              start=(jj == 0),
                                              stop=(jj == NJ - 1)),
                                     waits=w2,
                                     inc=("tT" if (n0 == 512 and j == NJ - 1)
                                          else None))
                    kmm = last
                    for n0 in (0, 512):
                        for j in range(NJ):
                            w2 = []
                            if n0 == 0 and j == 0:
                                w2 = [("tV", vv_snap.get(i - 1, 0)),
                                      ("dG", panel_loaded[i_wv])]
                            last = E("tensor",
                                     lambda t, jj=j, nn=n0, uu=u, pn=panv:
                                     t.matmul(ps_pB[:, nn:nn + 512],
                                              sc_xnT[:, uu,
                                                     jj * 128:(jj + 1) * 128],
                                              pn[:, jj * D + nn:
                                                 jj * D + nn + 512],
                                              start=(jj == 0),
                                              stop=(jj == NJ - 1)),
                                     waits=w2,
                                     inc=("tT" if (n0 == 512 and j == NJ - 1)
                                          else None))
                    vmm = last
                    ek_snap[i] = E("scalar", lambda s, uu=u: s.activation(
                        sc_a[:, uu, :], ps_pA[:, :], AF.Exp),
                        waits=[("tT", kmm)] + (uph if i <= 1 else []),
                        inc="tS")
                    vv_snap[i] = E("vector", lambda v, uu=u: v.tensor_copy(
                        sc_b[:, uu, :], ps_pB[:, :]),
                        waits=[("tT", vmm)], inc="tV")
                if 3 <= k_it and k_it - 3 <= NT - 1:
                    i = k_it - 3
                    u = i & 1
                    w0 = [("tS", ek_snap[i]), ("tV", vv_snap[i])]
                    if i == 0:
                        w0 += [S["ctxps"], S["ms"]]
                    for h in range(NH):
                        E("tensor", lambda t, hh=h, uu=u, ii=i: t.matmul(
                            ps_ctx[:, hh * 128:(hh + 1) * 128],
                            sc_a[:, uu, hh * 128:(hh + 1) * 128],
                            sc_b[:, uu, hh * 128:(hh + 1) * 128],
                            start=(ii == 0), stop=(ii == NT - 1)),
                            waits=(w0 if h == 0 else ()), inc=None)
                        snap_ctx = E("tensor", lambda t, hh=h, uu=u, ii=i:
                                     t.matmul(
                                         ps_ms[:, hh:hh + 1],
                                         sc_a[:, uu, hh * 128:(hh + 1) * 128],
                                         ones_bf[:, :],
                                         start=(ii == 0), stop=(ii == NT - 1)),
                                     inc=("tT" if h == NH - 1 else None))
            free_panel(i_wk)
            free_panel(i_wv)
            v11 = E("vector", lambda v: v.reciprocal(
                dnr[:, 0:8], ps_ms[:, 0:8]),
                waits=[("tT", snap_ctx)], inc="tV")
            S["ms"] = ("tV", v11)
            ctxsb_s = None
            for h in range(NH):
                ctxsb_s = E("vector", lambda v, hh=h: v.tensor_scalar(
                    ctx_sb[:, hh * 128:(hh + 1) * 128],
                    ps_ctx[:, hh * 128:(hh + 1) * 128],
                    dnr[:, hh:hh + 1], None, ALU.mult),
                    waits=([("tV", v11)] if h == 0 else ()),
                    inc=("tV" if h == NH - 1 else None))
            S["ctxps"] = ("tV", ctxsb_s)

            panq = pans[i_wq]
            pano = pans[i_wo]
            xnb_snap = {}
            xnTb = {}
            eq_snap = {}
            sq_snap = {}
            sqT_snap = {}
            atT_snap = {}
            res_snap = {}
            for k_it in range(NT + 4):
                if k_it < NT:
                    i = k_it
                    u = i & 1
                    x_i = x_sb[:, i, :]
                    E("vector", lambda v, a2=x_i, ii=i: v.tensor_scalar(
                        fdump[:, :], a2, nm_all[:, ii:ii + 1],
                        rs_all[:, ii:ii + 1], ALU.add, ALU.mult),
                        waits=[S["fdump"]], inc="tV")
                    E("vector", lambda v: v.tensor_mul(
                        fdump[:, :], fdump[:, :], rep[:, 0, :]), inc=None)
                    xn_s = E("vector", lambda v, uu=u: v.tensor_add(
                        sc_xn[:, uu, :], fdump[:, :], rep[:, 1, :]), inc="tV")
                    S["fdump"] = ("tV", xn_s)
                    xnb_snap[i] = xn_s
                if 1 <= k_it <= NT:
                    i = k_it - 1
                    u = i & 1
                    xnTb[i] = transposes(
                        lambda j, uu=u: sc_xn[:, uu, j * 128:(j + 1) * 128],
                        ("tV", xnb_snap[i]), sc_xnT[:, u, :])
                if 2 <= k_it <= NT + 1:
                    i = k_it - 2
                    u = i & 1
                    q_s = None
                    for n0 in (0, 512):
                        for j in range(NJ):
                            w2 = []
                            if n0 == 0 and j == 0:
                                w2 = [("tV", xnTb[i]),
                                      ("tS", eq_snap.get(i - 1, 0)),
                                      ("dG", panel_loaded[i_wq])]
                            q_s = E("tensor",
                                    lambda t, jj=j, nn=n0, uu=u, pn=panq:
                                    t.matmul(ps_pA[:, nn:nn + 512],
                                             sc_xnT[:, uu,
                                                    jj * 128:(jj + 1) * 128],
                                             pn[:, jj * D + nn:
                                                jj * D + nn + 512],
                                             start=(jj == 0),
                                             stop=(jj == NJ - 1)),
                                    waits=w2,
                                    inc=("tT" if (n0 == 512 and j == NJ - 1)
                                         else None))
                    last = None
                    for h in range(NH):
                        last = E("scalar", lambda s, hh=h, uu=u: s.activation(
                            sc_a[:, uu, hh * 128:(hh + 1) * 128],
                            ps_pA[:, hh * 128:(hh + 1) * 128], AF.Exp,
                            accum_out=qrs[:, uu * 8 + hh:uu * 8 + hh + 1]),
                            waits=([("tT", q_s)] if h == 0 else ()),
                            inc=("tS" if h == NH - 1 else None))
                    eq_snap[i] = last
                    qr_s = E("vector", lambda v, uu=u: v.reciprocal(
                        qrcp[:, uu * 8:uu * 8 + 8],
                        qrs[:, uu * 8:uu * 8 + 8]),
                        waits=[("tS", last)], inc="tV")
                    last = None
                    for h in range(NH):
                        last = E("vector",
                                 lambda v, hh=h, uu=u: v.tensor_scalar(
                                     sc_b[:, uu, hh * 128:(hh + 1) * 128],
                                     sc_a[:, uu, hh * 128:(hh + 1) * 128],
                                     qrcp[:, uu * 8 + hh:uu * 8 + hh + 1],
                                     None, ALU.mult),
                                 waits=([("tV", qr_s)] if h == 0 else ()),
                                 inc=("tV" if h == NH - 1 else None))
                    sq_snap[i] = last
                if 3 <= k_it <= NT + 2:
                    c = k_it - 3
                    u = c & 1
                    sqT_snap[c] = transposes(
                        lambda j, uu=u: sc_b[:, uu, j * 128:(j + 1) * 128],
                        ("tV", sq_snap[c]), sc_a[:, u, :])
                if 4 <= k_it <= NT + 3:
                    yi = k_it - 4
                    u = yi & 1
                    x_i = x_sb[:, yi, :]
                    y_s = None
                    for n0 in (0, 512):
                        for h in range(NH):
                            w2 = []
                            if n0 == 0 and h == 0:
                                w2 = [("tV", atT_snap[yi]),
                                      ("tV", res_snap.get(yi - 1, 0)),
                                      ("dG", panel_loaded[i_wo]),
                                      ("dG", rep_ready)]
                                if yi == 0:
                                    w2.append(S["ctxps"])
                            E("tensor", lambda t, hh=h, nn=n0, uu=u,
                              pn=pano:
                              t.matmul(ps_ctx[:, nn:nn + 512],
                                       sc_c[:, uu, hh * 128:(hh + 1) * 128],
                                       pn[:, hh * D + nn:hh * D + nn + 512],
                                       start=(hh == 0), stop=False),
                              waits=w2, inc=None)
                        y_s = E("tensor", lambda t, nn=n0: t.matmul(
                            ps_ctx[:, nn:nn + 512], ones_row[:, :],
                            borow[:, nn:nn + 512], start=False, stop=True),
                            inc=("tT" if n0 == 512 else None))
                    res_snap[yi] = E("vector", lambda v, a2=x_i: v.tensor_add(
                        a2, a2, ps_ctx[:, :]),
                        waits=[("tT", y_s)], inc="tV")
                if 3 <= k_it <= NT + 2:
                    c = k_it - 3
                    u = c & 1
                    at_s = None
                    for h in range(NH):
                        w2 = []
                        if h == 0:
                            w2 = [("tV", sqT_snap[c]),
                                  ("tV", atT_snap.get(c - 1, 0))]
                            if c == 0:
                                w2.append(("tV", ctxsb_s))
                        at_s = E("tensor", lambda t, hh=h, uu=u: t.matmul(
                            ps_pB[:, hh * 128:(hh + 1) * 128],
                            ctx_sb[:, hh * 128:(hh + 1) * 128],
                            sc_a[:, uu, hh * 128:(hh + 1) * 128],
                            start=True, stop=True),
                            waits=w2, inc=("tT" if h == NH - 1 else None))
                    atT_snap[c] = E("vector", lambda v, uu=u: v.tensor_copy(
                        sc_c[:, uu, :], ps_pB[:, :]),
                        waits=[("tT", at_s)], inc="tV")
            free_panel(i_wq)
            free_panel(i_wo)
            S["pA"] = ("tS", eq_snap[NT - 1])
            S["pB"] = ("tV", atT_snap[NT - 1])
            S["ctxps"] = ("tV", res_snap[NT - 1])

            for half in range(2):
                tbase = half * 8
                x2T_ready = {}

                def ln2_stage(gl, guard):
                    ev = None
                    for sub in range(2):
                        i = tbase + 2 * gl + sub
                        u = i & 1
                        x_i = x_sb[:, i, :]
                        sq_s, rstd_s = ln_stats(x_i, u, i)
                        xn_s = ln_apply(x_i, u, i, rep[:, 2, :], rep[:, 3, :],
                                        sq_s, rstd_s)
                        tt = 2 * gl + sub
                        dst = bass.AP(x2T, tt * 128,
                                      [[8192, 128], [1024, NJ], [1, 128]])
                        ev = transposes(
                            lambda j, uu=u: sc_xn[:, uu,
                                                  j * 128:(j + 1) * 128],
                            ("tV", xn_s), dst,
                            extra=(guard if sub == 0 else ()))
                    x2T_ready[gl] = ev

                ln2_stage(0, [("tT", C["tT"]), ("tS", C["tS"])])
                for q in range(NQ4):
                    i_w1, w = alloc_panel()
                    wload2(i_w1, w1_d, l * D * FF + q * 1024, FF, 1024, w)
                    i_w2, w = alloc_panel()
                    wload2(i_w2, w2_d, l * FF * D + q * 1024 * D, D, 1024, w)
                    pw1, pw2 = pans[i_w1], pans[i_w2]
                    seq = [(gl, c) for gl in range(4) for c in range(NC8)]
                    lk_snap = {}
                    y2_snap = {}
                    ev_snap = {}

                    def emit_y2(idx2):
                        gl2, c2 = seq[idx2]
                        t0 = tbase + 2 * gl2
                        lastmm = None
                        for sub in range(2):
                            dst2 = ps_pA if sub == 0 else ps_pB
                            for n0 in (0, 512):
                                w3 = []
                                if sub == 0 and n0 == 0:
                                    w3 = [("tV", lk_snap[idx2]),
                                          ("dG", panel_loaded[i_w2])]
                                    if c2 == 0:
                                        w3 += [S["pA"], S["pB"]]
                                lastmm = E(
                                    "tensor",
                                    lambda t, ss=sub, nn=n0, cc=c2, d=dst2,
                                    ii=idx2, pn=pw2:
                                    t.matmul(
                                        d[:, nn:nn + 512],
                                        sc_c[:, ii & 1,
                                             ss * 128:ss * 128 + 128],
                                        pn[:, cc * 1024 + nn:
                                           cc * 1024 + nn + 512],
                                        start=(cc == 0), stop=(cc == NC8 - 1)),
                                    waits=w3,
                                    inc=("tT" if (sub == 1 and n0 == 512)
                                         else None))
                        y2_snap[idx2] = lastmm
                        if c2 == NC8 - 1:
                            ev = None
                            for sub in range(2):
                                src2 = ps_pA if sub == 0 else ps_pB
                                ev = E("vector",
                                       lambda v, ss=sub, s2=src2, t2=t0:
                                       v.tensor_add(x_sb[:, t2 + ss, :],
                                                    x_sb[:, t2 + ss, :],
                                                    s2[:, :]),
                                       waits=([("tT", lastmm)] if sub == 0
                                              else ()),
                                       inc=("tV" if sub == 1 else None))
                            ev_snap[gl2] = ev
                            S["pA"] = ("tV", ev)
                            S["pB"] = ("tV", ev)

                    for idx, (gl, c) in enumerate(seq):
                        if q == 0 and c == 0 and gl < 3:
                            ln2_stage(gl + 1, [])
                        boff = (idx & 1) * 512
                        lastmm = None
                        for j in range(NJ):
                            w2 = []
                            if j == 0:
                                w2 = [("tV", x2T_ready[gl]),
                                      ("tV", lk_snap.get(idx - 2, 0)),
                                      ("dG", panel_loaded[i_w1])]
                                if idx == 0:
                                    w2.append(S["ctxps"])
                            lastmm = E(
                                "tensor",
                                lambda t, jj=j, bo2=boff, cc=c, g2=gl,
                                pn=pw1:
                                t.matmul(
                                    ps_ctx[:, bo2:bo2 + 256],
                                    pn[:, jj * 1024 + cc * 128:
                                       jj * 1024 + cc * 128 + 128],
                                    bass.AP(x2T, jj * 1024 + g2 * 256,
                                            [[8192, 128], [1, 256]]),
                                    start=(jj == 0), stop=(jj == NJ - 1)),
                                waits=w2,
                                inc=("tT" if j == NJ - 1 else None))
                        E("vector", lambda v, bo2=boff: v.tensor_scalar(
                            ldump[:, :], ps_ctx[:, bo2:bo2 + 256], 0.01,
                            None, ALU.mult),
                            waits=[("tT", lastmm)], inc=None)
                        lk_snap[idx] = E("vector",
                                         lambda v, bo2=boff, ii=idx:
                                         v.tensor_max(
                                             sc_c[:, ii & 1, 0:256],
                                             ldump[:, :],
                                             ps_ctx[:, bo2:bo2 + 256]),
                                         inc="tV")
                        if idx >= 1:
                            emit_y2(idx - 1)
                    emit_y2(len(seq) - 1)
                    S["ctxps"] = ("tV", lk_snap[len(seq) - 1])
                    free_panel(i_w1)
                    free_panel(i_w2)
            S["rep"] = S["fdump"] if S["fdump"][0] == "tV" else ("tV", C["tV"])

        uphm = [("tT", C["tT"])]
        i_sp, sp_w = alloc_panel()
        pan_sp = pans[i_sp]
        mu_off = 4096
        so_off = 6144

        bc_load(rep[:, 0, :], fgamma, 0, D, [S["rep"]])
        bc_load(rep[:, 1, :], fbeta, 0, D, ())
        mod_rep_dg = C["dG"]
        mu_mm = 0
        mu_cp = 0
        last_mm = None
        for j in range(NJ):
            dl = E("sync", lambda s, jj=j: s.dma_start(
                mu_tmp[:, :], fmu[jj * 128:(jj + 1) * 128, :]),
                waits=([("tT", mu_mm), ("tV", mu_cp)] if j > 0
                       else [S["fdump"], ("tS", C["tS"])]),
                inc="dS", n=16)
            sqv = E("vector", lambda v: v.tensor_mul(
                ldump[:, :], mu_tmp[:, 0:256], mu_tmp[:, 0:256]),
                waits=[("dS", dl), ("tT", mu_mm)], inc="tV")
            mu_cp = E("vector", lambda v, jj=j, sp=pan_sp: v.tensor_copy(
                sp[:, mu_off + jj * 256:mu_off + (jj + 1) * 256],
                mu_tmp[:, :]),
                waits=(sp_w if j == 0 else ()), inc="tV")
            last_mm = E("tensor", lambda t, jj=j: t.matmul(
                ps_ms[0:1, 0:256], ones_f32c[:, :], ldump[:, :],
                start=(jj == 0), stop=(jj == NJ - 1)),
                waits=[("tV", sqv)] + ([S["ms"]] if j == 0 else []),
                inc="tT")
            mu_mm = last_mm
        n1 = E("scalar", lambda s: s.activation(
            ldump[0:1, 0:256], ps_ms[0:1, 0:256], AF.Sqrt),
            waits=[("tT", last_mm)], inc="tS")
        S["ms"] = ("tS", n1)
        n2 = E("vector", lambda v: v.tensor_scalar(
            ldump[0:1, 0:256], ldump[0:1, 0:256], 1e-12, None, ALU.max),
            waits=[("tS", n1)], inc="tV")
        n3 = E("vector", lambda v: v.reciprocal(ldump[0:1, 0:256], ldump[0:1, 0:256]),
               waits=[("tV", n2)], inc="tV")
        n4 = E("vector", lambda v: v.tensor_scalar(
            ldump[0:1, 0:256], ldump[0:1, 0:256], fsc_rep[0:1, 0:1], None, ALU.mult),
            waits=[("tV", n3)], inc="tV")
        nrep = E("tensor", lambda t: t.matmul(
            ps_ctx[:, 0:256], ones_f32r[:, :], ldump[0:1, 0:256],
            start=True, stop=True),
            waits=[("tV", n4), S["ctxps"], ("tG", ginit)], inc="tT")
        mu_ready = None
        for j in range(NJ):
            mu_ready = E("vector", lambda v, jj=j, sp=pan_sp: v.tensor_mul(
                sp[:, mu_off + jj * 256:mu_off + (jj + 1) * 256],
                sp[:, mu_off + jj * 256:mu_off + (jj + 1) * 256],
                ps_ctx[:, 0:256]),
                waits=([("tT", nrep), ("tV", mu_cp)] if j == 0 else ()),
                inc=("tV" if j == NJ - 1 else None))
        S["ctxps"] = ("tV", mu_ready)
        gid = E("gpsimd", lambda g: (
            g.memset(ldump[:, 0:128], 0.0),
            g.affine_select(out=ldump[:, 0:128], in_=ldump[:, 0:128],
                            compare_op=ALU.not_equal, fill=1.0, base=0,
                            pattern=[[-1, 128]], channel_multiplier=1))[-1],
            waits=[("tV", mu_cp), ("tV", C["tV"])], inc="tG")
        d1 = E("sync", lambda s: s.dma_start(
            fdump[0:16, 0:256], fb1[:, :]),
            waits=[S["fdump"]], inc="dS", n=16)
        S["fdump"] = ("dS", d1)
        lastv = None
        for hc in range(2):
            w2 = [("dS", d1), ("tG", gid), S["pA"]] if hc == 0 else [
                ("tV", lastv)]
            t1 = E("tensor", lambda t, h2=hc: t.transpose(
                ps_pA[:, h2 * 128:h2 * 128 + 16],
                fdump[0:16, h2 * 128:h2 * 128 + 128],
                ldump[0:16, 0:16]),
                waits=w2, inc="tT")
            lastv = E("vector", lambda v, h2=hc: v.tensor_copy(
                b1T[:, h2, :], ps_pA[:, h2 * 128:h2 * 128 + 16]),
                waits=[("tT", t1)], inc="tV")
        d2 = E("sync", lambda s: s.dma_start(fdump[0:16, :], fb2[:, :]),
               waits=[("tV", lastv)], inc="dS", n=16)
        S["fdump"] = ("dS", d2)
        for dc in range(NJ):
            w2 = [("dS", d2), ("tV", lastv)] if dc == 0 else [("tV", lastv)]
            t1 = E("tensor", lambda t, d3=dc: t.transpose(
                ps_pA[:, d3 * 128:d3 * 128 + 16],
                fdump[0:16, d3 * 128:d3 * 128 + 128],
                ldump[0:16, 0:16]),
                waits=w2, inc="tT")
            lastv = E("vector", lambda v, d3=dc: v.tensor_copy(
                b2T[:, d3, :], ps_pA[:, d3 * 128:d3 * 128 + 16]),
                waits=[("tT", t1)], inc="tV")
        S["pA"] = ("tV", lastv)
        S["fdump"] = ("tV", lastv)

        quartets = {}

        def load_quartet(tq):
            iq1, w = alloc_panel()
            for e2 in range(4):
                for j2 in range(NJ):
                    src = bass.AP(fw1, (4 * tq + e2) * D * 256 + j2 * 128 * 256,
                                  [[256, 128], [1, 256]])
                    E("gpsimd",
                      lambda g, d=pans[iq1][:, e2 * 2048 + j2 * 256:
                                            e2 * 2048 + (j2 + 1) * 256],
                      s2=src: g.dma_start(d, s2),
                      waits=(w if (e2 == 0 and j2 == 0) else ()),
                      inc="dG", n=16)
            panel_loaded[iq1] = C["dG"]
            iq2, w = alloc_panel()
            for e2 in range(4):
                for h2 in range(2):
                    src = bass.AP(fw2, (4 * tq + e2) * 256 * D + h2 * 128 * D,
                                  [[D, 128], [1, D]])
                    E("gpsimd",
                      lambda g, d=pans[iq2][:, e2 * 2048 + h2 * 1024:
                                            e2 * 2048 + (h2 + 1) * 1024],
                      s2=src: g.dma_start(d, s2),
                      waits=(w if (e2 == 0 and h2 == 0) else ()),
                      inc="dG", n=16)
            panel_loaded[iq2] = C["dG"]
            quartets[tq] = (iq1, iq2)

        load_quartet(0)
        load_quartet(1)

        xio_free = {0: ("tV", 0), 1: ("tV", 0)}
        logit_free = ("tS", 0)
        slot_mm_last = None
        for i in range(NT):
            u = i & 1
            x_i = x_sb[:, i, :]
            src = bass.AP(x_in, b * 2048 * D + i * 128 * D,
                          [[D, 128], [1, D]])
            dl = E("sync", lambda s, uu=u, sr=src: s.dma_start(
                xio[:, uu, :], sr),
                waits=[xio_free[u]] + (uphm if i <= 1 else []),
                inc="dS", n=16)
            add_s = E("vector", lambda v, a=x_i, uu=u: v.tensor_add(
                a, a, xio[:, uu, :]),
                waits=[("dS", dl)], inc="tV")
            xio_free[u] = ("tV", add_s)
            sq_s, rstd_s = ln_stats(x_i, u, i, xwait=[("tV", add_s)])
            x1_s = E("vector", lambda v, a=x_i, ii=i: v.tensor_scalar(
                a, a, nm_all[:, ii:ii + 1], rs_all[:, ii:ii + 1],
                ALU.add, ALU.mult),
                waits=[("tV", rstd_s), S["fdump"]], inc="tV")
            sq2, rstd2 = ln_stats(x_i, u, i, xwait=[("tV", x1_s)])
            xn_s = ln_apply(x_i, u, i, rep[:, 0, :], rep[:, 1, :],
                            sq2, rstd2, extra=[("dG", mod_rep_dg)])
            s3 = E("scalar", lambda s, uu=u: s.activation(
                fdump[:, :], sc_xn[:, uu, :], AF.Square,
                accum_out=st[:, uu * 8 + 6:uu * 8 + 7]),
                waits=[("tV", xn_s)], inc="tS")
            S["fdump"] = ("tS", s3)
            s4 = E("scalar", lambda s, uu=u: s.activation(
                st[:, uu * 8 + 7:uu * 8 + 8], st[:, uu * 8 + 6:uu * 8 + 7],
                AF.Sqrt),
                waits=[("tS", s3)], inc="tS")
            v8 = E("vector", lambda v, uu=u: v.tensor_scalar(
                st[:, uu * 8 + 6:uu * 8 + 7], st[:, uu * 8 + 7:uu * 8 + 8],
                1e-12, None, ALU.max),
                waits=[("tS", s4)], inc="tV")
            v9 = E("vector", lambda v, uu=u: v.reciprocal(
                st[:, uu * 8 + 7:uu * 8 + 8], st[:, uu * 8 + 6:uu * 8 + 7]),
                waits=[("tV", v8)], inc="tV")
            rn_s = E("vector", lambda v, uu=u: v.tensor_scalar(
                st[:, uu * 8 + 6:uu * 8 + 7], st[:, uu * 8 + 7:uu * 8 + 8],
                fsc_rep[:, 0:1], None, ALU.mult),
                waits=[("tV", v9)], inc="tV")
            if b == 0 and i == 0:
                dump("rn0", st[:, 0:16], [128, 16], F32)
                dump("mus", pan_sp[:, 4096:6144], [128, 2048], BF16, cast=True)
                dump("xnm0", sc_xn[:, 0, :], [128, D], BF16, cast=True)
            xnT_s = transposes(
                lambda j, uu=u: sc_xn[:, uu, j * 128:(j + 1) * 128],
                ("tV", xn_s), sc_xnT[:, u, :],
                extra=(uphm if i <= 1 else ()))
            lg_s = None
            for j in range(NJ):
                w2 = []
                if j == 0:
                    w2 = [("tV", xnT_s), ("tV", mu_ready), logit_free,
                          S["ctxps"] if i == 0 else ("tV", 0)]
                lg_s = E("tensor", lambda t, jj=j, uu=u, sp=pan_sp: t.matmul(
                    ps_ctx[:, 0:256],
                    sc_xnT[:, uu, jj * 128:(jj + 1) * 128],
                    sp[:, mu_off + jj * 256:mu_off + (jj + 1) * 256],
                    start=(jj == 0), stop=(jj == NJ - 1)),
                    waits=w2, inc=("tT" if j == NJ - 1 else None))
            eE = E("scalar", lambda s, ii=i, uu=u, sp=pan_sp: s.activation(
                sp[:, ii * 256:(ii + 1) * 256], ps_ctx[:, 0:256],
                AF.Exp, scale=st[:, uu * 8 + 6:uu * 8 + 7]),
                waits=[("tT", lg_s), ("tV", rn_s)], inc="tS")
            logit_free = ("tS", eE)
            for esc in range(2):
                w2 = [("tS", eE)]
                if i == 0 and esc == 0:
                    w2 += [S["ms"], S["pA"], S["pB"]]
                E("tensor", lambda t, e2=esc, ii=i, sp=pan_sp: t.matmul(
                    ps_ms[:, e2:e2 + 1],
                    sp[:, ii * 256 + e2 * 128:ii * 256 + e2 * 128 + 128],
                    ones_bf[:, :],
                    start=(ii == 0), stop=(ii == NT - 1)),
                    waits=w2, inc=None)
                dst = ps_pA if esc == 0 else ps_pB
                for n0 in (0, 512):
                    slot_mm_last = E(
                        "tensor",
                        lambda t, e2=esc, nn=n0, ii=i, uu=u, d=dst,
                        sp=pan_sp:
                        t.matmul(
                            d[:, nn:nn + 512],
                            sp[:, ii * 256 + e2 * 128:
                               ii * 256 + e2 * 128 + 128],
                            sc_xn[:, uu, nn:nn + 512],
                            start=(ii == 0), stop=(ii == NT - 1)),
                        inc=("tT" if (esc == 1 and n0 == 512) else None))
        if b == 0:
            dump("x1", x_sb[:, :, :], [128, NT * D], F32)
            dump("E", pan_sp[:, 0:4096], [128, 4096], BF16, cast=True)
        vz = E("vector", lambda v: v.reciprocal(zrcp[:, :], ps_ms[:, 0:2]),
               waits=[("tT", slot_mm_last)], inc="tV")
        S["ms"] = ("tV", vz)
        si_s = None
        for esc in range(2):
            src = ps_pA if esc == 0 else ps_pB
            si_s = E("vector", lambda v, e2=esc, s2=src: v.tensor_scalar(
                sc_a[:, e2, :], s2[:, :], zrcp[:, e2:e2 + 1], None, ALU.mult),
                waits=([("tV", vz)] if esc == 0 else ()),
                inc=("tV" if esc == 1 else None))
        S["pA"] = ("tV", si_s)
        S["pB"] = ("tV", si_s)
        if b == 0:
            dump("zrcp", zrcp[:, :], [128, 2], F32)
            dump("slotin", sc_a[:, :, :], [128, 2 * D], BF16, cast=True)
        siT_s = None
        for esc in range(2):
            dstv = bass.AP(sc_b, esc * 128, [[2048, 128], [256, NJ], [1, 128]])
            siT_s = transposes(
                lambda j, e2=esc: sc_a[:, e2, j * 128:(j + 1) * 128],
                ("tV", si_s), dstv)
        so_free = ("tV", 0)
        gprev = ("tS", 0)
        so_evac = None
        for e in range(16):
            tq, eo = e // 4, e % 4
            iq1, iq2 = quartets[tq]
            pq1, pq2 = pans[iq1], pans[iq2]
            gel = {}
            for hc in range(2):
                last = None
                for j in range(NJ):
                    w2 = []
                    if j == 0 and hc == 0:
                        w2 = [("tV", siT_s), ("dG", panel_loaded[iq1]), gprev]
                    elif j == 0 and hc == 1:
                        w2 = [("tS", gel[0])]
                    last = E("tensor",
                             lambda t, jj=j, h2=hc, e2=eo, p1=pq1, ee=e:
                             t.matmul(
                                 ps_ms[:, 32 + h2 * 16:32 + h2 * 16 + 16],
                                 p1[:, e2 * 2048 + jj * 256 + h2 * 128:
                                    e2 * 2048 + jj * 256 + h2 * 128 + 128],
                                 bass.AP(sc_b, jj * 256 + ee * 16,
                                         [[2048, 128], [1, 16]]),
                                 start=(jj == 0), stop=(jj == NJ - 1)),
                             waits=w2, inc=("tT" if j == NJ - 1 else None))
                gel[hc] = E("scalar", lambda s, h2=hc, e2=e: s.activation(
                    hT[:, h2, :], ps_ms[:, 32 + h2 * 16:32 + h2 * 16 + 16],
                    AF.Gelu, bias=b1T[:, h2, e2:e2 + 1]),
                    waits=[("tT", last)], inc="tS")
            gprev = ("tS", gel[1])
            if b == 0 and e == 0 and "hT0" in dbg:
                dw = dump("hT0", hT[:, :, :], [128, 32], BF16, cast=True)
                if dw:
                    gprev = dw
            if b == 0 and e == 0 and "siT" in dbg:
                dump("siT", sc_b[:, :, :], [128, 2048], BF16, cast=True)
            if b == 0 and e == 0 and "fw2p" in dbg:
                dump("fw2p", pans[iq2][:, 0:2048], [128, 2048], BF16, cast=True)
                dump("b2Td", b2T[:, :, :], [128, 128], F32)
            last = None
            for dc in range(NJ):
                for hc in range(2):
                    w2 = []
                    if dc == 0 and hc == 0:
                        w2 = [("tS", gel[0]), ("tS", gel[1]),
                              ("dG", panel_loaded[iq2]), so_free]
                    last = E("tensor",
                             lambda t, d3=dc, h2=hc, e2=eo, p2=pq2:
                             t.matmul(
                                 ps_ctx[:, d3 * 16:d3 * 16 + 16],
                                 p2[:, e2 * 2048 + h2 * 1024 + d3 * 128:
                                    e2 * 2048 + h2 * 1024 + d3 * 128 + 128],
                                 hT[:, h2, :],
                                 start=(h2 == 0), stop=(h2 == 1)),
                             waits=w2, inc=("tT" if (dc == NJ - 1 and hc == 1)
                                            else None))
            for dc in range(NJ):
                so_evac = E("vector", lambda v, d3=dc, e2=e, sp=pan_sp:
                    v.tensor_scalar(
                    bass.AP(sp, so_off + d3 * 256 + e2 * 16,
                            [[8192, 128], [1, 16]]),
                    ps_ctx[:, d3 * 16:d3 * 16 + 16],
                    b2T[:, d3, e2:e2 + 1], None, ALU.add),
                    waits=([("tT", last)] if dc == 0 else ()),
                    inc=("tV" if dc == NJ - 1 else None))
            so_free = ("tV", so_evac)
            if eo == 3:
                free_panel(iq1)
                free_panel(iq2)
                if tq + 2 < 4:
                    load_quartet(tq + 2)
        S["ctxps"] = ("tV", so_evac)
        if b == 0:
            dump("soT", pan_sp[:, 6144:8192], [128, 2048], BF16, cast=True)
        so_s = None
        for esc in range(2):
            dstv = bass.AP(sc_b, esc * 1024, [[2048, 128], [1, 1024]])
            so_s = transposes(
                lambda dc, e2=esc, sp=pan_sp: sp[
                    :, so_off + dc * 256 + e2 * 128:
                    so_off + dc * 256 + e2 * 128 + 128],
                ("tV", so_evac), dstv)
        if b == 0:
            dump("so", sc_b[:, :, :], [128, 2 * D], BF16, cast=True)
        st_dma = {0: 0, 1: 0}
        for i in range(NT):
            u = i & 1
            x_i = x_sb[:, i, :]
            rsum = E("vector", lambda v, ii=i, uu=u, sp=pan_sp: v.reduce_sum(
                st[:, uu * 8:uu * 8 + 1],
                sp[:, ii * 256:(ii + 1) * 256], axis=AX.X),
                waits=[("tV", so_s)], inc="tV")
            rrec = E("vector", lambda v, uu=u: v.reciprocal(
                st[:, uu * 8 + 1:uu * 8 + 2], st[:, uu * 8:uu * 8 + 1]),
                waits=[("tV", rsum)], inc="tV")
            eT_s = transposes(
                lambda e2, ii=i, sp=pan_sp: sp[:, ii * 256 + e2 * 128:
                                               ii * 256 + e2 * 128 + 128],
                ("tV", 0), sc_a[:, u, 0:256], nblk=2)
            last = None
            for esc in range(2):
                for n0 in (0, 512):
                    w2 = []
                    if esc == 0 and n0 == 0:
                        w2 = [("tV", eT_s), S["pA"]]
                    last = E("tensor",
                             lambda t, e2=esc, nn=n0, uu=u: t.matmul(
                                 ps_pA[:, nn:nn + 512],
                                 sc_a[:, uu, e2 * 128:e2 * 128 + 128],
                                 bass.AP(sc_b, e2 * 1024 + nn,
                                         [[2048, 128], [1, 512]]),
                                 start=(e2 == 0), stop=(e2 == 1)),
                             waits=w2, inc=("tT" if (esc == 1 and n0 == 512)
                                            else None))
            x2 = E("vector", lambda v, a=x_i, uu=u: v.scalar_tensor_tensor(
                a, ps_pA[:, :], st[:, uu * 8 + 1:uu * 8 + 2], a,
                ALU.mult, ALU.add),
                waits=[("tT", last), ("tV", rrec)], inc="tV")
            S["pA"] = ("tV", x2)
            sq_s, rstd_s = ln_stats(x_i, u, i, xwait=[("tV", x2)])
            ov = E("vector", lambda v, a=x_i, ii=i, uu=u: v.tensor_scalar(
                xio[:, uu, :], a, nm_all[:, ii:ii + 1], rs_all[:, ii:ii + 1],
                ALU.add, ALU.mult),
                waits=[("tV", rstd_s), S["fdump"], ("dS", st_dma[u]),
                       xio_free[u]], inc="tV")
            xio_free[u] = ("tV", ov)
            dst = bass.AP(y_out, b * 2048 * D + i * 128 * D,
                          [[D, 128], [1, D]])
            st_dma[u] = E("sync", lambda s, uu=u, d2=dst: s.dma_start(
                d2, xio[:, uu, :]),
                waits=[("tV", ov)], inc="dS", n=16)
        if b == 0:
            dump("xend", x_sb[:, :, :], [128, NT * D], F32)
            dump("Eb0", pan_sp[:, 0:4096], [128, 4096], BF16, cast=True)
            dump("sob", sc_b[:, :, :], [128, 2048], BF16, cast=True)
        free_panel(i_sp)
        S["x_free"] = ("tV", C["tV"])
        S["rep"] = ("tV", C["tV"])

    E("sync", None, waits=[("dS", C["dS"])])

    semh = {s: es.enter_context(nc.semaphore(s)) for s in sems}
    block = es.enter_context(nc.Block())

    def mk_replay(name):
        def fn(engobj):
            waited = {}
            for f, ws, inc, n in p.ev[name]:
                for s2, v in ws:
                    if s2 == "zz_taken":
                        raise RuntimeError("panel used while reserved")
                    if waited.get(s2, 0) < v:
                        engobj.wait_ge(semh[s2], v)
                        waited[s2] = v
                ins = f(engobj) if f is not None else None
                if inc is not None:
                    if ins is not None:
                        ins.then_inc(semh[inc], n)
                    else:
                        engobj.sem_inc(semh[inc], n)
        return fn

    block.sync(mk_replay("sync"))
    block.tensor(mk_replay("tensor"))
    block.vector(mk_replay("vector"))
    block.scalar(mk_replay("scalar"))
    block.gpsimd(mk_replay("gpsimd"))

    es.close()
    return nc


def kernel(**inputs):
    f = lambda k: np.ascontiguousarray(np.asarray(inputs[k], dtype=np.float32))
    x = f("L0_attn_x")
    shared = {
        "ln1_w": f("L0_attn_ln1_w"), "ln1_b": f("L0_attn_ln1_b"),
        "ln2_w": f("L0_attn_ln2_w"), "ln2_b": f("L0_attn_ln2_b"),
        "bo_d": f("L0_attn_bo"),
        "wq_d": f("L0_attn_wq"), "wk_d": f("L0_attn_wk"),
        "wv_d": f("L0_attn_wv"), "wo_d": f("L0_attn_wo"),
        "w1_d": f("L0_attn_w1"), "w2_d": f("L0_attn_w2"),
        "fgamma": f("L0_ffn_ln_gamma"), "fbeta": f("L0_ffn_ln_beta"),
        "fmu": f("L0_ffn_mu").reshape(1024, 256),
        "fscale": f("L0_ffn_scale"),
        "fw1": f("L0_ffn_w1"), "fb1": f("L0_ffn_b1"),
        "fw2": f("L0_ffn_w2"), "fb2": f("L0_ffn_b2"),
    }
    nc = build_nc()
    in_maps = []
    for c in range(8):
        m = {"x_in": x[2 * c:2 * c + 2]}
        m.update(shared)
        in_maps.append(m)
    res = run_bass_kernel_spmd(nc, in_maps, core_ids=list(range(8)))
    out = np.concatenate([np.asarray(r["y_out"]) for r in res.results], axis=0)
    return out.astype(np.float32)


# version 04
# speedup vs baseline: 1.7648x; 1.0700x over previous
import numpy as np
from contextlib import ExitStack

import concourse.bass as bass
import concourse.mybir as mybir
from concourse.bass_utils import run_bass_kernel_spmd

F32 = mybir.dt.float32
BF16 = mybir.dt.bfloat16
AF = mybir.ActivationFunctionType
ALU = mybir.AluOpType
AX = mybir.AxisListType

ENGINES = ("sync", "tensor", "vector", "scalar", "gpsimd")

D = 1024
NT = 16
NJ = 8
NH = 8
FF = 4096
NQ4 = 4
NC8 = 8


class Prog:
    def __init__(self, semnames):
        self.ev = {e: [] for e in ENGINES}
        self.cnt = {s: 0 for s in semnames}

    def emit(self, eng, fn=None, waits=(), inc=None, n=1):
        ws = tuple((s, v) for (s, v) in waits if v and v > 0)
        self.ev[eng].append((fn, ws, inc, n))
        if inc is not None:
            self.cnt[inc] += n
            return self.cnt[inc]
        return None


def build_nc(NB=2, NL=6, dbg=(), only_b0=False):
    nc = bass.Bass("TRN2", target_bir_lowering=False)

    x_in = nc.dram_tensor("x_in", [NB, 2048, D], F32, kind="ExternalInput")
    ln1_w = nc.dram_tensor("ln1_w", [6, D], F32, kind="ExternalInput")
    ln1_b = nc.dram_tensor("ln1_b", [6, D], F32, kind="ExternalInput")
    ln2_w = nc.dram_tensor("ln2_w", [6, D], F32, kind="ExternalInput")
    ln2_b = nc.dram_tensor("ln2_b", [6, D], F32, kind="ExternalInput")
    bo_d = nc.dram_tensor("bo_d", [6, D], F32, kind="ExternalInput")
    wq_d = nc.dram_tensor("wq_d", [6, D, D], F32, kind="ExternalInput")
    wk_d = nc.dram_tensor("wk_d", [6, D, D], F32, kind="ExternalInput")
    wv_d = nc.dram_tensor("wv_d", [6, D, D], F32, kind="ExternalInput")
    wo_d = nc.dram_tensor("wo_d", [6, D, D], F32, kind="ExternalInput")
    w1_d = nc.dram_tensor("w1_d", [6, D, FF], F32, kind="ExternalInput")
    w2_d = nc.dram_tensor("w2_d", [6, FF, D], F32, kind="ExternalInput")
    fgamma = nc.dram_tensor("fgamma", [D], F32, kind="ExternalInput")
    fbeta = nc.dram_tensor("fbeta", [D], F32, kind="ExternalInput")
    fmu = nc.dram_tensor("fmu", [D, 256], F32, kind="ExternalInput")
    fscale = nc.dram_tensor("fscale", [1], F32, kind="ExternalInput")
    fw1 = nc.dram_tensor("fw1", [16, D, 256], F32, kind="ExternalInput")
    fb1 = nc.dram_tensor("fb1", [16, 256], F32, kind="ExternalInput")
    fw2 = nc.dram_tensor("fw2", [16, 256, D], F32, kind="ExternalInput")
    fb2 = nc.dram_tensor("fb2", [16, D], F32, kind="ExternalInput")
    y_out = nc.dram_tensor("y_out", [NB, 2048, D], F32, kind="ExternalOutput")

    es = ExitStack()
    sb = lambda n, s, d: es.enter_context(nc.sbuf_tensor(n, s, d))
    psa = lambda n, s, d: es.enter_context(nc.psum_tensor(n, s, d))

    x_sb = sb("x_sb", [128, NT, D], F32)
    pans = [sb(f"pan{i}", [128, 8192], BF16) for i in range(6)]
    u_arena = sb("u_arena", [128, 16384], mybir.dt.uint8)
    ua_addr = nc.lookup_mloc(u_arena).addr
    sc_xnT = nc.alloc_sbuf_tensor_at("sc_xnT", [128, 2, D], BF16,
                                     offset=ua_addr)
    sc_a = nc.alloc_sbuf_tensor_at("sc_a", [128, 2, D], BF16,
                                   offset=ua_addr + 4096)
    xio = nc.alloc_sbuf_tensor_at("xio", [128, 2, D], F32,
                                  offset=ua_addr + 8192)
    x2T = nc.alloc_sbuf_tensor_at("x2T", [128, NJ, 1024], BF16,
                                  offset=ua_addr)
    rep = sb("rep", [128, 4, D], BF16)
    borow = sb("borow", [1, D], BF16)
    ident = sb("ident", [128, 128], BF16)
    ones_bf = sb("ones_bf", [128, 1], BF16)
    ones_row = sb("ones_row", [1, 128], BF16)
    ones_f32c = sb("ones_f32c", [128, 1], F32)
    ones_f32r = sb("ones_f32r", [1, 128], F32)
    ctx_sb = sb("ctx_sb", [128, D], BF16)
    sc_xn = sb("sc_xn", [128, 2, D], BF16)
    sc_b = sb("sc_b", [128, 2, D], BF16)
    sc_c = sb("sc_c", [128, 2, D], BF16)
    fdump = sb("fdump", [128, D], F32)
    ldump = sb("ldump", [128, 256], F32)
    st = sb("st", [128, 16], F32)
    nm_all = sb("nm_all", [128, NT], F32)
    rs_all = sb("rs_all", [128, NT], F32)
    qrs = sb("qrs", [128, 16], F32)
    qrcp = sb("qrcp", [128, 16], F32)
    dnr = sb("dnr", [128, 8], F32)
    fsc_rep = sb("fsc_rep", [128, 1], F32)
    b1T = sb("b1T", [128, 2, 16], F32)
    b2T = sb("b2T", [128, 8, 16], F32)
    hT = sb("hT", [128, 2, 16], BF16)
    zrcp = sb("zrcp", [128, 2], F32)
    eps_sb = sb("eps_sb", [128, 1], F32)
    mu_tmp = nc.alloc_sbuf_tensor_at(
        "mu_tmp", [128, 256], F32,
        offset=nc.lookup_mloc(fdump).addr + 1024)

    ps_pA = psa("ps_pA", [128, 1024], F32)
    ps_pB = psa("ps_pB", [128, 1024], F32)
    ps_ctx = psa("ps_ctx", [128, 1024], F32)
    ps_tr = psa("ps_tr", [128, 1024], BF16)
    ps_ms = psa("ps_ms", [128, 512], F32)

    sems = ("dG", "dS", "tT", "tV", "tS", "tG")
    p = Prog(sems)
    E = p.emit
    C = p.cnt

    S = {
        "fdump": ("tV", 0),
        "pA": ("tV", 0),
        "pB": ("tV", 0),
        "ctxps": ("tV", 0),
        "ms": ("tV", 0),
        "tr": ("tV", 0),
        "rep": ("tV", 0),
        "x_free": ("tV", 0),
    }

    dbg_outs = {}

    def dump(tag, src_ap, shape, dt, cast=False):
        if tag not in dbg:
            return
        h = nc.dram_tensor("dbg_" + tag, shape, F32 if cast else dt,
                           kind="ExternalOutput")
        eng = "gpsimd" if cast else "sync"
        sem = "dG" if eng == "gpsimd" else "dS"
        snapd = E(eng, lambda g, d=h, s2=src_ap: g.dma_start(
            bass.AP(d, 0, [[int(np.prod(shape[1:])), shape[0]]] +
                    [[1, int(np.prod(shape[1:]))]]), s2),
          waits=[("tT", C["tT"]), ("tV", C["tV"]), ("tS", C["tS"]),
                 ("tG", C["tG"])],
          inc=sem, n=16)
        dbg_outs[tag] = True
        return (sem, snapd)

    def init_fn(g):
        g.memset(ident[:, :], 0.0)
        g.affine_select(out=ident[:, :], in_=ident[:, :],
                        compare_op=ALU.not_equal, fill=1.0, base=0,
                        pattern=[[-1, 128]], channel_multiplier=1)
        g.memset(ones_bf[:, :], 1.0)
        g.memset(ones_row[:, :], 1.0)
        g.memset(ones_f32c[:, :], 1.0)
        g.memset(eps_sb[:, :], 1e-5)
        return g.memset(ones_f32r[:, :], 1.0)
    ginit = E("gpsimd", init_fn, inc="tG")
    E("gpsimd", lambda g: g.dma_start(
        fsc_rep[:, :], bass.AP(fscale, 0, [[0, 128], [1, 1]])),
        inc="dG", n=16)

    panel_free = [[] for _ in range(6)]
    panel_loaded = [0] * 6

    def alloc_panel():
        def key(i):
            f = panel_free[i]
            return max((v for (_, v) in f), default=0)
        idx = min(range(6), key=key)
        w = list(panel_free[idx])
        panel_free[idx] = [("zz_taken", 10 ** 9)]
        return idx, w

    def free_panel(idx):
        panel_free[idx] = [("tT", C["tT"]), ("tV", C["tV"]), ("tS", C["tS"])]

    def wload2(idx, dram_h, base_off, row_stride, cols, w):
        pan = pans[idx]
        for j in range(NJ):
            src = bass.AP(dram_h, base_off + j * 128 * row_stride,
                          [[row_stride, 128], [1, cols]])
            E("gpsimd",
              lambda g, d=pan[:, j * cols:(j + 1) * cols], s2=src:
              g.dma_start(d, s2),
              waits=(w if j == 0 else ()), inc="dG", n=16)
        panel_loaded[idx] = C["dG"]

    def bc_load(dst_ap, src_handle, src_off, ncols, waits):
        return E("gpsimd",
                 lambda g, d=dst_ap, h=src_handle, o=src_off, n2=ncols:
                 g.dma_start(d, bass.AP(h, o, [[0, 128], [1, n2]])),
                 waits=waits, inc="dG", n=16)

    def ln_stats(tile_ap, u, col, xwait=()):
        b0 = u * 8
        ns, sq, vr, n2v, v2, sd = b0, b0 + 1, b0 + 2, b0 + 3, b0 + 4, b0 + 5
        v1 = E("vector", lambda v, a=tile_ap: v.reduce_sum(
            st[:, ns:ns + 1], a, axis=AX.X, negate=True),
            waits=xwait, inc="tV")
        s1 = E("scalar", lambda s, a=tile_ap: s.activation(
            fdump[:, :], a, AF.Square, accum_out=st[:, sq:sq + 1]),
            waits=[S["fdump"]] + list(xwait), inc="tS")
        S["fdump"] = ("tS", s1)
        v2s = E("vector", lambda v, c2=col: v.tensor_scalar(
            nm_all[:, c2:c2 + 1], st[:, ns:ns + 1], 1.0 / D, None, ALU.mult),
            waits=[("tV", v1)], inc="tV")
        v3 = E("vector", lambda v: v.tensor_scalar(
            st[:, vr:vr + 1], st[:, sq:sq + 1], 1.0 / D, None, ALU.mult),
            waits=[("tS", s1)], inc="tV")
        v4 = E("vector", lambda v, c2=col: v.tensor_scalar(
            st[:, n2v:n2v + 1], nm_all[:, c2:c2 + 1],
            nm_all[:, c2:c2 + 1], None, ALU.mult),
            waits=[("tV", v2s)], inc="tV")
        v5 = E("vector", lambda v: v.tensor_sub(
            st[:, v2:v2 + 1], st[:, vr:vr + 1], st[:, n2v:n2v + 1]),
            waits=[("tV", v4)], inc="tV")
        s2 = E("scalar", lambda s: s.activation(
            st[:, sd:sd + 1], st[:, v2:v2 + 1], AF.Sqrt,
            bias=eps_sb[:, 0:1]),
            waits=[("tV", v5), ("tG", ginit)], inc="tS")
        v6 = E("vector", lambda v, c2=col: v.reciprocal(
            rs_all[:, c2:c2 + 1], st[:, sd:sd + 1]),
            waits=[("tS", s2)], inc="tV")
        return s1, v6

    def ln_apply(tile_ap, u, col, wrep_ap, brep_ap, sq_snap, rstd_snap,
                 extra=()):
        E("vector", lambda v, a=tile_ap, c2=col: v.tensor_scalar(
            fdump[:, :], a, nm_all[:, c2:c2 + 1], rs_all[:, c2:c2 + 1],
            ALU.add, ALU.mult),
            waits=[("tV", rstd_snap), S["fdump"]] + list(extra), inc="tV")
        E("vector", lambda v, w=wrep_ap: v.tensor_mul(
            fdump[:, :], fdump[:, :], w), inc=None)
        vc = E("vector", lambda v, b2=brep_ap, u2=u: v.tensor_add(
            sc_xn[:, u2, :], fdump[:, :], b2), inc="tV")
        S["fdump"] = ("tV", vc)
        return vc

    def transposes(src_view, src_wait, dst_ap, nblk=8, extra=(),
                   eng="vector"):
        last_t = None
        for j in range(nblk):
            w = []
            if j == 0:
                w = [src_wait, S["tr"], ("tG", ginit)] + list(extra)
            last_t = E("tensor",
                       lambda t, jj=j, sv=src_view: t.transpose(
                           ps_tr[:, jj * 128:(jj + 1) * 128],
                           sv(jj), ident[:, :]),
                       waits=w, inc=("tT" if j == nblk - 1 else None))
        if eng == "vector":
            ev = E("vector", lambda v, d=dst_ap, n2=nblk: v.tensor_copy(
                d, ps_tr[:, 0:n2 * 128]),
                waits=[("tT", last_t)], inc="tV")
            pair = ("tV", ev)
        else:
            ev = E("scalar", lambda s, d=dst_ap, n2=nblk: s.copy(
                d, ps_tr[:, 0:n2 * 128]),
                waits=[("tT", last_t)], inc="tS")
            pair = ("tS", ev)
        S["tr"] = pair
        return pair

    for b in range(1 if only_b0 else NB):
        if b > 0:
            bar = [("tT", C["tT"]), ("tV", C["tV"]), ("tS", C["tS"]),
                   ("tG", C["tG"]), ("dS", C["dS"]), ("dG", C["dG"])]
            for eng in ENGINES:
                E(eng, None, waits=bar)
        xl = None
        for ii in range(0, NT, 2):
            src = bass.AP(x_in, b * 2048 * D + ii * 128 * D,
                          [[D, 128], [128 * D, 2], [1, D]])
            xl = E("sync", lambda s, d=x_sb[:, ii:ii + 2, :], sr=src:
                   s.dma_start(d, sr),
                   waits=([S["x_free"]] if ii == 0 else ()),
                   inc="dS", n=16)
        xload = xl

        for l in range(NL):
            for ri, hsrc in enumerate((ln1_w, ln1_b, ln2_w, ln2_b)):
                bc_load(rep[:, ri, :], hsrc, l * D, D,
                        [S["rep"]] if ri == 0 else ())
            E("gpsimd", lambda g, o=l * D: g.dma_start(
                borow[:, :], bass.AP(bo_d, o, [[D, 1], [1, D]])),
                inc="dG", n=16)
            rep_ready = C["dG"]

            i_wk, w = alloc_panel()
            wload2(i_wk, wk_d, l * D * D, D, D, w)
            kv_dg = panel_loaded[i_wk]
            i_wv, w = alloc_panel()
            wload2(i_wv, wv_d, l * D * D, D, D, w)
            kv_dg = panel_loaded[i_wv]
            i_wq, w = alloc_panel()
            wload2(i_wq, wq_d, l * D * D, D, D, w)
            i_wo, w = alloc_panel()
            wload2(i_wo, wo_d, l * D * D, D, D, w)

            uph = [("tT", C["tT"])]

            pank = pans[i_wk]
            panv = pans[i_wv]
            xnT_snap = {}
            ek_snap = {}
            vv_snap = {}
            snap_ctx = None
            xn_snap = {}
            for k_it in range(NT + 3):
                if k_it < NT:
                    i = k_it
                    u = i & 1
                    x_i = x_sb[:, i, :]
                    sq_s, rstd_s = ln_stats(x_i, u, i, xwait=[("dS", xload)])
                    xn_snap[i] = ln_apply(x_i, u, i, rep[:, 0, :],
                                          rep[:, 1, :], sq_s, rstd_s,
                                          extra=[("dG", rep_ready)])
                if 1 <= k_it <= NT:
                    i = k_it - 1
                    u = i & 1
                    xnT_snap[i] = transposes(
                        lambda j, uu=u: sc_xn[:, uu, j * 128:(j + 1) * 128],
                        ("tV", xn_snap[i]), sc_xnT[:, u, :],
                        extra=(uph if i <= 1 else ()))
                if 2 <= k_it <= NT + 1:
                    i = k_it - 2
                    u = i & 1
                    last = None
                    for n0 in (0, 512):
                        for j in range(NJ):
                            w2 = []
                            if n0 == 0 and j == 0:
                                w2 = [xnT_snap[i],
                                      ("tS", ek_snap.get(i - 1, 0)),
                                      ("dG", panel_loaded[i_wk])]
                            last = E("tensor",
                                     lambda t, jj=j, nn=n0, uu=u, pn=pank:
                                     t.matmul(ps_pA[:, nn:nn + 512],
                                              sc_xnT[:, uu,
                                                     jj * 128:(jj + 1) * 128],
                                              pn[:, jj * D + nn:
                                                 jj * D + nn + 512],
                                              start=(jj == 0),
                                              stop=(jj == NJ - 1)),
                                     waits=w2,
                                     inc=("tT" if (n0 == 512 and j == NJ - 1)
                                          else None))
                    kmm = last
                    for n0 in (0, 512):
                        for j in range(NJ):
                            w2 = []
                            if n0 == 0 and j == 0:
                                w2 = ([vv_snap[i - 1]] if i >= 1 else []) + [
                                    ("dG", panel_loaded[i_wv])]
                            last = E("tensor",
                                     lambda t, jj=j, nn=n0, uu=u, pn=panv:
                                     t.matmul(ps_pB[:, nn:nn + 512],
                                              sc_xnT[:, uu,
                                                     jj * 128:(jj + 1) * 128],
                                              pn[:, jj * D + nn:
                                                 jj * D + nn + 512],
                                              start=(jj == 0),
                                              stop=(jj == NJ - 1)),
                                     waits=w2,
                                     inc=("tT" if (n0 == 512 and j == NJ - 1)
                                          else None))
                    vmm = last
                    ek_snap[i] = E("scalar", lambda s, uu=u: s.activation(
                        sc_a[:, uu, :], ps_pA[:, :], AF.Exp),
                        waits=[("tT", kmm)] + (uph if i <= 1 else []),
                        inc="tS")
                    vv_snap[i] = ("tS", E("scalar", lambda s, uu=u: s.copy(
                        sc_b[:, uu, :], ps_pB[:, :]),
                        waits=[("tT", vmm)], inc="tS"))
                if 3 <= k_it and k_it - 3 <= NT - 1:
                    i = k_it - 3
                    u = i & 1
                    w0 = [("tS", ek_snap[i]), vv_snap[i]]
                    if i == 0:
                        w0 += [S["ctxps"], S["ms"]]
                    for h in range(NH):
                        E("tensor", lambda t, hh=h, uu=u, ii=i: t.matmul(
                            ps_ctx[:, hh * 128:(hh + 1) * 128],
                            sc_a[:, uu, hh * 128:(hh + 1) * 128],
                            sc_b[:, uu, hh * 128:(hh + 1) * 128],
                            start=(ii == 0), stop=(ii == NT - 1)),
                            waits=(w0 if h == 0 else ()), inc=None)
                        snap_ctx = E("tensor", lambda t, hh=h, uu=u, ii=i:
                                     t.matmul(
                                         ps_ms[:, hh:hh + 1],
                                         sc_a[:, uu, hh * 128:(hh + 1) * 128],
                                         ones_bf[:, :],
                                         start=(ii == 0), stop=(ii == NT - 1)),
                                     inc=("tT" if h == NH - 1 else None))
            free_panel(i_wk)
            free_panel(i_wv)
            v11 = E("vector", lambda v: v.reciprocal(
                dnr[:, 0:8], ps_ms[:, 0:8]),
                waits=[("tT", snap_ctx)], inc="tV")
            S["ms"] = ("tV", v11)
            ctxsb_s = None
            for h in range(NH):
                ctxsb_s = E("vector", lambda v, hh=h: v.tensor_scalar(
                    ctx_sb[:, hh * 128:(hh + 1) * 128],
                    ps_ctx[:, hh * 128:(hh + 1) * 128],
                    dnr[:, hh:hh + 1], None, ALU.mult),
                    waits=([("tV", v11)] if h == 0 else ()),
                    inc=("tV" if h == NH - 1 else None))
            S["ctxps"] = ("tV", ctxsb_s)

            panq = pans[i_wq]
            pano = pans[i_wo]
            xnb_snap = {}
            xnTb = {}
            eq_snap = {}
            sq_snap = {}
            sqT_snap = {}
            atT_snap = {}
            res_snap = {}
            for k_it in range(NT + 4):
                if k_it < NT:
                    i = k_it
                    u = i & 1
                    x_i = x_sb[:, i, :]
                    E("vector", lambda v, a2=x_i, ii=i: v.tensor_scalar(
                        fdump[:, :], a2, nm_all[:, ii:ii + 1],
                        rs_all[:, ii:ii + 1], ALU.add, ALU.mult),
                        waits=[S["fdump"]], inc="tV")
                    E("vector", lambda v: v.tensor_mul(
                        fdump[:, :], fdump[:, :], rep[:, 0, :]), inc=None)
                    xn_s = E("vector", lambda v, uu=u: v.tensor_add(
                        sc_xn[:, uu, :], fdump[:, :], rep[:, 1, :]), inc="tV")
                    S["fdump"] = ("tV", xn_s)
                    xnb_snap[i] = xn_s
                if 1 <= k_it <= NT:
                    i = k_it - 1
                    u = i & 1
                    xnTb[i] = transposes(
                        lambda j, uu=u: sc_xn[:, uu, j * 128:(j + 1) * 128],
                        ("tV", xnb_snap[i]), sc_xnT[:, u, :], eng="scalar")
                if 2 <= k_it <= NT + 1:
                    i = k_it - 2
                    u = i & 1
                    q_s = None
                    for n0 in (0, 512):
                        for j in range(NJ):
                            w2 = []
                            if n0 == 0 and j == 0:
                                w2 = [xnTb[i],
                                      ("tS", eq_snap.get(i - 1, 0)),
                                      ("dG", panel_loaded[i_wq])]
                            q_s = E("tensor",
                                    lambda t, jj=j, nn=n0, uu=u, pn=panq:
                                    t.matmul(ps_pA[:, nn:nn + 512],
                                             sc_xnT[:, uu,
                                                    jj * 128:(jj + 1) * 128],
                                             pn[:, jj * D + nn:
                                                jj * D + nn + 512],
                                             start=(jj == 0),
                                             stop=(jj == NJ - 1)),
                                    waits=w2,
                                    inc=("tT" if (n0 == 512 and j == NJ - 1)
                                         else None))
                    last = None
                    for h in range(NH):
                        last = E("scalar", lambda s, hh=h, uu=u: s.activation(
                            sc_a[:, uu, hh * 128:(hh + 1) * 128],
                            ps_pA[:, hh * 128:(hh + 1) * 128], AF.Exp,
                            accum_out=qrs[:, uu * 8 + hh:uu * 8 + hh + 1]),
                            waits=([("tT", q_s)] if h == 0 else ()),
                            inc=("tS" if h == NH - 1 else None))
                    eq_snap[i] = last
                    qr_s = E("vector", lambda v, uu=u: v.reciprocal(
                        qrcp[:, uu * 8:uu * 8 + 8],
                        qrs[:, uu * 8:uu * 8 + 8]),
                        waits=[("tS", last)], inc="tV")
                    last = None
                    for h in range(NH):
                        last = E("vector",
                                 lambda v, hh=h, uu=u: v.tensor_scalar(
                                     sc_b[:, uu, hh * 128:(hh + 1) * 128],
                                     sc_a[:, uu, hh * 128:(hh + 1) * 128],
                                     qrcp[:, uu * 8 + hh:uu * 8 + hh + 1],
                                     None, ALU.mult),
                                 waits=([("tV", qr_s)] if h == 0 else ()),
                                 inc=("tV" if h == NH - 1 else None))
                    sq_snap[i] = last
                if 3 <= k_it <= NT + 2:
                    c = k_it - 3
                    u = c & 1
                    sqT_snap[c] = transposes(
                        lambda j, uu=u: sc_b[:, uu, j * 128:(j + 1) * 128],
                        ("tV", sq_snap[c]), sc_a[:, u, :], eng="scalar")
                if 4 <= k_it <= NT + 3:
                    yi = k_it - 4
                    u = yi & 1
                    x_i = x_sb[:, yi, :]
                    y_s = None
                    for n0 in (0, 512):
                        for h in range(NH):
                            w2 = []
                            if n0 == 0 and h == 0:
                                w2 = [("tS", atT_snap[yi]),
                                      ("tV", res_snap.get(yi - 1, 0)),
                                      ("dG", panel_loaded[i_wo]),
                                      ("dG", rep_ready)]
                                if yi == 0:
                                    w2.append(S["ctxps"])
                            E("tensor", lambda t, hh=h, nn=n0, uu=u,
                              pn=pano:
                              t.matmul(ps_ctx[:, nn:nn + 512],
                                       sc_c[:, uu, hh * 128:(hh + 1) * 128],
                                       pn[:, hh * D + nn:hh * D + nn + 512],
                                       start=(hh == 0), stop=False),
                              waits=w2, inc=None)
                        y_s = E("tensor", lambda t, nn=n0: t.matmul(
                            ps_ctx[:, nn:nn + 512], ones_row[:, :],
                            borow[:, nn:nn + 512], start=False, stop=True),
                            inc=("tT" if n0 == 512 else None))
                    res_snap[yi] = E("vector", lambda v, a2=x_i: v.tensor_add(
                        a2, a2, ps_ctx[:, :]),
                        waits=[("tT", y_s)], inc="tV")
                if 3 <= k_it <= NT + 2:
                    c = k_it - 3
                    u = c & 1
                    at_s = None
                    for h in range(NH):
                        w2 = []
                        if h == 0:
                            w2 = [sqT_snap[c],
                                  ("tS", atT_snap.get(c - 1, 0))]
                            if c == 0:
                                w2.append(("tV", ctxsb_s))
                        at_s = E("tensor", lambda t, hh=h, uu=u: t.matmul(
                            ps_pB[:, hh * 128:(hh + 1) * 128],
                            ctx_sb[:, hh * 128:(hh + 1) * 128],
                            sc_a[:, uu, hh * 128:(hh + 1) * 128],
                            start=True, stop=True),
                            waits=w2, inc=("tT" if h == NH - 1 else None))
                    atT_snap[c] = E("scalar", lambda s, uu=u: s.copy(
                        sc_c[:, uu, :], ps_pB[:, :]),
                        waits=[("tT", at_s)], inc="tS")
            free_panel(i_wq)
            free_panel(i_wo)
            S["pA"] = ("tS", eq_snap[NT - 1])
            S["pB"] = ("tS", atT_snap[NT - 1])
            S["ctxps"] = ("tV", res_snap[NT - 1])

            for half in range(2):
                tbase = half * 8
                x2T_ready = {}

                def ln2_stage(gl, guard):
                    ev = None
                    for sub in range(2):
                        i = tbase + 2 * gl + sub
                        u = i & 1
                        x_i = x_sb[:, i, :]
                        sq_s, rstd_s = ln_stats(x_i, u, i)
                        xn_s = ln_apply(x_i, u, i, rep[:, 2, :], rep[:, 3, :],
                                        sq_s, rstd_s)
                        tt = 2 * gl + sub
                        dst = bass.AP(x2T, tt * 128,
                                      [[8192, 128], [1024, NJ], [1, 128]])
                        ev = transposes(
                            lambda j, uu=u: sc_xn[:, uu,
                                                  j * 128:(j + 1) * 128],
                            ("tV", xn_s), dst,
                            extra=(guard if sub == 0 else ()), eng="scalar")
                    x2T_ready[gl] = ev

                ln2_stage(0, [("tT", C["tT"]), ("tS", C["tS"])])
                for q in range(NQ4):
                    i_w1, w = alloc_panel()
                    wload2(i_w1, w1_d, l * D * FF + q * 1024, FF, 1024, w)
                    i_w2, w = alloc_panel()
                    wload2(i_w2, w2_d, l * FF * D + q * 1024 * D, D, 1024, w)
                    pw1, pw2 = pans[i_w1], pans[i_w2]
                    seq = [(gl, c) for gl in range(4) for c in range(NC8)]
                    lk_snap = {}
                    y2_snap = {}
                    ev_snap = {}

                    def emit_y2(idx2):
                        gl2, c2 = seq[idx2]
                        t0 = tbase + 2 * gl2
                        lastmm = None
                        for sub in range(2):
                            dst2 = ps_pA if sub == 0 else ps_pB
                            for n0 in (0, 512):
                                w3 = []
                                if sub == 0 and n0 == 0:
                                    w3 = [("tV", lk_snap[idx2]),
                                          ("dG", panel_loaded[i_w2])]
                                    if c2 == 0:
                                        w3 += [S["pA"], S["pB"]]
                                lastmm = E(
                                    "tensor",
                                    lambda t, ss=sub, nn=n0, cc=c2, d=dst2,
                                    ii=idx2, pn=pw2:
                                    t.matmul(
                                        d[:, nn:nn + 512],
                                        sc_c[:, ii & 1,
                                             ss * 128:ss * 128 + 128],
                                        pn[:, cc * 1024 + nn:
                                           cc * 1024 + nn + 512],
                                        start=(cc == 0), stop=(cc == NC8 - 1)),
                                    waits=w3,
                                    inc=("tT" if (sub == 1 and n0 == 512)
                                         else None))
                        y2_snap[idx2] = lastmm
                        if c2 == NC8 - 1:
                            ev = None
                            for sub in range(2):
                                src2 = ps_pA if sub == 0 else ps_pB
                                ev = E("vector",
                                       lambda v, ss=sub, s2=src2, t2=t0:
                                       v.tensor_add(x_sb[:, t2 + ss, :],
                                                    x_sb[:, t2 + ss, :],
                                                    s2[:, :]),
                                       waits=([("tT", lastmm)] if sub == 0
                                              else ()),
                                       inc=("tV" if sub == 1 else None))
                            ev_snap[gl2] = ev
                            S["pA"] = ("tV", ev)
                            S["pB"] = ("tV", ev)

                    for idx, (gl, c) in enumerate(seq):
                        if q == 0 and c == 0 and gl < 3:
                            ln2_stage(gl + 1, [])
                        boff = (idx & 1) * 512
                        lastmm = None
                        for j in range(NJ):
                            w2 = []
                            if j == 0:
                                w2 = [x2T_ready[gl],
                                      ("tV", lk_snap.get(idx - 2, 0)),
                                      ("dG", panel_loaded[i_w1])]
                                if idx == 0:
                                    w2.append(S["ctxps"])
                            lastmm = E(
                                "tensor",
                                lambda t, jj=j, bo2=boff, cc=c, g2=gl,
                                pn=pw1:
                                t.matmul(
                                    ps_ctx[:, bo2:bo2 + 256],
                                    pn[:, jj * 1024 + cc * 128:
                                       jj * 1024 + cc * 128 + 128],
                                    bass.AP(x2T, jj * 1024 + g2 * 256,
                                            [[8192, 128], [1, 256]]),
                                    start=(jj == 0), stop=(jj == NJ - 1)),
                                waits=w2,
                                inc=("tT" if j == NJ - 1 else None))
                        sm = E("scalar", lambda s, bo2=boff: s.mul(
                            ldump[:, :], ps_ctx[:, bo2:bo2 + 256], 0.01),
                            waits=[("tT", lastmm)], inc="tS")
                        lk_snap[idx] = E("vector",
                                         lambda v, bo2=boff, ii=idx:
                                         v.tensor_max(
                                             sc_c[:, ii & 1, 0:256],
                                             ldump[:, :],
                                             ps_ctx[:, bo2:bo2 + 256]),
                                         waits=[("tS", sm)], inc="tV")
                        if idx >= 1:
                            emit_y2(idx - 1)
                    emit_y2(len(seq) - 1)
                    S["ctxps"] = ("tV", lk_snap[len(seq) - 1])
                    free_panel(i_w1)
                    free_panel(i_w2)
            S["rep"] = S["fdump"] if S["fdump"][0] == "tV" else ("tV", C["tV"])

        uphm = [("tT", C["tT"])]
        i_sp, sp_w = alloc_panel()
        pan_sp = pans[i_sp]
        mu_off = 4096
        so_off = 6144

        bc_load(rep[:, 0, :], fgamma, 0, D, [S["rep"]])
        bc_load(rep[:, 1, :], fbeta, 0, D, ())
        mod_rep_dg = C["dG"]
        mu_mm = 0
        mu_cp = 0
        last_mm = None
        for j in range(NJ):
            dl = E("sync", lambda s, jj=j: s.dma_start(
                mu_tmp[:, :], fmu[jj * 128:(jj + 1) * 128, :]),
                waits=([("tT", mu_mm), ("tV", mu_cp)] if j > 0
                       else [S["fdump"], ("tS", C["tS"])]),
                inc="dS", n=16)
            sqv = E("vector", lambda v: v.tensor_mul(
                ldump[:, :], mu_tmp[:, 0:256], mu_tmp[:, 0:256]),
                waits=[("dS", dl), ("tT", mu_mm)], inc="tV")
            mu_cp = E("vector", lambda v, jj=j, sp=pan_sp: v.tensor_copy(
                sp[:, mu_off + jj * 256:mu_off + (jj + 1) * 256],
                mu_tmp[:, :]),
                waits=(sp_w if j == 0 else ()), inc="tV")
            last_mm = E("tensor", lambda t, jj=j: t.matmul(
                ps_ms[0:1, 0:256], ones_f32c[:, :], ldump[:, :],
                start=(jj == 0), stop=(jj == NJ - 1)),
                waits=[("tV", sqv)] + ([S["ms"]] if j == 0 else []),
                inc="tT")
            mu_mm = last_mm
        n1 = E("scalar", lambda s: s.activation(
            ldump[0:1, 0:256], ps_ms[0:1, 0:256], AF.Sqrt),
            waits=[("tT", last_mm)], inc="tS")
        S["ms"] = ("tS", n1)
        n2 = E("vector", lambda v: v.tensor_scalar(
            ldump[0:1, 0:256], ldump[0:1, 0:256], 1e-12, None, ALU.max),
            waits=[("tS", n1)], inc="tV")
        n3 = E("vector", lambda v: v.reciprocal(ldump[0:1, 0:256], ldump[0:1, 0:256]),
               waits=[("tV", n2)], inc="tV")
        n4 = E("vector", lambda v: v.tensor_scalar(
            ldump[0:1, 0:256], ldump[0:1, 0:256], fsc_rep[0:1, 0:1], None, ALU.mult),
            waits=[("tV", n3)], inc="tV")
        nrep = E("tensor", lambda t: t.matmul(
            ps_ctx[:, 0:256], ones_f32r[:, :], ldump[0:1, 0:256],
            start=True, stop=True),
            waits=[("tV", n4), S["ctxps"], ("tG", ginit)], inc="tT")
        mu_ready = None
        for j in range(NJ):
            mu_ready = E("vector", lambda v, jj=j, sp=pan_sp: v.tensor_mul(
                sp[:, mu_off + jj * 256:mu_off + (jj + 1) * 256],
                sp[:, mu_off + jj * 256:mu_off + (jj + 1) * 256],
                ps_ctx[:, 0:256]),
                waits=([("tT", nrep), ("tV", mu_cp)] if j == 0 else ()),
                inc=("tV" if j == NJ - 1 else None))
        S["ctxps"] = ("tV", mu_ready)
        gid = E("gpsimd", lambda g: (
            g.memset(ldump[:, 0:128], 0.0),
            g.affine_select(out=ldump[:, 0:128], in_=ldump[:, 0:128],
                            compare_op=ALU.not_equal, fill=1.0, base=0,
                            pattern=[[-1, 128]], channel_multiplier=1))[-1],
            waits=[("tV", mu_cp), ("tV", C["tV"])], inc="tG")
        d1 = E("sync", lambda s: s.dma_start(
            fdump[0:16, 0:256], fb1[:, :]),
            waits=[S["fdump"]], inc="dS", n=16)
        S["fdump"] = ("dS", d1)
        lastv = None
        for hc in range(2):
            w2 = [("dS", d1), ("tG", gid), S["pA"]] if hc == 0 else [
                ("tV", lastv)]
            t1 = E("tensor", lambda t, h2=hc: t.transpose(
                ps_pA[:, h2 * 128:h2 * 128 + 16],
                fdump[0:16, h2 * 128:h2 * 128 + 128],
                ldump[0:16, 0:16]),
                waits=w2, inc="tT")
            lastv = E("vector", lambda v, h2=hc: v.tensor_copy(
                b1T[:, h2, :], ps_pA[:, h2 * 128:h2 * 128 + 16]),
                waits=[("tT", t1)], inc="tV")
        d2 = E("sync", lambda s: s.dma_start(fdump[0:16, :], fb2[:, :]),
               waits=[("tV", lastv)], inc="dS", n=16)
        S["fdump"] = ("dS", d2)
        for dc in range(NJ):
            w2 = [("dS", d2), ("tV", lastv)] if dc == 0 else [("tV", lastv)]
            t1 = E("tensor", lambda t, d3=dc: t.transpose(
                ps_pA[:, d3 * 128:d3 * 128 + 16],
                fdump[0:16, d3 * 128:d3 * 128 + 128],
                ldump[0:16, 0:16]),
                waits=w2, inc="tT")
            lastv = E("vector", lambda v, d3=dc: v.tensor_copy(
                b2T[:, d3, :], ps_pA[:, d3 * 128:d3 * 128 + 16]),
                waits=[("tT", t1)], inc="tV")
        S["pA"] = ("tV", lastv)
        S["fdump"] = ("tV", lastv)

        quartets = {}

        def load_quartet(tq):
            iq1, w = alloc_panel()
            for e2 in range(4):
                for j2 in range(NJ):
                    src = bass.AP(fw1, (4 * tq + e2) * D * 256 + j2 * 128 * 256,
                                  [[256, 128], [1, 256]])
                    E("gpsimd",
                      lambda g, d=pans[iq1][:, e2 * 2048 + j2 * 256:
                                            e2 * 2048 + (j2 + 1) * 256],
                      s2=src: g.dma_start(d, s2),
                      waits=(w if (e2 == 0 and j2 == 0) else ()),
                      inc="dG", n=16)
            panel_loaded[iq1] = C["dG"]
            iq2, w = alloc_panel()
            for e2 in range(4):
                for h2 in range(2):
                    src = bass.AP(fw2, (4 * tq + e2) * 256 * D + h2 * 128 * D,
                                  [[D, 128], [1, D]])
                    E("gpsimd",
                      lambda g, d=pans[iq2][:, e2 * 2048 + h2 * 1024:
                                            e2 * 2048 + (h2 + 1) * 1024],
                      s2=src: g.dma_start(d, s2),
                      waits=(w if (e2 == 0 and h2 == 0) else ()),
                      inc="dG", n=16)
            panel_loaded[iq2] = C["dG"]
            quartets[tq] = (iq1, iq2)

        load_quartet(0)
        load_quartet(1)

        xio_free = {0: ("tV", 0), 1: ("tV", 0)}
        logit_free = ("tS", 0)
        slot_mm_last = None
        for i in range(NT):
            u = i & 1
            x_i = x_sb[:, i, :]
            src = bass.AP(x_in, b * 2048 * D + i * 128 * D,
                          [[D, 128], [1, D]])
            dl = E("sync", lambda s, uu=u, sr=src: s.dma_start(
                xio[:, uu, :], sr),
                waits=[xio_free[u]] + (uphm if i <= 1 else []),
                inc="dS", n=16)
            add_s = E("vector", lambda v, a=x_i, uu=u: v.tensor_add(
                a, a, xio[:, uu, :]),
                waits=[("dS", dl)], inc="tV")
            xio_free[u] = ("tV", add_s)
            sq_s, rstd_s = ln_stats(x_i, u, i, xwait=[("tV", add_s)])
            x1_s = E("vector", lambda v, a=x_i, ii=i: v.tensor_scalar(
                a, a, nm_all[:, ii:ii + 1], rs_all[:, ii:ii + 1],
                ALU.add, ALU.mult),
                waits=[("tV", rstd_s), S["fdump"]], inc="tV")
            sq2, rstd2 = ln_stats(x_i, u, i, xwait=[("tV", x1_s)])
            xn_s = ln_apply(x_i, u, i, rep[:, 0, :], rep[:, 1, :],
                            sq2, rstd2, extra=[("dG", mod_rep_dg)])
            s3 = E("scalar", lambda s, uu=u: s.activation(
                fdump[:, :], sc_xn[:, uu, :], AF.Square,
                accum_out=st[:, uu * 8 + 6:uu * 8 + 7]),
                waits=[("tV", xn_s)], inc="tS")
            S["fdump"] = ("tS", s3)
            s4 = E("scalar", lambda s, uu=u: s.activation(
                st[:, uu * 8 + 7:uu * 8 + 8], st[:, uu * 8 + 6:uu * 8 + 7],
                AF.Sqrt),
                waits=[("tS", s3)], inc="tS")
            v8 = E("vector", lambda v, uu=u: v.tensor_scalar(
                st[:, uu * 8 + 6:uu * 8 + 7], st[:, uu * 8 + 7:uu * 8 + 8],
                1e-12, None, ALU.max),
                waits=[("tS", s4)], inc="tV")
            v9 = E("vector", lambda v, uu=u: v.reciprocal(
                st[:, uu * 8 + 7:uu * 8 + 8], st[:, uu * 8 + 6:uu * 8 + 7]),
                waits=[("tV", v8)], inc="tV")
            rn_s = E("vector", lambda v, uu=u: v.tensor_scalar(
                st[:, uu * 8 + 6:uu * 8 + 7], st[:, uu * 8 + 7:uu * 8 + 8],
                fsc_rep[:, 0:1], None, ALU.mult),
                waits=[("tV", v9)], inc="tV")
            if b == 0 and i == 0:
                dump("rn0", st[:, 0:16], [128, 16], F32)
                dump("mus", pan_sp[:, 4096:6144], [128, 2048], BF16, cast=True)
                dump("xnm0", sc_xn[:, 0, :], [128, D], BF16, cast=True)
            xnT_s = transposes(
                lambda j, uu=u: sc_xn[:, uu, j * 128:(j + 1) * 128],
                ("tV", xn_s), sc_xnT[:, u, :],
                extra=(uphm if i <= 1 else ()))
            lg_s = None
            for j in range(NJ):
                w2 = []
                if j == 0:
                    w2 = [xnT_s, ("tV", mu_ready), logit_free,
                          S["ctxps"] if i == 0 else ("tV", 0)]
                lg_s = E("tensor", lambda t, jj=j, uu=u, sp=pan_sp: t.matmul(
                    ps_ctx[:, 0:256],
                    sc_xnT[:, uu, jj * 128:(jj + 1) * 128],
                    sp[:, mu_off + jj * 256:mu_off + (jj + 1) * 256],
                    start=(jj == 0), stop=(jj == NJ - 1)),
                    waits=w2, inc=("tT" if j == NJ - 1 else None))
            eE = E("scalar", lambda s, ii=i, uu=u, sp=pan_sp: s.activation(
                sp[:, ii * 256:(ii + 1) * 256], ps_ctx[:, 0:256],
                AF.Exp, scale=st[:, uu * 8 + 6:uu * 8 + 7]),
                waits=[("tT", lg_s), ("tV", rn_s)], inc="tS")
            logit_free = ("tS", eE)
            for esc in range(2):
                w2 = [("tS", eE)]
                if i == 0 and esc == 0:
                    w2 += [S["ms"], S["pA"], S["pB"]]
                E("tensor", lambda t, e2=esc, ii=i, sp=pan_sp: t.matmul(
                    ps_ms[:, e2:e2 + 1],
                    sp[:, ii * 256 + e2 * 128:ii * 256 + e2 * 128 + 128],
                    ones_bf[:, :],
                    start=(ii == 0), stop=(ii == NT - 1)),
                    waits=w2, inc=None)
                dst = ps_pA if esc == 0 else ps_pB
                for n0 in (0, 512):
                    slot_mm_last = E(
                        "tensor",
                        lambda t, e2=esc, nn=n0, ii=i, uu=u, d=dst,
                        sp=pan_sp:
                        t.matmul(
                            d[:, nn:nn + 512],
                            sp[:, ii * 256 + e2 * 128:
                               ii * 256 + e2 * 128 + 128],
                            sc_xn[:, uu, nn:nn + 512],
                            start=(ii == 0), stop=(ii == NT - 1)),
                        inc=("tT" if (esc == 1 and n0 == 512) else None))
        if b == 0:
            dump("x1", x_sb[:, :, :], [128, NT * D], F32)
            dump("E", pan_sp[:, 0:4096], [128, 4096], BF16, cast=True)
        vz = E("vector", lambda v: v.reciprocal(zrcp[:, :], ps_ms[:, 0:2]),
               waits=[("tT", slot_mm_last)], inc="tV")
        S["ms"] = ("tV", vz)
        si_s = None
        for esc in range(2):
            src = ps_pA if esc == 0 else ps_pB
            si_s = E("vector", lambda v, e2=esc, s2=src: v.tensor_scalar(
                sc_a[:, e2, :], s2[:, :], zrcp[:, e2:e2 + 1], None, ALU.mult),
                waits=([("tV", vz)] if esc == 0 else ()),
                inc=("tV" if esc == 1 else None))
        S["pA"] = ("tV", si_s)
        S["pB"] = ("tV", si_s)
        if b == 0:
            dump("zrcp", zrcp[:, :], [128, 2], F32)
            dump("slotin", sc_a[:, :, :], [128, 2 * D], BF16, cast=True)
        siT_s = None
        for esc in range(2):
            dstv = bass.AP(sc_b, esc * 128, [[2048, 128], [256, NJ], [1, 128]])
            siT_s = transposes(
                lambda j, e2=esc: sc_a[:, e2, j * 128:(j + 1) * 128],
                ("tV", si_s), dstv)
        so_free = ("tV", 0)
        gprev = ("tS", 0)
        so_evac = None
        for e in range(16):
            tq, eo = e // 4, e % 4
            iq1, iq2 = quartets[tq]
            pq1, pq2 = pans[iq1], pans[iq2]
            gel = {}
            for hc in range(2):
                last = None
                for j in range(NJ):
                    w2 = []
                    if j == 0 and hc == 0:
                        w2 = [siT_s, ("dG", panel_loaded[iq1]), gprev]
                    elif j == 0 and hc == 1:
                        w2 = [("tS", gel[0])]
                    last = E("tensor",
                             lambda t, jj=j, h2=hc, e2=eo, p1=pq1, ee=e:
                             t.matmul(
                                 ps_ms[:, 32 + h2 * 16:32 + h2 * 16 + 16],
                                 p1[:, e2 * 2048 + jj * 256 + h2 * 128:
                                    e2 * 2048 + jj * 256 + h2 * 128 + 128],
                                 bass.AP(sc_b, jj * 256 + ee * 16,
                                         [[2048, 128], [1, 16]]),
                                 start=(jj == 0), stop=(jj == NJ - 1)),
                             waits=w2, inc=("tT" if j == NJ - 1 else None))
                gel[hc] = E("scalar", lambda s, h2=hc, e2=e: s.activation(
                    hT[:, h2, :], ps_ms[:, 32 + h2 * 16:32 + h2 * 16 + 16],
                    AF.Gelu, bias=b1T[:, h2, e2:e2 + 1]),
                    waits=[("tT", last)], inc="tS")
            gprev = ("tS", gel[1])
            if b == 0 and e == 0 and "hT0" in dbg:
                dw = dump("hT0", hT[:, :, :], [128, 32], BF16, cast=True)
                if dw:
                    gprev = dw
            if b == 0 and e == 0 and "siT" in dbg:
                dump("siT", sc_b[:, :, :], [128, 2048], BF16, cast=True)
            if b == 0 and e == 0 and "fw2p" in dbg:
                dump("fw2p", pans[iq2][:, 0:2048], [128, 2048], BF16, cast=True)
                dump("b2Td", b2T[:, :, :], [128, 128], F32)
            last = None
            for dc in range(NJ):
                for hc in range(2):
                    w2 = []
                    if dc == 0 and hc == 0:
                        w2 = [("tS", gel[0]), ("tS", gel[1]),
                              ("dG", panel_loaded[iq2]), so_free]
                    last = E("tensor",
                             lambda t, d3=dc, h2=hc, e2=eo, p2=pq2:
                             t.matmul(
                                 ps_ctx[:, d3 * 16:d3 * 16 + 16],
                                 p2[:, e2 * 2048 + h2 * 1024 + d3 * 128:
                                    e2 * 2048 + h2 * 1024 + d3 * 128 + 128],
                                 hT[:, h2, :],
                                 start=(h2 == 0), stop=(h2 == 1)),
                             waits=w2, inc=("tT" if (dc == NJ - 1 and hc == 1)
                                            else None))
            for dc in range(NJ):
                so_evac = E("vector", lambda v, d3=dc, e2=e, sp=pan_sp:
                    v.tensor_scalar(
                    bass.AP(sp, so_off + d3 * 256 + e2 * 16,
                            [[8192, 128], [1, 16]]),
                    ps_ctx[:, d3 * 16:d3 * 16 + 16],
                    b2T[:, d3, e2:e2 + 1], None, ALU.add),
                    waits=([("tT", last)] if dc == 0 else ()),
                    inc=("tV" if dc == NJ - 1 else None))
            so_free = ("tV", so_evac)
            if eo == 3:
                free_panel(iq1)
                free_panel(iq2)
                if tq + 2 < 4:
                    load_quartet(tq + 2)
        S["ctxps"] = ("tV", so_evac)
        if b == 0:
            dump("soT", pan_sp[:, 6144:8192], [128, 2048], BF16, cast=True)
        so_s = None
        for esc in range(2):
            dstv = bass.AP(sc_b, esc * 1024, [[2048, 128], [1, 1024]])
            so_s = transposes(
                lambda dc, e2=esc, sp=pan_sp: sp[
                    :, so_off + dc * 256 + e2 * 128:
                    so_off + dc * 256 + e2 * 128 + 128],
                ("tV", so_evac), dstv)
        if b == 0:
            dump("so", sc_b[:, :, :], [128, 2 * D], BF16, cast=True)
        st_dma = {0: 0, 1: 0}
        for i in range(NT):
            u = i & 1
            x_i = x_sb[:, i, :]
            rsum = E("vector", lambda v, ii=i, uu=u, sp=pan_sp: v.reduce_sum(
                st[:, uu * 8:uu * 8 + 1],
                sp[:, ii * 256:(ii + 1) * 256], axis=AX.X),
                waits=[so_s], inc="tV")
            rrec = E("vector", lambda v, uu=u: v.reciprocal(
                st[:, uu * 8 + 1:uu * 8 + 2], st[:, uu * 8:uu * 8 + 1]),
                waits=[("tV", rsum)], inc="tV")
            eT_s = transposes(
                lambda e2, ii=i, sp=pan_sp: sp[:, ii * 256 + e2 * 128:
                                               ii * 256 + e2 * 128 + 128],
                ("tV", 0), sc_a[:, u, 0:256], nblk=2)
            last = None
            for esc in range(2):
                for n0 in (0, 512):
                    w2 = []
                    if esc == 0 and n0 == 0:
                        w2 = [eT_s, S["pA"]]
                    last = E("tensor",
                             lambda t, e2=esc, nn=n0, uu=u: t.matmul(
                                 ps_pA[:, nn:nn + 512],
                                 sc_a[:, uu, e2 * 128:e2 * 128 + 128],
                                 bass.AP(sc_b, e2 * 1024 + nn,
                                         [[2048, 128], [1, 512]]),
                                 start=(e2 == 0), stop=(e2 == 1)),
                             waits=w2, inc=("tT" if (esc == 1 and n0 == 512)
                                            else None))
            x2 = E("vector", lambda v, a=x_i, uu=u: v.scalar_tensor_tensor(
                a, ps_pA[:, :], st[:, uu * 8 + 1:uu * 8 + 2], a,
                ALU.mult, ALU.add),
                waits=[("tT", last), ("tV", rrec)], inc="tV")
            S["pA"] = ("tV", x2)
            sq_s, rstd_s = ln_stats(x_i, u, i, xwait=[("tV", x2)])
            ov = E("vector", lambda v, a=x_i, ii=i, uu=u: v.tensor_scalar(
                xio[:, uu, :], a, nm_all[:, ii:ii + 1], rs_all[:, ii:ii + 1],
                ALU.add, ALU.mult),
                waits=[("tV", rstd_s), S["fdump"], ("dS", st_dma[u]),
                       xio_free[u]], inc="tV")
            xio_free[u] = ("tV", ov)
            dst = bass.AP(y_out, b * 2048 * D + i * 128 * D,
                          [[D, 128], [1, D]])
            st_dma[u] = E("sync", lambda s, uu=u, d2=dst: s.dma_start(
                d2, xio[:, uu, :]),
                waits=[("tV", ov)], inc="dS", n=16)
        if b == 0:
            dump("xend", x_sb[:, :, :], [128, NT * D], F32)
            dump("Eb0", pan_sp[:, 0:4096], [128, 4096], BF16, cast=True)
            dump("sob", sc_b[:, :, :], [128, 2048], BF16, cast=True)
        free_panel(i_sp)
        S["x_free"] = ("tV", C["tV"])
        S["rep"] = ("tV", C["tV"])

    E("sync", None, waits=[("dS", C["dS"])])

    semh = {s: es.enter_context(nc.semaphore(s)) for s in sems}
    block = es.enter_context(nc.Block())

    def mk_replay(name):
        def fn(engobj):
            waited = {}
            for f, ws, inc, n in p.ev[name]:
                for s2, v in ws:
                    if s2 == "zz_taken":
                        raise RuntimeError("panel used while reserved")
                    if waited.get(s2, 0) < v:
                        engobj.wait_ge(semh[s2], v)
                        waited[s2] = v
                ins = f(engobj) if f is not None else None
                if inc is not None:
                    if ins is not None:
                        ins.then_inc(semh[inc], n)
                    else:
                        engobj.sem_inc(semh[inc], n)
        return fn

    block.sync(mk_replay("sync"))
    block.tensor(mk_replay("tensor"))
    block.vector(mk_replay("vector"))
    block.scalar(mk_replay("scalar"))
    block.gpsimd(mk_replay("gpsimd"))

    es.close()
    return nc


def kernel(**inputs):
    f = lambda k: np.ascontiguousarray(np.asarray(inputs[k], dtype=np.float32))
    x = f("L0_attn_x")
    shared = {
        "ln1_w": f("L0_attn_ln1_w"), "ln1_b": f("L0_attn_ln1_b"),
        "ln2_w": f("L0_attn_ln2_w"), "ln2_b": f("L0_attn_ln2_b"),
        "bo_d": f("L0_attn_bo"),
        "wq_d": f("L0_attn_wq"), "wk_d": f("L0_attn_wk"),
        "wv_d": f("L0_attn_wv"), "wo_d": f("L0_attn_wo"),
        "w1_d": f("L0_attn_w1"), "w2_d": f("L0_attn_w2"),
        "fgamma": f("L0_ffn_ln_gamma"), "fbeta": f("L0_ffn_ln_beta"),
        "fmu": f("L0_ffn_mu").reshape(1024, 256),
        "fscale": f("L0_ffn_scale"),
        "fw1": f("L0_ffn_w1"), "fb1": f("L0_ffn_b1"),
        "fw2": f("L0_ffn_w2"), "fb2": f("L0_ffn_b2"),
    }
    nc = build_nc()
    in_maps = []
    for c in range(8):
        m = {"x_in": x[2 * c:2 * c + 2]}
        m.update(shared)
        in_maps.append(m)
    res = run_bass_kernel_spmd(nc, in_maps, core_ids=list(range(8)))
    out = np.concatenate([np.asarray(r["y_out"]) for r in res.results], axis=0)
    return out.astype(np.float32)
